# version 10
# baseline (speedup 1.0000x reference)
import contextlib
import numpy as np
import ml_dtypes
import concourse.bass as bass
import concourse.mybir as mybir
from concourse.bass_utils import run_bass_kernel_spmd
from concourse.alu_op_type import AluOpType as Alu

F32 = mybir.dt.float32
BF16 = mybir.dt.bfloat16
F16 = mybir.dt.float16
I32 = mybir.dt.int32
U8 = mybir.dt.uint8
AF = mybir.ActivationFunctionType
AX = mybir.AxisListType
BF16NP = ml_dtypes.bfloat16

ENGINES = ("sync", "tensor", "vector", "scalar", "gpsimd")
SERIAL_ENGINES = ("vector", "scalar", "gpsimd")

D = 2048
DC = 16
NH = 16
HD = 128
S = 4096
OWN = 2048
TT = 512
NTT = OWN // TT
SCALE = 1.0 / float(np.sqrt(128.0))
EPS_LN = 1e-5
EPS_RMS = 1e-6
II = 5632
DENOM_B = float(np.sqrt(2048.0) / 4096.0)

MASK_Y0 = 2432
MASK_W = MASK_Y0 + 512

STAGE = 4
ATTN_LA = 2


class Prog:
    def __init__(self, nc, es):
        self.nc = nc
        self.es = es
        self.ops = {e: [] for e in ENGINES}
        self.counts = {}
        self.sems = {}
        self.last_tok = {e: None for e in ENGINES}
        self.ndma = 0

    def sem(self, name):
        if name not in self.sems:
            self.sems[name] = self.es.enter_context(self.nc.semaphore(name))
            self.counts[name] = 0
        return self.sems[name]

    MAX_CONSTS = 32

    def const_init(self, dram_handle):
        self._const_map = {}
        self._const_vals = []
        self._const_sb = self.es.enter_context(
            self.nc.sbuf_tensor("const_pool", [128, self.MAX_CONSTS], F32))
        self.const_tok = self.dma("sync", self._const_sb[:, :], dram_handle.ap(), sem="dcst")

    def const(self, val):
        val = float(val)
        if val not in self._const_map:
            i = len(self._const_vals)
            assert i < self.MAX_CONSTS
            self._const_vals.append(val)
            ap = self._const_sb[:, i:i + 1]
            self.nc.const_aps.register(F32, val, ap)
            self._const_map[val] = ap
        return self._const_map[val]

    def const_host(self):
        arr = np.zeros((128, self.MAX_CONSTS), dtype=np.float32)
        for i, v in enumerate(self._const_vals):
            arr[:, i] = v
        return arr

    def _tok(self, name, by):
        self.sem(name)
        self.counts[name] += by
        return (name, self.counts[name])

    def op(self, eng, emit, waits=(), sem=None, by=1, serialize=None):
        if serialize is None:
            serialize = eng in SERIAL_ENGINES
        w = [t for t in waits if t is not None]
        if serialize and self.last_tok[eng] is not None:
            w.append(self.last_tok[eng])
        if sem is None and serialize:
            sem = "_ser_" + eng
        tok = self._tok(sem, by) if sem is not None else None
        self.ops[eng].append((emit, w, tok, by))
        if serialize:
            self.last_tok[eng] = tok
        return tok

    def dma(self, eng, out, in_, waits=(), sem="dma", **kw):
        self.ndma += 1

        def emit(e, out=out, in_=in_, kw=kw):
            return e.dma_start(out, in_, **kw)
        return self.op(eng, emit, waits=waits, sem=sem, by=16, serialize=False)

    def wait(self, eng, *toks):
        self.ops[eng].append((None, [t for t in toks if t is not None], None, 0))

    def pe_now(self):
        return ("pe", self.counts.get("pe", 0)) if self.counts.get("pe", 0) else None

    def phase_gate(self):
        return [t for t in (self.pe_now(), self.last_tok["scalar"],
                            self.last_tok["vector"], self.last_tok["gpsimd"])
                if t is not None]

    def finalize(self, block):
        for ename in ENGINES:
            ops = self.ops[ename]
            if not ops:
                continue
            sems = self.sems

            def run(e, ops=ops):
                water = {}
                for emit, waits, tok, by in ops:
                    need = {}
                    for (sn, sv) in waits:
                        if sv > need.get(sn, 0):
                            need[sn] = sv
                    for sn, sv in need.items():
                        if water.get(sn, 0) >= sv:
                            continue
                        e.wait_ge(sems[sn], sv)
                        water[sn] = sv
                    if emit is None:
                        continue
                    instr = emit(e)
                    if tok is not None:
                        instr.then_inc(sems[tok[0]], by)
            getattr(block, ename)(run)



def build_nc():
    nc = bass.Bass("TRN2", target_bir_lowering=False)

    x_own_t = nc.dram_tensor("x_own", [OWN, D], F32, kind="ExternalInput")
    x_full_t = nc.dram_tensor("x_full", [S, D], F32, kind="ExternalInput")
    consts_t = nc.dram_tensor("consts", [128, 32], F32, kind="ExternalInput")
    wqkv_t = nc.dram_tensor("wqkv", [D, 3 * D], F32, kind="ExternalInput")
    wqkvT_t = nc.dram_tensor("wqkvT", [48 * 128, 2048], F32, kind="ExternalInput")
    bqkv_t = nc.dram_tensor("bqkv", [3 * D], F32, kind="ExternalInput")
    woa_t = nc.dram_tensor("woa", [D, D], F32, kind="ExternalInput")
    bo_t = nc.dram_tensor("bo", [D], F32, kind="ExternalInput")
    mask_t = nc.dram_tensor("mask", [128, MASK_W], BF16, kind="ExternalInput")
    ident_t = nc.dram_tensor("identf", [128, 128], F32, kind="ExternalInput")
    gupT_t = nc.dram_tensor("gupT", [256, 2 * II], I32, kind="ExternalInput")
    gusc_t = nc.dram_tensor("gusc", [16, 2 * II], F16, kind="ExternalInput")
    dnpT_t = nc.dram_tensor("dnpT", [704, D], I32, kind="ExternalInput")
    dnsc_t = nc.dram_tensor("dnsc", [44, D], F16, kind="ExternalInput")
    shift_t = nc.dram_tensor("shifts", [128, 1], I32, kind="ExternalInput")
    invs_t = nc.dram_tensor("invs", [128, 1], F32, kind="ExternalInput")
    wqt_t = nc.dram_tensor("wqt", [D, D], F32, kind="ExternalInput")
    wkt_t = nc.dram_tensor("wkt", [D, D], F32, kind="ExternalInput")
    wvt_t = nc.dram_tensor("wvt", [D, D], F32, kind="ExternalInput")
    wot_t = nc.dram_tensor("wot", [D, D], F32, kind="ExternalInput")
    w8_t = nc.dram_tensor("w8", [8, 1], F32, kind="ExternalInput")

    out_t = nc.dram_tensor("out", [OWN, D], F32, kind="ExternalOutput")

    Ksc = nc.dram_tensor("Ksc", [D, S], BF16)
    Vsc = nc.dram_tensor("Vsc", [16 * 128, 32 * 128], BF16)
    Qsc = nc.dram_tensor("Qsc", [D, OWN], BF16)
    WGU = nc.dram_tensor("WGU", [88 * 128, 2048], BF16)
    WDN = nc.dram_tensor("WDN", [44 * 128, 2048], BF16)
    BWq = nc.dram_tensor("BWq", [16 * 128, 2048], BF16)
    BWk = nc.dram_tensor("BWk", [16 * 128, 2048], BF16)
    BWv = nc.dram_tensor("BWv", [16 * 128, 2048], BF16)
    BWot = nc.dram_tensor("BWot", [16 * 128, 2048], BF16)
    XN3 = nc.dram_tensor("XN3", [D, OWN], BF16)
    KVT = nc.dram_tensor("KVT", [D, OWN], BF16)
    AGI = nc.dram_tensor("AGI", [1, D], F32)
    AGO = nc.dram_tensor("AGO", [8, D], F32)

    es = contextlib.ExitStack()
    with es:
        p = Prog(nc, es)
        p.const_init(consts_t)
        p.const(EPS_LN)
        p.const(EPS_RMS)

        def mksb(stack):
            def sb(n, sh, dt):
                return stack.enter_context(nc.sbuf_tensor(n, sh, dt))
            return sb

        sbg = mksb(es)
        identf = sbg("identf_sb", [128, 128], F32)
        onesr1 = sbg("onesr1", [1, 128], BF16)
        onescol = sbg("onescol", [128, 1], BF16)
        bo_col = sbg("bo_col", [128, 16], F32)
        mh_col = sbg("mh_col", [128, NH], F32)

        psf = [es.enter_context(nc.psum_tensor(f"psf{i}", [128, 512], F32)) for i in range(6)]
        psbk = [es.enter_context(nc.psum_tensor(f"psb{i}", [128, 1024], BF16)) for i in range(2)]

        t_id = p.dma("sync", identf[:, :], ident_t.ap(), sem="did")
        t_ones = p.op("vector", lambda e: e.memset(onesr1[:, :], 1.0))
        t_onec = p.op("vector", lambda e: e.memset(onescol[:, :], 1.0))
        t_bcol = None
        for oc in range(16):
            t_bcol = p.dma("sync", bo_col[:, oc:oc + 1],
                           bass.AP(bo_t, oc * 128, [[1, 128], [1, 1]]), sem="dbo")

        esA = contextlib.ExitStack()
        sba = mksb(esA)
        nrm = sba("nrm", [1, 256], F32)
        rows = sba("rows", [1, 64], F32)
        rows_b = sba("rows_b", [1, 64], BF16)
        identb = sba("identb_sb", [128, 128], BF16)
        bqkv_col = sba("bqkv_col", [128, 32], F32)
        xtm = [sba(f"xtm{i}", [128, D], F32) for i in range(2)]
        xntm = [sba(f"xntm{i}", [128, D], BF16) for i in range(2)]
        xnf = sba("xnf", [128, DC * 2048], BF16)
        bns24 = sba("bns24", [128, 24], F32)
        bna = sba("bna", [128, 2], F32)
        lncol = sba("lncol", [128, 4], F32)
        wstg = [sba(f"wstg{i}", [128, 2048], F32) for i in range(2)]
        wbf = [sba(f"wbf{i}", [128, 2048], BF16) for i in range(2)]
        wvstg = sba("wvstg", [128, 512], F32)
        wvbf = sba("wvbf", [128, DC * 512], BF16)
        kst = [sba(f"kst{i}", [128, 512], BF16) for i in range(2)]
        ksq = sba("ksq", [128, 512], BF16)
        vst = [sba(f"vst{i}", [128, 512], BF16) for i in range(2)]
        vrow = sba("vrow", [1, 512], F32)
        vrowb = sba("vrowb", [1, 512], BF16)

        t_idb = p.op("vector", lambda e: e.tensor_copy(identb[:, :], identf[:, :]), waits=[t_id])
        t_bqc = None
        for oc in range(16):
            p.dma("sync", bqkv_col[:, oc:oc + 1],
                  bass.AP(bqkv_t, oc * 128, [[1, 128], [1, 1]]), sem="dbq")
            t_bqc = p.dma("sync", bqkv_col[:, 16 + oc:17 + oc],
                  bass.AP(bqkv_t, D + oc * 128, [[1, 128], [1, 1]]), sem="dbq")

        dq_w = [sba(f"dqw{i}", [128, 512], I32) for i in range(5)]
        dq_nib_b = [sba(f"dq_nib{i}", [128, 512], I32) for i in range(5)]
        dq_f_b = [sba(f"dq_f{i}", [128, 512], BF16) for i in range(5)]
        dq_s_b = [sba(f"dq_s{i}", [128, 512], BF16) for i in range(5)]
        dq_m_b = [sba(f"dq_m{i}", [128, 512], BF16) for i in range(5)]
        dq_t1_b = [sba(f"dq_t1{i}", [128, 512], BF16) for i in range(5)]
        dq_out = [sba(f"dqo{i}", [128, 512], BF16) for i in range(5)]
        dq_srow = [sba(f"dq_srow{i}", [1, 512], F16) for i in range(5)]
        dq_srowb = [sba(f"dq_srowb{i}", [1, 512], BF16) for i in range(5)]
        shift_sb = sba("shift_sb", [128, 1], I32)
        invs_sb = sba("invs_sb", [128, 1], F32)
        wcst = [sba(f"wcst{i}", [128, 1024], F32) for i in range(2)]
        wcb = sba("wcb", [128, 1024], BF16)
        t_shift = p.dma("sync", shift_sb[:, :], shift_t.ap(), sem="dsh")
        t_invs = p.dma("sync", invs_sb[:, :], invs_t.ap(), sem="dsi")
        p.const(1.0)
        p.const(-4.0)
        p.const(-3.5)
        p.const(-6.0)

        DQ_TILES = ([("gu", c, o0) for c in range(16) for o0 in range(0, 2 * II, 512)]
                    + [("dn", c, o0) for c in range(44) for o0 in range(0, D, 512)])
        dq_state = {"w": [None] * 5, "o": [None] * 5, "bank": [None, None],
                    "srow": [None] * 5, "srowb": [None] * 5, "nib_r": [None] * 5,
                    "f_r": [None] * 5, "s_r": [None] * 5, "m_r": [None] * 5,
                    "t1_r": [None] * 5}

        NT_DQ = len(DQ_TILES)
        dq_cur = [0] * 12
        dq_tok = {}

        def _dq_info(j):
            kind, c, o0 = DQ_TILES[j]
            if kind == "gu":
                return (gupT_t, 2 * II, gusc_t, WGU, c, o0, j % 5, 3 + (j % 2), kind)
            return (dnpT_t, D, dnsc_t, WDN, c, o0, j % 5, 3 + (j % 2), kind)

        def dq_advance(n):
            st = dq_state
            olds = list(dq_cur)
            NB = 5
            lims = [0] * 12
            lims[0] = min(olds[0] + n, NT_DQ, olds[1] + NB, olds[2] + NB)
            lims[1] = min(olds[0], olds[10] + NB)
            lims[2] = min(olds[1], olds[3] + NB)
            lims[3] = min(olds[2], olds[5] + NB)
            lims[4] = min(olds[3], olds[9] + NB)
            lims[5] = min(olds[4], olds[9] + NB, olds[10] + NB)
            lims[6] = min(olds[5], olds[8] + NB)
            lims[7] = olds[6]
            lims[8] = olds[7]
            lims[9] = olds[8]
            lims[10] = min(olds[9], olds[11] + NB)
            lims[11] = olds[10]
            for k in range(12):
                lims[k] = max(lims[k], olds[k])
            for k in range(12):
                for j in range(dq_cur[k], lims[k]):
                    srcw, ncols, scl, wdst, c, o0, par, bk, kind = _dq_info(j)
                    dq_nib, dq_f, dq_s, dq_m, dq_t1 = (dq_nib_b[par], dq_f_b[par],
                                                       dq_s_b[par], dq_m_b[par], dq_t1_b[par])
                    if k == 0:
                        dq_tok[(0, j)] = p.dma("gpsimd", dq_w[par][:, :],
                                               bass.AP(srcw, (16 * c) * ncols + o0,
                                                       [[ncols, 16], [0, 8], [1, 512]]),
                                               waits=[st["w"][par]], sem=f"ddq{par}")
                        dq_tok[("sr", j)] = p.dma("gpsimd", dq_srow[par][:, :],
                                                  bass.AP(scl, c * ncols + o0,
                                                          [[1, 1], [1, 512]]),
                                                  waits=[st["srow"][par]], sem=f"ddqs{par}")
                    elif k == 1:
                        t = p.op("scalar", (lambda par: lambda e: e.mul(
                            dq_srowb[par][:, :], dq_srow[par][:, :], -1.0))(par),
                            waits=[dq_tok.pop(("sr", j)), st["srowb"][par]])
                        st["srow"][par] = t
                        dq_tok[(1, j)] = t
                    elif k == 2:
                        t = p.op("vector", (lambda par, dq_nib: lambda e: e.tensor_scalar(
                            dq_nib[:, :], dq_w[par][:, :], shift_sb[:, :], None,
                            Alu.bitwise_and))(par, dq_nib),
                            waits=[dq_tok.pop((0, j)), t_shift, st["nib_r"][par]])
                        st["w"][par] = t
                        dq_tok[(2, j)] = t
                    elif k == 3:
                        t = p.op("scalar", (lambda par, dq_nib, dq_f: lambda e: e.activation(
                            dq_f[:, :], dq_nib[:, :], AF.Identity,
                            bias=-4.0, scale=invs_sb[:, :]))(par, dq_nib, dq_f),
                            waits=[dq_tok.pop((2, j)), t_invs, p.const_tok, st["f_r"][par]])
                        st["nib_r"][par] = t
                        dq_tok[(3, j)] = t
                    elif k == 4:
                        dq_tok[(4, j)] = p.op("scalar", (lambda par, dq_f, dq_s: lambda e: e.activation(
                            dq_s[:, :], dq_f[:, :], AF.Sign, bias=-3.5))(par, dq_f, dq_s),
                            waits=[dq_tok[(3, j)], st["s_r"][par]])
                    elif k == 5:
                        t = p.op("vector", (lambda par, dq_s, dq_f, dq_m: lambda e: e.scalar_tensor_tensor(
                            dq_m[:, :], dq_s[:, :], -4.0, dq_f[:, :],
                            Alu.mult, Alu.add))(par, dq_s, dq_f, dq_m),
                            waits=[dq_tok.pop((4, j)), st["m_r"][par], dq_tok.pop((3, j))])
                        st["f_r"][par] = t
                        dq_tok[(5, j)] = t
                    elif k == 6:
                        dq_tok[(6, j)] = p.op("scalar", (lambda par, dq_m, dq_t1: lambda e: e.activation(
                            dq_t1[:, :], dq_m[:, :], AF.Relu, bias=-6.0))(par, dq_m, dq_t1),
                            waits=[dq_tok[(5, j)], st["t1_r"][par]])
                    elif k == 7:
                        dq_tok[(7, j)] = p.op("vector", (lambda par, dq_m, dq_t1: lambda e: e.scalar_tensor_tensor(
                            dq_t1[:, :], dq_m[:, :], -2.0, dq_t1[:, :],
                            Alu.add, Alu.add))(par, dq_m, dq_t1),
                            waits=[dq_tok.pop((6, j))])
                    elif k == 8:
                        t = p.op("vector", (lambda par, dq_m, dq_t1: lambda e: e.scalar_tensor_tensor(
                            dq_m[:, :], dq_m[:, :], 0.5, dq_t1[:, :],
                            Alu.mult, Alu.max))(par, dq_m, dq_t1),
                            waits=[dq_tok.pop((7, j)), dq_tok.pop((5, j))])
                        st["t1_r"][par] = t
                        dq_tok[(8, j)] = t
                    elif k == 9:
                        t = p.op("gpsimd", (lambda par, dq_s, dq_m: lambda e: e.tensor_tensor(
                            dq_m[:, :], dq_s[:, :], dq_m[:, :], Alu.mult))(par, dq_s, dq_m),
                            waits=[dq_tok.pop((8, j))])
                        st["s_r"][par] = t
                        st["m_r"][par] = t
                        dq_tok[(9, j)] = t
                    elif k == 10:
                        tbc = p.op("tensor", (lambda bk, par: lambda e: e.matmul(
                            psf[bk][:, :], onesr1[:, :], dq_srowb[par][0:1, :],
                            start=True, stop=True))(bk, par),
                            waits=[dq_tok.pop((1, j)), t_ones, st["bank"][j % 2]], sem="pe")
                        st["srowb"][par] = tbc
                        t_out = p.op("vector", (lambda par, bk, dq_m: lambda e: e.tensor_tensor(
                            dq_out[par][:, :], dq_m[:, :], psf[bk][:, :],
                            Alu.mult))(par, bk, dq_m),
                            waits=[dq_tok.pop((9, j)), tbc, st["o"][par]])
                        st["bank"][j % 2] = t_out
                        dq_tok[(10, j)] = t_out
                    else:
                        if kind == "gu":
                            dst = bass.AP(WGU, (o0 // 128) * 128 * 2048 + c * 128,
                                          [[2048, 128], [128 * 2048, 4], [1, 128]])
                        else:
                            dst = bass.AP(WDN, c * 128 * 2048 + o0, [[2048, 128], [1, 512]])
                        st["o"][par] = p.dma("gpsimd", dst, dq_out[par][:, :],
                                             waits=[dq_tok.pop((10, j))], sem=f"ddqo{par}")
                dq_cur[k] = lims[k]

        def emit_dequant(lo, hi):
            while dq_cur[11] < hi:
                dq_advance(max(2, hi - dq_cur[0]))

        bw_state = {"l": [None, None], "c": None}

        def emit_wcast(lo, hi):
            pairs = [(wqt_t, BWq), (wkt_t, BWk), (wvt_t, BWv), (wot_t, BWot)]
            for i in range(lo, hi):
                wi, r = i // 16, i % 16
                srct, dstt = pairs[wi]
                for hf in range(2):
                    par = (2 * i + hf) % 2
                    t_l = p.dma("scalar", wcst[par][:, :],
                                srct.ap()[r * 128:(r + 1) * 128, hf * 1024:(hf + 1) * 1024],
                                waits=[bw_state["l"][par]], sem=f"dbw{par}")
                    t_c = p.op("vector", (lambda par: lambda e: e.tensor_copy(
                        wcb[:, :], wcst[par][:, :]))(par), waits=[t_l, bw_state["c"]])
                    bw_state["l"][par] = t_c
                    if wi < 3:
                        dst = bass.AP(dstt, (hf * 8) * 128 * 2048 + r * 128,
                                      [[2048, 128], [128 * 2048, 8], [1, 128]])
                    else:
                        dst = bass.AP(dstt, r * 128 * 2048 + hf * 1024,
                                      [[2048, 128], [1, 1024]])
                    t_s = p.dma("scalar", dst, wcb[:, :], waits=[t_c], sem="dbws")
                    bw_state["c"] = t_s

        dq_cursor = [0, 0]

        def dq_tick(n):
            dq_advance(n)
            lo2 = dq_cursor[1]
            hi2 = min(lo2 + 1, 64)
            if hi2 > lo2:
                emit_wcast(lo2, hi2)
                dq_cursor[1] = hi2

        xtm_free = [None, None]
        psb_free = [None, None]
        wcast_free = [None, None]
        wbf_free = [None, None]
        wv_free = [None]
        wvbf_free = [None]
        kst_free = [[], []]
        ksq_free = [None]
        vst_free = [None, None]
        vrow_free = [None]
        bank_free = [None] * 7
        xnf_ready = None
        xnf_read = None
        nrm_idx = [0]
        prev_xn = [None]
        prev_r1 = [None]

        def ln_tile(src_ap, par):
            xt = xtm[par]
            t_load = p.dma("sync", xt[:, :], src_ap, waits=[xtm_free[par]], sem=f"dax{par}")
            tb_tok = None
            for i in range(4):
                tb_tok = p.op("vector", (lambda i: lambda e: e.bn_stats(
                    bns24[:, i * 6:(i + 1) * 6], xt[:, i * 512:(i + 1) * 512]))(i),
                    waits=[t_load])
            t_ag = p.op("vector", lambda e: e.bn_aggr(bna[:, :], bns24[:, :]),
                        waits=[tb_tok, prev_r1[0]])
            t_r1 = p.op("scalar", lambda e: e.activation(
                lncol[:, 0:1], bna[:, 1:2], AF.Ln, bias=EPS_LN), waits=[t_ag, p.const_tok])
            prev_r1[0] = t_r1
            t_r2 = p.op("scalar", lambda e: e.activation(
                lncol[:, 0:1], lncol[:, 0:1], AF.Exp, scale=-0.5), waits=[t_r1])
            t_r3 = p.op("vector", lambda e: e.tensor_tensor(
                lncol[:, 1:2], bna[:, 0:1], lncol[:, 0:1], Alu.mult),
                waits=[t_r2, prev_xn[0]])
            t_r4 = p.op("vector", lambda e: e.tensor_scalar(
                lncol[:, 1:2], lncol[:, 1:2], -1.0, None, Alu.mult), waits=[t_r3])
            t_xn = p.op("scalar", (lambda par: lambda e: e.activation(
                xntm[par][:, :], xtm[par][:, :], AF.Identity,
                bias=lncol[:, 1:2], scale=lncol[:, 0:1]))(par), waits=[t_r4])
            prev_xn[0] = t_xn
            xtm_free[par] = t_xn
            return t_xn

        def transpose_to_xnf(par, tb, t_xn):
            last = None
            for grp in range(4):
                bk = grp % 2
                half = grp // 2
                dcs = [half * 8 + bk + 2 * j for j in range(4)]
                t_tp = None
                for j, dc in enumerate(dcs):
                    slot = half * 4 + j
                    t_tp = p.op("tensor", (lambda dc, bk, slot, par: lambda e: e.transpose(
                        psbk[bk][:, slot * 128:(slot + 1) * 128],
                        xntm[par][:, dc * 128:(dc + 1) * 128], identb[:, :]))(dc, bk, slot, par),
                        waits=[t_xn, t_idb, psb_free[bk], xnf_read], sem="pe")
                dst = bass.AP(xnf.handle if hasattr(xnf, 'handle') else xnf._handle, 0, [[1, 1]]) if False else None
                t_ev = p.op("vector", (lambda dcs, bk, half, tb: lambda e: e.tensor_copy(
                    bass.AP(xnf[:, :].tensor, tb * 128 + dcs[0] * 2048,
                            [[DC * 2048, 128], [2 * 2048, 4], [1, 128]]),
                    psbk[bk][:, half * 512:(half + 1) * 512]))(dcs, bk, half, tb),
                    waits=[t_tp])
                psb_free[bk] = t_ev
                last = t_ev
            return last

        def proj_fm(seg, kind):
            col0 = D if kind == "k" else 0
            scdst = Ksc if kind == "k" else Qsc
            for oc in range(NH):
                wpar = oc % 2
                ocg = (col0 // 128) + oc
                tw = [p.dma("sync", wstg[wpar][:, :],
                            wqkvT_t.ap()[ocg * 128:(ocg + 1) * 128, :],
                            waits=[wcast_free[wpar]], sem=f"daw{wpar}")]
                t_cast = p.op("vector", (lambda wpar: lambda e: e.tensor_copy(
                    wbf[wpar][:, :], wstg[wpar][:, :]))(wpar),
                    waits=tw + [wbf_free[wpar]])
                wcast_free[wpar] = t_cast
                for tt in range(4):
                    bk = (oc * 4 + tt) % 2
                    ps = psf[bk]

                    def mm(e, wpar=wpar, tt=tt, ps=ps):
                        last = None
                        for dc in range(DC):
                            last = e.matmul(
                                ps[:, :], wbf[wpar][:, dc * 128:(dc + 1) * 128],
                                xnf[:, dc * 2048 + tt * 512: dc * 2048 + (tt + 1) * 512],
                                start=(dc == 0), stop=(dc == DC - 1))
                        return last
                    t_mm = p.op("tensor", mm, waits=[t_cast, xnf_ready, bank_free[bk]],
                                sem="pe")
                    kpar = (oc * 4 + tt) % 2
                    ci = (16 + oc) if kind == "k" else oc
                    t_ev = p.op("vector", (lambda kpar, ps, ci: lambda e: e.tensor_scalar(
                        kst[kpar][:, :], ps[:, :], bqkv_col[:, ci:ci + 1], None,
                        Alu.add))(kpar, ps, ci),
                        waits=[t_mm, t_bqc] + kst_free[kpar])
                    bank_free[bk] = t_ev
                    colbase = (seg * 2048 if kind == "k" else 0) + tt * 512
                    t_st = p.dma("sync",
                                 scdst.ap()[oc * 128:(oc + 1) * 128, colbase: colbase + 512],
                                 kst[kpar][:, :], waits=[t_ev], sem=f"dsck{kpar}")
                    t_sq = p.op("scalar", (lambda kpar: lambda e: e.activation(
                        ksq[:, :], kst[kpar][:, :], AF.Square))(kpar),
                        waits=[t_ev, ksq_free[0]])
                    kst_free[kpar] = [t_st, t_sq]
                    t_nmm = p.op("tensor", lambda e: e.matmul(
                        psf[2][0:1, :], onescol[:, :], ksq[:, :], start=True, stop=True),
                        waits=[t_sq, t_onec, bank_free[2]], sem="pe")
                    idx = nrm_idx[0]
                    nrm_idx[0] += 1
                    t_red = p.op("vector", (lambda idx: lambda e: e.reduce_max(
                        nrm[0:1, idx:idx + 1], psf[2][0:1, :], AX.X, negate=True))(idx),
                        waits=[t_nmm])
                    ksq_free[0] = t_red
                    bank_free[2] = t_red
                wbf_free[wpar] = p.pe_now()
                dq_tick(3)

        def proj_v(seg):
            for vc2 in range(4):
                t_vr = p.dma("sync", vrow[:, :],
                             bass.AP(bqkv_t, 2 * D + vc2 * 512, [[1, 1], [1, 512]]),
                             waits=[vrow_free[0]], sem="davr")
                t_vrb = p.op("vector", lambda e: e.tensor_copy(vrowb[:, :], vrow[:, :]),
                             waits=[t_vr])
                t_casts = [t_vrb]
                for dc in range(DC):
                    t_l = p.dma("sync", wvstg[:, :],
                                bass.AP(wqkv_t, dc * 128 * (3 * D) + 2 * D + vc2 * 512,
                                        [[3 * D, 128], [1, 512]]),
                                waits=[wv_free[0]], sem="davw")
                    t_c = p.op("vector", (lambda dc: lambda e: e.tensor_copy(
                        wvbf[:, dc * 512:(dc + 1) * 512], wvstg[:, :]))(dc),
                        waits=[t_l, wvbf_free[0]])
                    wv_free[0] = t_c
                    t_casts.append(t_c)
                for tb in range(16):
                    bk = tb % 2
                    ps = psf[bk]

                    def mm(e, tb=tb, vc2=vc2, ps=ps):
                        for dc in range(DC):
                            e.matmul(ps[:, :],
                                     xnf[:, dc * 2048 + tb * 128: dc * 2048 + (tb + 1) * 128],
                                     wvbf[:, dc * 512:(dc + 1) * 512],
                                     start=(dc == 0), stop=False)
                        return e.matmul(ps[:, :], onesr1[:, :], vrowb[0:1, :],
                                        start=False, stop=True)
                    t_mm = p.op("tensor", mm,
                                waits=t_casts + [xnf_ready, bank_free[bk], t_ones],
                                sem="pe")
                    vpar = tb % 2
                    t_ev = p.op("vector", (lambda vpar, ps: lambda e: e.tensor_copy(
                        vst[vpar][:, :], ps[:, :]))(vpar, ps),
                        waits=[t_mm, vst_free[vpar]])
                    bank_free[bk] = t_ev
                    kc_ = seg * 16 + tb
                    t_st = p.dma("sync",
                                 bass.AP(Vsc, (vc2 * 4) * 128 * 4096 + kc_ * 128,
                                         [[4096, 128], [128 * 4096, 4], [1, 128]]),
                                 vst[vpar][:, :], waits=[t_ev], sem=f"dscv{vpar}")
                    vst_free[vpar] = t_st
                wvbf_free[0] = p.pe_now()
                vrow_free[0] = p.pe_now()
                dq_tick(3)

        for seg in range(3):
            for tb in range(16):
                t_xn = ln_tile(
                    x_full_t.ap()[seg * 2048 + tb * 128: seg * 2048 + (tb + 1) * 128, :]
                    if seg < 2 else x_own_t.ap()[tb * 128:(tb + 1) * 128, :],
                    tb % 2)
                xnf_ready = transpose_to_xnf(tb % 2, tb, t_xn)
                dq_tick(3)
            if seg < 2:
                proj_fm(seg, "k")
                proj_v(seg)
            else:
                proj_fm(seg, "q")
            xnf_read = p.pe_now()
            wbf_free[0] = xnf_read
            wbf_free[1] = xnf_read

        for h in range(NH):
            p.op("vector", (lambda h: lambda e: e.tensor_reduce(
                rows[0:1, h:h + 1], nrm[0:1, h * 4:(h + 1) * 4], AX.X, Alu.min))(h))
            p.op("vector", (lambda h: lambda e: e.tensor_reduce(
                rows[0:1, 16 + h:17 + h], nrm[0:1, 64 + h * 4:64 + (h + 1) * 4],
                AX.X, Alu.min))(h))
            p.op("vector", (lambda h: lambda e: e.tensor_reduce(
                rows[0:1, 32 + h:33 + h], nrm[0:1, 128 + h * 4:128 + (h + 1) * 4],
                AX.X, Alu.min))(h))
        p.op("vector", lambda e: e.tensor_tensor(
            rows[0:1, 0:16], rows[0:1, 0:16], rows[0:1, 16:32], Alu.min))
        p.op("vector", lambda e: e.tensor_tensor(
            rows[0:1, 0:16], rows[0:1, 0:16], rows[0:1, 32:48], Alu.add))
        p.op("vector", lambda e: e.tensor_scalar(
            rows[0:1, 0:16], rows[0:1, 0:16], SCALE / 2.0, None, Alu.mult))
        t7 = p.op("vector", lambda e: e.tensor_copy(rows_b[0:1, 0:16], rows[0:1, 0:16]))
        t_mh_done = None
        for h in range(NH):
            t_b = p.op("tensor", (lambda h: lambda e: e.matmul(
                psf[2][0:128, 0:1], onesr1[:, :], rows_b[0:1, h:h + 1],
                start=True, stop=True))(h), waits=[t7, t_ones, bank_free[2]], sem="pe")
            t_mh_done = p.op("vector", (lambda h: lambda e: e.tensor_copy(
                mh_col[:, h:h + 1], psf[2][0:128, 0:1]))(h), waits=[t_b])
            bank_free[2] = t_mh_done

        while dq_cur[11] < NT_DQ:
            dq_advance(3)
        while dq_cursor[1] < 64:
            emit_wcast(dq_cursor[1], dq_cursor[1] + 1)
            dq_cursor[1] += 1
        a_done = [(sn, p.counts[sn]) for sn in
                  ("dsck0", "dsck1", "dscv0", "dscv1", "ddqo0", "ddqo1", "ddqo2",
                   "ddqo3", "ddqo4", "dbws")]
        a_done.append(t_mh_done)
        a_done.append(p.pe_now())
        dq_bank_done = [dq_state["bank"][0], dq_state["bank"][1]]
        esA.close()

        stream = [sbg(f"st{i}", [128, OWN], F32) for i in range(DC)]

        esB = contextlib.ExitStack()
        sbb = mksb(esB)
        NS = 3
        NPT = 6
        LA = 3
        GH = 2
        NG = NH // GH
        sbanks = [psf[0], psf[1], psf[2]]
        ksb = [sbb(f"ksb{i}", [128, S], BF16) for i in range(2)]
        vsb = [sbb(f"vsb{i}", [128, S], BF16) for i in range(2)]
        qsb = [sbb(f"qsb{i}", [128, OWN], BF16) for i in range(2)]
        msb = sbb("msb", [128, MASK_W], BF16)
        ptb = [sbb(f"ptb{i}", [128, 512], BF16) for i in range(NPT)]
        rec_bc = sbb("rec_bc", [128, 512], BF16)
        recrow = sbb("recrow", [1, 512], F32)
        recrowb = sbb("recrowb", [1, 512], BF16)
        attn_g = [sbb(f"attn{i}", [128, OWN], BF16) for i in range(GH)]
        wo_stg = sbb("wo_stg", [128, 1024], F32)
        wo_bf = [sbb(f"wo_bf{i}", [128, 2048], BF16) for i in range(GH)]
        xch = [sbb(f"xch{i}", [128, 128], F32) for i in range(2)]
        xfm = sbb("xfm", [128, 512], F32)

        t_mask = p.dma("sync", msb[:, :], mask_t.ap(), waits=a_done, sem="dbm")

        kvq_free = [None, None]
        pt_free = [None] * NPT
        attn_free = []
        qidx = [0]
        pending_tail = [None]
        last_at = [None]
        sumbanks = [psf[3], psbk[0][:, :].bitcast(F32)]
        avbanks = [psf[4], psbk[1][:, :].bitcast(F32)]
        wo_stg_free = None
        wo_free = [None] * GH
        xch_free = [None, None]
        stream_tok = [[None] * NTT for _ in range(DC)]
        sbank_free = [bank_free[0], bank_free[1], bank_free[2]]
        sum_free = [dq_bank_done[0], psb_free[0]]
        av_free = [dq_bank_done[1], psb_free[1]]
        y_free = None

        for g in range(NG):
            attn_done = []
            for hh in range(GH):
                h = GH * g + hh
                par = h % 2
                t_k = p.dma("sync", ksb[par][:, :], Ksc.ap()[h * 128:(h + 1) * 128, :],
                            waits=a_done + [kvq_free[par]], sem=f"dbk{par}")
                t_q = p.dma("sync", qsb[par][:, :], Qsc.ap()[h * 128:(h + 1) * 128, :],
                            waits=a_done + [kvq_free[par]], sem=f"dbq{par}")
                t_v = p.dma("sync", vsb[par][:, :],
                            Vsc.ap()[h * 128:(h + 1) * 128, :],
                            waits=a_done + [kvq_free[par]], sem=f"dbv{par}")
                t_at = None
                for qb in range(NTT):
                    nkc = 20 + 4 * qb
                    m_toks = [None] * nkc
                    qi = qidx[0]
                    qidx[0] += 1
                    sb_ = qi % 2
                    sumbank = sumbanks[sb_]
                    avbank = avbanks[sb_]
                    t_sum_last = None
                    t_av_last = None

                    def emit_s(kc, qb=qb, par=par, h=h, t_k=t_k, t_q=t_q, m_toks=m_toks):
                        bs = kc % NS
                        t_s = p.op("tensor", (lambda bs, kc, qb, par: lambda e: e.matmul(
                            sbanks[bs][:, :], ksb[par][:, kc * 128:(kc + 1) * 128],
                            qsb[par][:, qb * 512:(qb + 1) * 512],
                            start=True, stop=True))(bs, kc, qb, par),
                            waits=[t_k, t_q, sbank_free[bs]], sem="pe")
                        t_e = p.op("scalar", (lambda bs, kc, h: lambda e: e.activation(
                            ptb[kc % NPT][:, :], sbanks[bs][:, :], AF.Exp,
                            bias=mh_col[:, h:h + 1], scale=SCALE))(bs, kc, h),
                            waits=[t_s, t_mh_done, pt_free[kc % NPT]])
                        sbank_free[bs] = t_e
                        tv = 128 * kc - 512 * qb
                        t_m = t_e
                        if tv >= 0:
                            eng = "gpsimd" if (kc % 2 == 0) else "vector"
                            t_m = p.op(eng, (lambda kc, tv: lambda e: e.tensor_tensor(
                                ptb[kc % NPT][:, :], ptb[kc % NPT][:, :],
                                msb[:, MASK_Y0 - tv: MASK_Y0 - tv + 512],
                                Alu.mult))(kc, tv),
                                waits=[t_e, t_mask])
                        m_toks[kc] = t_m

                    def emit_sumav(kc, nkc=nkc, par=par, m_toks=m_toks,
                                   sumbank=sumbank, avbank=avbank, sb_=sb_):
                        t_sum = p.op("tensor", (lambda kc, nkc: lambda e: e.matmul(
                            sumbank[0:1, :], onescol[:, :], ptb[kc % NPT][:, :],
                            start=(kc == 0), stop=(kc == nkc - 1)))(kc, nkc),
                            waits=[m_toks[kc], t_onec] + ([sum_free[sb_]] if kc == 0 else []),
                            sem="pe")
                        t_av = p.op("tensor", (lambda kc, nkc, par: lambda e: e.matmul(
                            avbank[:, :], vsb[par][:, kc * 128:(kc + 1) * 128],
                            ptb[kc % NPT][:, :],
                            start=(kc == 0), stop=(kc == nkc - 1)))(kc, nkc, par),
                            waits=[t_v] + ([av_free[sb_]] if kc == 0 else []),
                            sem="pe")
                        pt_free[kc % NPT] = t_av
                        return t_sum, t_av

                    for kc in range(nkc):
                        emit_s(kc)
                        if kc == 1 and pending_tail[0] is not None:
                            pending_tail[0]()
                            pending_tail[0] = None
                        if kc >= LA:
                            t_sum_last, t_av_last = emit_sumav(kc - LA)
                    for kc in range(nkc - LA, nkc):
                        t_sum_last, t_av_last = emit_sumav(kc)

                    def tail(h=h, hh=hh, qb=qb, t_sum_last=t_sum_last,
                             t_av_last=t_av_last, sumbank=sumbank, avbank=avbank, sb_=sb_):
                        nonlocal y_free
                        t_rc = p.op("vector", (lambda sumbank: lambda e: e.reciprocal(
                            recrow[0:1, :], sumbank[0:1, :]))(sumbank),
                            waits=[t_sum_last])
                        sum_free[sb_] = t_rc
                        t_rcb = p.op("vector", lambda e: e.tensor_copy(recrowb[0:1, :], recrow[0:1, :]),
                                     waits=[t_rc])
                        t_bc = p.op("tensor", lambda e: e.matmul(
                            psf[5][:, :], onesr1[:, :], recrowb[0:1, :], start=True, stop=True),
                            waits=[t_rcb, y_free, t_ones], sem="pe")
                        t_bce = p.op("vector", lambda e: e.tensor_copy(rec_bc[:, :], psf[5][:, :]),
                                     waits=[t_bc])
                        y_free = t_bce
                        t_at2 = p.op("vector", (lambda hh, qb, avbank: lambda e: e.tensor_tensor(
                            attn_g[hh][:, qb * 512:(qb + 1) * 512], avbank[:, :], rec_bc[:, :],
                            Alu.mult))(hh, qb, avbank),
                            waits=[t_av_last, t_bce] + attn_free)
                        av_free[sb_] = t_at2
                        attn_done.append(t_at2)
                        last_at[0] = t_at2
                    pending_tail[0] = tail
                if pending_tail[0] is not None:
                    pending_tail[0]()
                    pending_tail[0] = None
                t_at = last_at[0]
                kvq_free[par] = t_at
            for hh in range(GH):
                h = GH * g + hh
                tc0 = p.dma("scalar", wo_stg[:, :], woa_t.ap()[h * 128:(h + 1) * 128, 0:1024],
                            waits=a_done + [wo_stg_free], sem="dbwo")
                th0 = p.op("vector", (lambda hh: lambda e: e.tensor_copy(
                    wo_bf[hh][:, 0:1024], wo_stg[:, :]))(hh),
                    waits=[tc0, None if g == 0 else p.pe_now()])
                tc1 = p.dma("scalar", wo_stg[:, :], woa_t.ap()[h * 128:(h + 1) * 128, 1024:2048],
                            waits=[th0], sem="dbwo")
                wo_free[hh] = p.op("vector", (lambda hh: lambda e: e.tensor_copy(
                    wo_bf[hh][:, 1024:2048], wo_stg[:, :]))(hh), waits=[tc1])
                wo_stg_free = wo_free[hh]
            for tt in range(NTT):
                for dc in range(DC):
                    if g == 0:
                        tp_toks = []
                        for j in range(4):
                            xp = j % 2
                            t_xl = p.dma("scalar", xch[xp][:, :],
                                         x_own_t.ap()[tt * 512 + j * 128: tt * 512 + (j + 1) * 128,
                                                      dc * 128:(dc + 1) * 128],
                                         waits=a_done + [xch_free[xp]], sem=f"dbx{xp}")
                            t_tp = p.op("tensor", (lambda xp, j: lambda e: e.transpose(
                                psf[5][:, j * 128:(j + 1) * 128], xch[xp][:, :],
                                identf[:, :]))(xp, j),
                                waits=[t_xl, t_id, y_free], sem="pe")
                            xch_free[xp] = t_tp
                            tp_toks.append(t_tp)
                        t_xf = p.op("vector", (lambda: lambda e: e.tensor_copy(xfm[:, :], psf[5][:, 0:512]))(),
                                    waits=tp_toks)
                        y_free = t_xf

                    def mmy(e, tt=tt, dc=dc):
                        last = None
                        for hh2 in range(GH):
                            last = e.matmul(psf[5][:, :], wo_bf[hh2][:, dc * 128:(dc + 1) * 128],
                                            attn_g[hh2][:, tt * 512:(tt + 1) * 512],
                                            start=(hh2 == 0), stop=(hh2 == GH - 1))
                        return last
                    t_y = p.op("tensor", mmy,
                               waits=[y_free] + attn_done + list(wo_free),
                               sem="pe")
                    if g == 0:
                        t_se = p.op("vector", (lambda dc, tt: lambda e: e.scalar_tensor_tensor(
                            stream[dc][:, tt * 512:(tt + 1) * 512], psf[5][:, :],
                            bo_col[:, dc:dc + 1], xfm[:, :], Alu.add, Alu.add))(dc, tt),
                            waits=[t_y, t_bcol])
                    else:
                        t_se = p.op("vector", (lambda dc, tt: lambda e: e.tensor_tensor(
                            stream[dc][:, tt * 512:(tt + 1) * 512],
                            stream[dc][:, tt * 512:(tt + 1) * 512], psf[5][:, :],
                            Alu.add))(dc, tt),
                            waits=[t_y, stream_tok[dc][tt]])
                    y_free = t_se
                    stream_tok[dc][tt] = t_se
            attn_free = [p.pe_now()]
        esB.close()

        def make_lnfm(xnb, sqt, rows3, rbrow, eps):
            def lnfm(tt, stream_tok, bank_gate, xnb_gate=()):
                t_xb = []
                for dc in range(DC):
                    t_xb.append(p.op("vector", (lambda dc: lambda e: e.tensor_copy(
                        xnb[dc][:, :], stream[dc][:, tt * 512:(tt + 1) * 512]))(dc),
                        waits=[stream_tok[dc][tt]] + list(xnb_gate)))

                def mmsum(e):
                    last = None
                    for dc in range(DC):
                        last = e.matmul(psf[0][0:1, :], onescol[:, :], xnb[dc][:, :],
                                        start=(dc == 0), stop=(dc == DC - 1))
                    return last
                t_sx = p.op("tensor", mmsum, waits=t_xb + [t_onec] + bank_gate, sem="pe")
                sq_toks = []
                mm2 = None
                nsq = len(sqt)
                for dc in range(DC):
                    sp = dc % nsq
                    t_sq = p.op("scalar", (lambda dc, sp: lambda e: e.activation(
                        sqt[sp][:, :], xnb[dc][:, :], AF.Square))(dc, sp),
                        waits=[t_xb[dc]] + ([sq_toks[dc - nsq]] if dc >= nsq else []))
                    mm2 = p.op("tensor", (lambda dc, sp: lambda e: e.matmul(
                        psf[1][0:1, :], onescol[:, :], sqt[sp][:, :],
                        start=(dc == 0), stop=(dc == DC - 1)))(dc, sp),
                        waits=[t_sq] + (bank_gate if dc == 0 else []), sem="pe")
                    sq_toks.append(mm2)
                t_m = p.op("vector", lambda e: e.tensor_scalar(
                    rows3[0:1, 0:512], psf[0][0:1, :], 1.0 / D, None, Alu.mult),
                    waits=[t_sx])
                t_q = p.op("vector", lambda e: e.tensor_scalar(
                    rows3[0:1, 512:1024], psf[1][0:1, :], 1.0 / D, None, Alu.mult),
                    waits=[mm2])
                t_t = p.op("vector", lambda e: e.tensor_tensor(
                    psf[0][0:1, :], rows3[0:1, 0:512], rows3[0:1, 0:512], Alu.mult),
                    waits=[t_m])
                t_v = p.op("vector", lambda e: e.tensor_tensor(
                    rows3[0:1, 512:1024], rows3[0:1, 512:1024], psf[0][0:1, :],
                    Alu.subtract), waits=[t_q, t_t])
                t_l = p.op("scalar", (lambda: lambda e: e.activation(
                    rows3[0:1, 512:1024], rows3[0:1, 512:1024], AF.Ln, bias=eps))(),
                    waits=[t_v, p.const_tok])
                t_e2 = p.op("scalar", lambda e: e.activation(
                    rows3[0:1, 512:1024], rows3[0:1, 512:1024], AF.Exp, scale=-0.5),
                    waits=[t_l])
                t_rb1 = p.op("vector", lambda e: e.tensor_copy(rbrow[0:1, :], rows3[0:1, 512:1024]),
                             waits=[t_e2])
                t_b1 = p.op("tensor", lambda e: e.matmul(
                    psf[1][:, :], onesr1[:, :], rbrow[0:1, :], start=True, stop=True),
                    waits=[t_rb1, t_ones, t_q], sem="pe")
                tmul = []
                for dc in range(DC):
                    tmul.append(p.op("vector", (lambda dc: lambda e: e.tensor_tensor(
                        xnb[dc][:, :], xnb[dc][:, :], psf[1][:, :], Alu.mult))(dc),
                        waits=[t_b1]))
                t_mr = p.op("vector", lambda e: e.tensor_tensor(
                    rows3[0:1, 0:512], rows3[0:1, 0:512], rows3[0:1, 512:1024], Alu.mult),
                    waits=[t_e2])
                t_ng = p.op("vector", lambda e: e.tensor_scalar(
                    rows3[0:1, 0:512], rows3[0:1, 0:512], -1.0, None, Alu.mult),
                    waits=[t_mr])
                t_rb2 = p.op("vector", lambda e: e.tensor_copy(rbrow[0:1, :], rows3[0:1, 0:512]),
                             waits=[t_ng, t_b1])
                t_b2 = p.op("tensor", lambda e: e.matmul(
                    psf[1][:, :], onesr1[:, :], rbrow[0:1, :], start=True, stop=True),
                    waits=[t_rb2] + tmul, sem="pe")
                t_app = []
                for dc in range(DC):
                    tb = p.op("vector", (lambda dc: lambda e: e.tensor_tensor(
                        xnb[dc][:, :], xnb[dc][:, :], psf[1][:, :], Alu.add))(dc),
                        waits=[t_b2])
                    t_app.append(tb)
                return t_app, t_app[-1]
            return lnfm

        def emit_ffn(call):
            esD = contextlib.ExitStack()
            sbd = mksb(esD)
            xnb = [sbd(f"fx{call}_{i}", [128, 512], BF16) for i in range(DC)]
            hdn = [sbd(f"fh{call}_{i}", [128, 512], BF16) for i in range(44)]
            sqt = [sbd(f"fsq{call}_{i}", [128, 512], BF16) for i in range(1)]
            wgu3 = [sbd(f"fwg{call}_{i}", [128, 2048], BF16) for i in range(3)]
            rows3 = sbd(f"fr3{call}", [1, 1024], F32)
            rbrow = sbd(f"frb{call}", [1, 512], BF16)
            lnfm = make_lnfm(xnb, sqt, rows3, rbrow, EPS_LN)
            pg = p.phase_gate()
            bank_gate = pg
            wslot_free = [None, None, None]
            gidx = [0]
            hdn_free = [pg[0]] * 44
            g_free = None
            u_free = None
            y_free2 = [None] * 6
            first = True
            hd_prev = [None, None]
            for tt in range(NTT):
                t_app, _ = lnfm(tt, stream_tok, bank_gate,
                                xnb_gate=(pg if first else [p.pe_now()]))
                bank_gate = [None]
                for ih in range(2):
                    for ic in range(22):
                        gi = ih * 22 + ic
                        i0 = ih * 2816 + ic * 128
                        fgate = pg if first else []
                        sg_ = (2 * gidx[0]) % 3
                        su_ = (2 * gidx[0] + 1) % 3
                        gidx[0] += 1
                        wg = wgu3[sg_]
                        wu = wgu3[su_]
                        tg = [p.dma("gpsimd", wg[:, :],
                                    WGU.ap()[(i0 // 128) * 128:(i0 // 128) * 128 + 128, :],
                                    waits=[wslot_free[sg_]] + fgate, sem=f"dfg{call}_{sg_}")]
                        tu = [p.dma("scalar", wu[:, :],
                                    WGU.ap()[((II + i0) // 128) * 128:((II + i0) // 128) * 128 + 128, :],
                                    waits=[wslot_free[su_]] + fgate, sem=f"dfu{call}_{su_}")]
                        first = False

                        def mmg(e, wg=wg):
                            last = None
                            for dc in range(DC):
                                last = e.matmul(psf[0][:, :], wg[:, dc * 128:(dc + 1) * 128],
                                                xnb[dc][:, :], start=(dc == 0),
                                                stop=(dc == DC - 1))
                            return last

                        def mmu(e, wu=wu):
                            last = None
                            for dc in range(DC):
                                last = e.matmul(psf[1][:, :], wu[:, dc * 128:(dc + 1) * 128],
                                                xnb[dc][:, :], start=(dc == 0),
                                                stop=(dc == DC - 1))
                            return last
                        t_g = p.op("tensor", mmg, waits=tg + t_app + [g_free], sem="pe")
                        t_uu = p.op("tensor", mmu, waits=tu + t_app + [u_free], sem="pe")
                        wslot_free[sg_] = t_g
                        wslot_free[su_] = t_uu
                        sp = 0
                        t_sl = p.op("scalar", (lambda sp: lambda e: e.activation(
                            sqt[sp][:, :], psf[0][:, :], AF.Silu))(sp),
                            waits=[t_g, hd_prev[sp]])
                        t_hd = p.op("vector", (lambda gi, sp: lambda e: e.tensor_tensor(
                            hdn[gi][:, :], sqt[sp][:, :], psf[1][:, :], Alu.mult))(gi, sp),
                            waits=[t_sl, t_uu, hdn_free[gi]])
                        hd_prev[sp] = t_hd
                        g_free = t_sl
                        u_free = t_hd
                groups = [list(range(0, 6)), list(range(6, 12)), list(range(12, 16))]
                ev_toks = []
                for grp in groups:
                    for k in range(44):
                        wpar = k % 3
                        wdnb = wgu3[wpar]
                        t_w = p.dma("sync", wdnb[:, 0:len(grp) * 128],
                                    bass.AP(WDN, k * 128 * 2048 + grp[0] * 128,
                                            [[2048, 128], [1, len(grp) * 128]]),
                                    waits=[wslot_free[wpar]], sem=f"dfd{call}_{wpar}")

                        def mmd(e, grp=grp, k=k, wdnb=wdnb):
                            last = None
                            for j in range(len(grp)):
                                last = e.matmul(psf[j][:, :],
                                                wdnb[:, j * 128:(j + 1) * 128],
                                                hdn[k][:, :], start=(k == 0),
                                                stop=(k == 43))
                            return last
                        t_d = p.op("tensor", mmd,
                                   waits=[t_w, u_free] + ([y_free2[j] for j in range(len(grp))]
                                                          if k == 0 else []), sem="pe")
                        wslot_free[wpar] = t_d
                    for j, hc in enumerate(grp):
                        t_ev = p.op("vector", (lambda j, hc: lambda e: e.tensor_copy(
                            xnb[hc][:, :], psf[j][:, :]))(j, hc),
                            waits=[t_d])
                        y_free2[j] = t_ev
                        ev_toks.append(t_ev)
                pe_after_down = p.pe_now()
                for gi in range(44):
                    hdn_free[gi] = pe_after_down
                mm2 = None
                rms_mm = [None]
                for hc in range(DC):
                    sp = 0
                    t_sq = p.op("scalar", (lambda hc, sp: lambda e: e.activation(
                        sqt[sp][:, :], xnb[hc][:, :], AF.Square))(hc, sp),
                        waits=ev_toks + [rms_mm[sp]])
                    mm2 = p.op("tensor", (lambda hc, sp: lambda e: e.matmul(
                        psf[0][0:1, :], onescol[:, :], sqt[sp][:, :],
                        start=(hc == 0), stop=(hc == DC - 1)))(hc, sp),
                        waits=[t_sq] + ([y_free2[0]] if hc == 0 else []), sem="pe")
                    rms_mm[sp] = mm2
                t_q2 = p.op("vector", lambda e: e.tensor_scalar(
                    rows3[0:1, 0:512], psf[0][0:1, :], 1.0 / D, None, Alu.mult),
                    waits=[mm2])
                t_l2 = p.op("scalar", (lambda: lambda e: e.activation(
                    rows3[0:1, 0:512], rows3[0:1, 0:512], AF.Ln, bias=EPS_RMS))(),
                    waits=[t_q2])
                t_e3 = p.op("scalar", lambda e: e.activation(
                    rows3[0:1, 0:512], rows3[0:1, 0:512], AF.Exp, scale=-0.5),
                    waits=[t_l2])
                t_rb = p.op("vector", lambda e: e.tensor_copy(rbrow[0:1, :], rows3[0:1, 0:512]),
                            waits=[t_e3])
                t_b3 = p.op("tensor", lambda e: e.matmul(
                    psf[1][:, :], onesr1[:, :], rbrow[0:1, :], start=True, stop=True),
                    waits=[t_rb, u_free], sem="pe")
                t_su = None
                for hc in range(DC):
                    t_t2 = p.op("vector", (lambda hc: lambda e: e.tensor_tensor(
                        sqt[0][:, :], xnb[hc][:, :], psf[1][:, :], Alu.mult))(hc),
                        waits=[t_b3, mm2])
                    t_su = p.op("vector", (lambda hc, tt: lambda e: e.tensor_tensor(
                        stream[hc][:, tt * 512:(tt + 1) * 512],
                        stream[hc][:, tt * 512:(tt + 1) * 512], sqt[0][:, :],
                        Alu.add))(hc, tt),
                        waits=[t_t2, stream_tok[hc][tt]])
                    stream_tok[hc][tt] = t_su
                bank_gate = [p.pe_now(), t_su]
            esD.close()

        def emit_attnb():
            esE = contextlib.ExitStack()
            sbe = mksb(esE)
            xnb = [sbe(f"ex{i}", [128, 512], BF16) for i in range(DC)]
            sqt = [sbe(f"esq{i}", [128, 512], BF16) for i in range(2)]
            rows3 = sbe("er3", [1, 1024], F32)
            rbrow = sbe("erb", [1, 512], BF16)
            wk = sbe("ewk", [128, 2048], BF16)
            wv = sbe("ewv", [128, 2048], BF16)
            kb = sbe("ekb", [128, 512], BF16)
            kvf = sbe("ekv", [128, 512], F32)
            kvtf = sbe("ekt", [128, 512], F32)
            kvtb = sbe("ektb", [128, 512], BF16)
            onesf = sbe("eon", [128, 512], F32)
            carry = sbe("ecar", [128, 16], F32)
            ccol = sbe("eccl", [128, 16], F32)
            agi_sb = sbe("eagi", [16, 128], F32)
            ago_sb = sbe("eago", [8, D], F32)
            w8_sb = sbe("ew8", [8, 1], F32)
            vhat = [sbe(f"evh{i}", [128, 512], BF16) for i in range(DC)]
            wot = [sbe(f"ewo{i}", [128, 768], BF16) for i in range(2)]
            lnfm = make_lnfm(xnb, sqt, rows3, rbrow, EPS_LN)
            pg = p.phase_gate()
            t_on = p.op("vector", lambda e: e.memset(onesf[:, :], 1.0))
            t_zc = p.op("vector", lambda e: e.memset(carry[:, :], 0.0))
            t_w8 = p.dma("sync", w8_sb[:, :], w8_t.ap(), waits=pg, sem="dw8")
            bank_gate = pg
            wk_free = None
            wv_free = None
            kb_free = [None]
            kv_free = [None]
            kvtb_r = [None]
            xn_gate = pg
            for tt in range(NTT):
                t_app, _ = lnfm(tt, stream_tok, bank_gate, xnb_gate=xn_gate)
                bank_gate = [None]
                for dc in range(DC):
                    p.dma("sync", XN3.ap()[dc * 128:(dc + 1) * 128,
                                           tt * 512:(tt + 1) * 512],
                          xnb[dc][:, :], waits=[t_app[dc]], sem="dxn")
                for ec in range(DC):
                    egate = pg if (tt == 0 and ec == 0) else []
                    tk = [p.dma("gpsimd", wk[:, :], BWk.ap()[ec * 128:(ec + 1) * 128, :],
                                waits=[wk_free] + egate, sem="dek")]
                    tv = [p.dma("scalar", wv[:, :], BWv.ap()[ec * 128:(ec + 1) * 128, :],
                                waits=[wv_free] + egate, sem="dev")]

                    def mmk(e):
                        last = None
                        for dc in range(DC):
                            last = e.matmul(psf[0][:, :], wk[:, dc * 128:(dc + 1) * 128],
                                            xnb[dc][:, :], start=(dc == 0),
                                            stop=(dc == DC - 1))
                        return last

                    def mmv(e):
                        last = None
                        for dc in range(DC):
                            last = e.matmul(psf[1][:, :], wv[:, dc * 128:(dc + 1) * 128],
                                            xnb[dc][:, :], start=(dc == 0),
                                            stop=(dc == DC - 1))
                        return last
                    t_k = p.op("tensor", mmk, waits=tk + t_app + [kb_free[0]], sem="pe")
                    t_vv = p.op("tensor", mmv, waits=tv + t_app + [kv_free[0]], sem="pe")
                    t_kb = p.op("vector", lambda e: e.tensor_copy(kb[:, :], psf[0][:, :]),
                                waits=[t_k])
                    kb_free[0] = t_kb
                    t_kv = p.op("vector", lambda e: e.tensor_tensor(
                        kvf[:, :], psf[1][:, :], kb[:, :], Alu.mult), waits=[t_kb, t_vv])
                    kv_free[0] = t_kv
                    t_sc = p.op("vector", (lambda ec: lambda e: e.tensor_tensor_scan(
                        kvtf[:, :], onesf[:, :], kvf[:, :], carry[:, ec:ec + 1],
                        Alu.mult, Alu.add))(ec), waits=[t_kv, t_on, t_zc])
                    t_cu = p.op("vector", (lambda ec: lambda e: e.tensor_copy(
                        carry[:, ec:ec + 1], kvtf[:, 511:512]))(ec), waits=[t_sc])
                    t_cb = p.op("vector", (lambda: lambda e: e.tensor_copy(
                        kvtb[:, :], kvtf[:, :]))(), waits=[t_sc, kvtb_r[0]])
                    kvtb_r[0] = p.dma("sync", KVT.ap()[ec * 128:(ec + 1) * 128,
                                                       tt * 512:(tt + 1) * 512],
                                      kvtb[:, :], waits=[t_cb], sem="dkt")
                    wk_free = t_k
                    wv_free = t_vv
                bank_gate = [p.pe_now()]
                xn_gate = [p.pe_now(), ("dxn", p.counts["dxn"])]
            t_tpc = p.op("tensor", lambda e: e.transpose(
                psf[2][0:16, 0:128], carry[:, :], identf[:, :]),
                waits=[("_ser_vector", p.counts["_ser_vector"]), t_id], sem="pe")
            t_agi = p.op("vector", lambda e: e.tensor_copy(agi_sb[:, :], psf[2][0:16, 0:128]),
                         waits=[t_tpc])
            t_agid = p.dma("gpsimd", bass.AP(AGI, 0, [[128, 16], [1, 128]]), agi_sb[:, :],
                           waits=[t_agi], sem="dagi")
            t_cc = p.op("gpsimd", lambda e: e.collective_compute(
                "AllGather", Alu.bypass, replica_groups=[list(range(8))],
                ins=[AGI.ap().opt()], outs=[AGO.ap().opt()]),
                waits=[t_agid], sem="cc", by=1)
            t_ago = p.dma("gpsimd", ago_sb[:, :], AGO.ap(), waits=[("cc", 1)], sem="dago")
            ccol_toks = []
            for ec in range(DC):
                t_sel = p.op("tensor", (lambda ec: lambda e: e.matmul(
                    psf[2][0:128, 0:1], ago_sb[0:8, ec * 128:(ec + 1) * 128],
                    w8_sb[:, :], start=True, stop=True))(ec),
                    waits=[t_ago, t_w8, t_agi if ec == 0 else None,
                           ccol_toks[-1] if ec else None], sem="pe")
                t_ce = p.op("vector", (lambda ec: lambda e: e.tensor_copy(
                    ccol[:, ec:ec + 1], psf[2][0:128, 0:1]))(ec), waits=[t_sel])
                ccol_toks.append(t_ce)
            t_ccol = ccol_toks[-1]
            wq_free = p.pe_now()
            q_free[0] = p.pe_now()
            kvtb_free = [kvtb_r[0]]
            vh_free = [None] * DC
            wot_free = [None, None]
            xnb_free2 = [p.pe_now()] * DC
            for tt in range(NTT):
                xl = []
                for dc in range(DC):
                    xl.append(p.dma("sync", xnb[dc][:, :],
                                    XN3.ap()[dc * 128:(dc + 1) * 128,
                                             tt * 512:(tt + 1) * 512],
                                    waits=[("dxn", p.counts["dxn"]), xnb_free2[dc]],
                                    sem="dxl"))
                for ec in range(DC):
                    tq = [p.dma("gpsimd", wk[:, :], BWq.ap()[ec * 128:(ec + 1) * 128, :],
                                waits=[wq_free], sem="deq")]

                    def mmq(e):
                        last = None
                        for dc in range(DC):
                            last = e.matmul(psf[0][:, :], wk[:, dc * 128:(dc + 1) * 128],
                                            xnb[dc][:, :], start=(dc == 0),
                                            stop=(dc == DC - 1))
                        return last
                    t_qq = p.op("tensor", mmq, waits=tq + xl + [q_free[0]], sem="pe")
                    wq_free = t_qq
                    t_kl = p.dma("sync", kvtb[:, :],
                                 KVT.ap()[ec * 128:(ec + 1) * 128,
                                          tt * 512:(tt + 1) * 512],
                                 waits=[("dkt", p.counts["dkt"])] + list(kvtb_free), sem="dkl")
                    t_kc = p.op("vector", (lambda ec: lambda e: e.tensor_scalar(
                        kvf[:, :], kvtb[:, :], ccol[:, ec:ec + 1], None, Alu.add))(ec),
                        waits=[t_kl, t_ccol])
                    kvtb_free = [t_kc]
                    t_vh = p.op("vector", (lambda ec: lambda e: e.scalar_tensor_tensor(
                        vhat[ec][:, :], kvf[:, :], DENOM_B, psf[0][:, :],
                        Alu.mult, Alu.mult))(ec),
                        waits=[t_kc, t_qq, vh_free[ec]])
                    q_free[0] = t_vh
                groups = [list(range(0, 6)), list(range(6, 12)), list(range(12, 16))]
                for grp in groups:
                    t_d = None
                    for ec in range(DC):
                        wpar = ec % 2
                        t_w = p.dma("scalar", wot[wpar][:, 0:len(grp) * 128],
                                    bass.AP(BWot, ec * 128 * 2048 + grp[0] * 128,
                                            [[2048, 128], [1, len(grp) * 128]]),
                                    waits=[wot_free[wpar]] + (pg if wot_free[wpar] is None else []),
                                    sem=f"dew{wpar}")

                        def mmo(e, grp=grp, ec=ec, wpar=wpar):
                            last = None
                            for j in range(len(grp)):
                                last = e.matmul(psf[j][:, :],
                                                wot[wpar][:, j * 128:(j + 1) * 128],
                                                vhat[ec][:, :], start=(ec == 0),
                                                stop=(ec == DC - 1))
                            return last
                        t_d = p.op("tensor", mmo,
                                   waits=[t_w,
                                          p.last_tok["vector"] if ec == 0 else None]
                                   + ([yb_free[j] for j in range(len(grp))] if ec == 0 else []),
                                   sem="pe")
                        wot_free[wpar] = t_d
                    for j, fc in enumerate(grp):
                        t_su = p.op("vector", (lambda j, fc, tt: lambda e: e.tensor_tensor(
                            stream[fc][:, tt * 512:(tt + 1) * 512],
                            stream[fc][:, tt * 512:(tt + 1) * 512], psf[j][:, :],
                            Alu.add))(j, fc, tt),
                            waits=[t_d, stream_tok[fc][tt]])
                        yb_free[j] = t_su
                        stream_tok[fc][tt] = t_su
                pe_now2 = p.pe_now()
                for ec in range(DC):
                    vh_free[ec] = pe_now2
                for dc in range(DC):
                    xnb_free2[dc] = pe_now2
                q_free[0] = yb_free[0]
            esE.close()

        q_free = [None]
        yb_free = [None] * 6
        if STAGE >= 2:
            emit_ffn(0)
        if STAGE >= 3:
            emit_attnb()
        if STAGE >= 4:
            emit_ffn(1)

        esG = contextlib.ExitStack()
        sbgg = mksb(esG)
        otile = [sbgg(f"otile{i}", [128, D], F32) for i in range(2)]
        ot_free = [None, None]
        gtp_free = p.last_tok["vector"]
        for tb in range(16):
            par = tb % 2
            ev_last = None
            for dc in range(DC):
                t_tp = p.op("tensor", (lambda dc, tb: lambda e: e.transpose(
                    psf[5][:, 0:128], stream[dc][:, tb * 128:(tb + 1) * 128],
                    identf[:, :]))(dc, tb),
                    waits=[stream_tok[dc][tb // 4], gtp_free, ot_free[par]], sem="pe")
                ev_last = p.op("vector", (lambda dc, par: lambda e: e.tensor_copy(
                    otile[par][:, dc * 128:(dc + 1) * 128], psf[5][:, 0:128]))(dc, par),
                    waits=[t_tp])
                gtp_free = ev_last
            t_o = p.dma("sync", out_t.ap()[tb * 128:(tb + 1) * 128, :], otile[par][:, :],
                        waits=[ev_last], sem=f"dout{par}")
            ot_free[par] = t_o
        p.wait("sync", ("dout0", p.counts["dout0"]), ("dout1", p.counts["dout1"]))
        esG.close()

        with nc.Block() as block:
            p.finalize(block)
    return nc, p.const_host()



def make_mask(off):
    pp = np.arange(128)[:, None]
    yy = np.arange(MASK_W)[None, :]
    return (yy >= pp + MASK_Y0 - off).astype(BF16NP)


_CACHE = {}


def kernel(L0_attn_a_x, L0_attn_a_Wqkv, L0_attn_a_bqkv, L0_attn_a_Wo, L0_attn_a_bo,
           L0_attn_b_Wq, L0_attn_b_Wk, L0_attn_b_Wv, L0_attn_b_Wo,
           L0_ffn_gate_up_packed, L0_ffn_gate_up_scales,
           L0_ffn_down_packed, L0_ffn_down_scales):
    if "nc" not in _CACHE:
        _CACHE["nc"], _CACHE["consts"] = build_nc()
    nc = _CACHE["nc"]
    consts = _CACHE["consts"]

    x = np.ascontiguousarray(np.asarray(L0_attn_a_x, dtype=np.float32))
    wqkv = np.ascontiguousarray(np.asarray(L0_attn_a_Wqkv, dtype=np.float32))
    wqkvT = np.ascontiguousarray(
        wqkv.reshape(16, 128, 48, 128).transpose(2, 1, 0, 3).reshape(48 * 128, 2048))
    bqkv = np.ascontiguousarray(np.asarray(L0_attn_a_bqkv, dtype=np.float32))
    woa = np.ascontiguousarray(np.asarray(L0_attn_a_Wo, dtype=np.float32))
    bo = np.ascontiguousarray(np.asarray(L0_attn_a_bo, dtype=np.float32))
    ident = np.eye(128, dtype=np.float32)
    gupT = np.ascontiguousarray(np.asarray(L0_ffn_gate_up_packed, dtype=np.int32).T)
    gusc = np.ascontiguousarray(np.asarray(L0_ffn_gate_up_scales, dtype=np.float16))
    dnpT = np.ascontiguousarray(np.asarray(L0_ffn_down_packed, dtype=np.int32).T)
    dnsc = np.ascontiguousarray(np.asarray(L0_ffn_down_scales, dtype=np.float16))
    _sh = 4 * (np.arange(128) % 8)
    shifts = (np.uint32(15) << _sh.astype(np.uint32)).astype(np.uint32).view(np.int32).reshape(128, 1).copy()
    invs = (2.0 ** (-_sh.astype(np.float64))).astype(np.float32).reshape(128, 1)
    wqt = np.ascontiguousarray(np.asarray(L0_attn_b_Wq, dtype=np.float32).T)
    wkt = np.ascontiguousarray(np.asarray(L0_attn_b_Wk, dtype=np.float32).T)
    wvt = np.ascontiguousarray(np.asarray(L0_attn_b_Wv, dtype=np.float32).T)
    wot = np.ascontiguousarray(np.asarray(L0_attn_b_Wo, dtype=np.float32).T)

    in_maps = []
    for c in range(8):
        b, half = c // 2, c % 2
        off = half * OWN
        w8 = np.zeros((8, 1), dtype=np.float32)
        if half == 1:
            w8[2 * b, 0] = 1.0
        in_maps.append({
            "x_own": np.ascontiguousarray(x[b, off:off + OWN]),
            "x_full": np.ascontiguousarray(x[b]),
            "consts": consts,
            "wqkv": wqkv,
            "wqkvT": wqkvT,
            "bqkv": bqkv,
            "woa": woa,
            "bo": bo,
            "mask": make_mask(off),
            "identf": ident,
            "gupT": gupT,
            "gusc": gusc,
            "dnpT": dnpT,
            "dnsc": dnsc,
            "shifts": shifts,
            "invs": invs,
            "wqt": wqt,
            "wkt": wkt,
            "wvt": wvt,
            "wot": wot,
            "w8": w8,
        })
    res = run_bass_kernel_spmd(nc, in_maps, core_ids=list(range(8)))
    out = np.empty((4, S, D), dtype=np.float32)
    for c in range(8):
        b, half = c // 2, c % 2
        out[b, half * OWN:(half + 1) * OWN] = res.results[c]["out"]
    return out


# version 11 (final)
# speedup vs baseline: 1.1146x; 1.1146x over previous
import contextlib
import numpy as np
import ml_dtypes
import concourse.bass as bass
import concourse.mybir as mybir
from concourse.bass_utils import run_bass_kernel_spmd
from concourse.alu_op_type import AluOpType as Alu

F32 = mybir.dt.float32
BF16 = mybir.dt.bfloat16
F16 = mybir.dt.float16
I32 = mybir.dt.int32
U8 = mybir.dt.uint8
AF = mybir.ActivationFunctionType
AX = mybir.AxisListType
BF16NP = ml_dtypes.bfloat16

ENGINES = ("sync", "tensor", "vector", "scalar", "gpsimd")
SERIAL_ENGINES = ("vector", "scalar", "gpsimd")

D = 2048
DC = 16
NH = 16
HD = 128
S = 4096
OWN = 2048
TT = 512
NTT = OWN // TT
SCALE = 1.0 / float(np.sqrt(128.0))
EPS_LN = 1e-5
EPS_RMS = 1e-6
II = 5632
DENOM_B = float(np.sqrt(2048.0) / 4096.0)

MASK_Y0 = 2432
MASK_W = MASK_Y0 + 512

STAGE = 4
ATTN_LA = 2


class Prog:
    def __init__(self, nc, es):
        self.nc = nc
        self.es = es
        self.ops = {e: [] for e in ENGINES}
        self.counts = {}
        self.sems = {}
        self.last_tok = {e: None for e in ENGINES}
        self.ndma = 0

    def sem(self, name):
        if name not in self.sems:
            self.sems[name] = self.es.enter_context(self.nc.semaphore(name))
            self.counts[name] = 0
        return self.sems[name]

    MAX_CONSTS = 32

    def const_init(self, dram_handle):
        self._const_map = {}
        self._const_vals = []
        self._const_sb = self.es.enter_context(
            self.nc.sbuf_tensor("const_pool", [128, self.MAX_CONSTS], F32))
        self.const_tok = self.dma("sync", self._const_sb[:, :], dram_handle.ap(), sem="dcst")

    def const(self, val):
        val = float(val)
        if val not in self._const_map:
            i = len(self._const_vals)
            assert i < self.MAX_CONSTS
            self._const_vals.append(val)
            ap = self._const_sb[:, i:i + 1]
            self.nc.const_aps.register(F32, val, ap)
            self._const_map[val] = ap
        return self._const_map[val]

    def const_host(self):
        arr = np.zeros((128, self.MAX_CONSTS), dtype=np.float32)
        for i, v in enumerate(self._const_vals):
            arr[:, i] = v
        return arr

    def _tok(self, name, by):
        self.sem(name)
        self.counts[name] += by
        return (name, self.counts[name])

    def op(self, eng, emit, waits=(), sem=None, by=1, serialize=None):
        if serialize is None:
            serialize = eng in SERIAL_ENGINES
        w = [t for t in waits if t is not None]
        if serialize and self.last_tok[eng] is not None:
            w.append(self.last_tok[eng])
        if sem is None and serialize:
            sem = "_ser_" + eng
        tok = self._tok(sem, by) if sem is not None else None
        self.ops[eng].append((emit, w, tok, by))
        if serialize:
            self.last_tok[eng] = tok
        return tok

    def dma(self, eng, out, in_, waits=(), sem="dma", **kw):
        self.ndma += 1

        def emit(e, out=out, in_=in_, kw=kw):
            return e.dma_start(out, in_, **kw)
        return self.op(eng, emit, waits=waits, sem=sem, by=16, serialize=False)

    def wait(self, eng, *toks):
        self.ops[eng].append((None, [t for t in toks if t is not None], None, 0))

    def pe_now(self):
        return ("pe", self.counts.get("pe", 0)) if self.counts.get("pe", 0) else None

    def phase_gate(self):
        return [t for t in (self.pe_now(), self.last_tok["scalar"],
                            self.last_tok["vector"], self.last_tok["gpsimd"])
                if t is not None]

    def finalize(self, block):
        for ename in ENGINES:
            ops = self.ops[ename]
            if not ops:
                continue
            sems = self.sems

            def run(e, ops=ops):
                water = {}
                for emit, waits, tok, by in ops:
                    need = {}
                    for (sn, sv) in waits:
                        if sv > need.get(sn, 0):
                            need[sn] = sv
                    for sn, sv in need.items():
                        if water.get(sn, 0) >= sv:
                            continue
                        e.wait_ge(sems[sn], sv)
                        water[sn] = sv
                    if emit is None:
                        continue
                    instr = emit(e)
                    if tok is not None:
                        instr.then_inc(sems[tok[0]], by)
            getattr(block, ename)(run)



def build_nc():
    nc = bass.Bass("TRN2", target_bir_lowering=False)

    x_own_t = nc.dram_tensor("x_own", [OWN, D], F32, kind="ExternalInput")
    x_full_t = nc.dram_tensor("x_full", [S, D], F32, kind="ExternalInput")
    consts_t = nc.dram_tensor("consts", [128, 32], F32, kind="ExternalInput")
    wqkv_t = nc.dram_tensor("wqkv", [D, 3 * D], F32, kind="ExternalInput")
    wqkvT_t = nc.dram_tensor("wqkvT", [48 * 128, 2048], F32, kind="ExternalInput")
    bqkv_t = nc.dram_tensor("bqkv", [3 * D], F32, kind="ExternalInput")
    woa_t = nc.dram_tensor("woa", [D, D], F32, kind="ExternalInput")
    bo_t = nc.dram_tensor("bo", [D], F32, kind="ExternalInput")
    mask_t = nc.dram_tensor("mask", [128, MASK_W], BF16, kind="ExternalInput")
    ident_t = nc.dram_tensor("identf", [128, 128], F32, kind="ExternalInput")
    gupT_t = nc.dram_tensor("gupT", [256, 2 * II], I32, kind="ExternalInput")
    gusc_t = nc.dram_tensor("gusc", [16, 2 * II], F16, kind="ExternalInput")
    dnpT_t = nc.dram_tensor("dnpT", [704, D], I32, kind="ExternalInput")
    dnsc_t = nc.dram_tensor("dnsc", [44, D], F16, kind="ExternalInput")
    shift_t = nc.dram_tensor("shifts", [128, 1], I32, kind="ExternalInput")
    invs_t = nc.dram_tensor("invs", [128, 1], F32, kind="ExternalInput")
    wqt_t = nc.dram_tensor("wqt", [D, D], F32, kind="ExternalInput")
    wkt_t = nc.dram_tensor("wkt", [D, D], F32, kind="ExternalInput")
    wvt_t = nc.dram_tensor("wvt", [D, D], F32, kind="ExternalInput")
    wot_t = nc.dram_tensor("wot", [D, D], F32, kind="ExternalInput")
    w8_t = nc.dram_tensor("w8", [8, 1], F32, kind="ExternalInput")

    out_t = nc.dram_tensor("out", [OWN, D], F32, kind="ExternalOutput")

    Ksc = nc.dram_tensor("Ksc", [D, S], BF16)
    Vsc = nc.dram_tensor("Vsc", [16 * 128, 32 * 128], BF16)
    Qsc = nc.dram_tensor("Qsc", [D, OWN], BF16)
    WGU = nc.dram_tensor("WGU", [88 * 128, 2048], BF16)
    WDN = nc.dram_tensor("WDN", [44 * 128, 2048], BF16)
    BWq = nc.dram_tensor("BWq", [16 * 128, 2048], BF16)
    BWk = nc.dram_tensor("BWk", [16 * 128, 2048], BF16)
    BWv = nc.dram_tensor("BWv", [16 * 128, 2048], BF16)
    BWot = nc.dram_tensor("BWot", [16 * 128, 2048], BF16)
    XN3 = nc.dram_tensor("XN3", [D, OWN], BF16)
    KVT = nc.dram_tensor("KVT", [D, OWN], BF16)
    AGI = nc.dram_tensor("AGI", [1, D], F32)
    AGO = nc.dram_tensor("AGO", [8, D], F32)

    es = contextlib.ExitStack()
    with es:
        p = Prog(nc, es)
        p.const_init(consts_t)
        p.const(EPS_LN)
        p.const(EPS_RMS)

        def mksb(stack):
            def sb(n, sh, dt):
                return stack.enter_context(nc.sbuf_tensor(n, sh, dt))
            return sb

        sbg = mksb(es)
        identf = sbg("identf_sb", [128, 128], F32)
        onesr1 = sbg("onesr1", [1, 128], BF16)
        onescol = sbg("onescol", [128, 1], BF16)
        bo_col = sbg("bo_col", [128, 16], F32)
        mh_col = sbg("mh_col", [128, NH], F32)

        psf = [es.enter_context(nc.psum_tensor(f"psf{i}", [128, 512], F32)) for i in range(6)]
        psbk = [es.enter_context(nc.psum_tensor(f"psb{i}", [128, 1024], BF16)) for i in range(2)]

        t_id = p.dma("sync", identf[:, :], ident_t.ap(), sem="did")
        t_ones = p.op("vector", lambda e: e.memset(onesr1[:, :], 1.0))
        t_onec = p.op("vector", lambda e: e.memset(onescol[:, :], 1.0))
        t_bcol = None
        for oc in range(16):
            t_bcol = p.dma("sync", bo_col[:, oc:oc + 1],
                           bass.AP(bo_t, oc * 128, [[1, 128], [1, 1]]), sem="dbo")

        esA = contextlib.ExitStack()
        sba = mksb(esA)
        nrm = sba("nrm", [1, 256], F32)
        rows = sba("rows", [1, 64], F32)
        rows_b = sba("rows_b", [1, 64], BF16)
        identb = sba("identb_sb", [128, 128], BF16)
        bqkv_col = sba("bqkv_col", [128, 32], F32)
        xtm = [sba(f"xtm{i}", [128, D], F32) for i in range(2)]
        xntm = [sba(f"xntm{i}", [128, D], BF16) for i in range(2)]
        xnf = sba("xnf", [128, DC * 2048], BF16)
        bns24 = sba("bns24", [128, 24], F32)
        bna = sba("bna", [128, 2], F32)
        lncol = sba("lncol", [128, 4], F32)
        wstg = [sba(f"wstg{i}", [128, 2048], F32) for i in range(2)]
        wbf = [sba(f"wbf{i}", [128, 2048], BF16) for i in range(2)]
        wvstg = sba("wvstg", [128, 512], F32)
        wvbf = sba("wvbf", [128, DC * 512], BF16)
        kst = [sba(f"kst{i}", [128, 512], BF16) for i in range(2)]
        ksq = sba("ksq", [128, 512], BF16)
        vst = [sba(f"vst{i}", [128, 512], BF16) for i in range(2)]
        vrow = sba("vrow", [1, 512], F32)
        vrowb = sba("vrowb", [1, 512], BF16)

        t_idb = p.op("vector", lambda e: e.tensor_copy(identb[:, :], identf[:, :]), waits=[t_id])
        t_bqc = None
        for oc in range(16):
            p.dma("sync", bqkv_col[:, oc:oc + 1],
                  bass.AP(bqkv_t, oc * 128, [[1, 128], [1, 1]]), sem="dbq")
            t_bqc = p.dma("sync", bqkv_col[:, 16 + oc:17 + oc],
                  bass.AP(bqkv_t, D + oc * 128, [[1, 128], [1, 1]]), sem="dbq")

        NBP = 6
        dq_w = [sba(f"dqw{i}", [128, 512], I32) for i in range(NBP)]
        dq_f_b = [sba(f"dq_f{i}", [128, 512], BF16) for i in range(NBP)]
        dq_s_b = [sba(f"dq_s{i}", [128, 512], BF16) for i in range(NBP)]
        dq_m_b = [sba(f"dq_m{i}", [128, 512], BF16) for i in range(NBP)]
        dq_t1_b = [sba(f"dq_t1{i}", [128, 512], BF16) for i in range(NBP)]
        dq_out = [sba(f"dqo{i}", [128, 512], BF16) for i in range(NBP)]
        dq_srow = [sba(f"dq_srow{i}", [1, 512], F16) for i in range(NBP)]
        dq_srowb = [sba(f"dq_srowb{i}", [1, 512], BF16) for i in range(NBP)]
        shift_sb = sba("shift_sb", [128, 1], I32)
        invs_sb = sba("invs_sb", [128, 1], F32)
        wcst = [sba(f"wcst{i}", [128, 1024], F32) for i in range(2)]
        wcb = sba("wcb", [128, 1024], BF16)
        t_shift = p.dma("sync", shift_sb[:, :], shift_t.ap(), sem="dsh")
        t_invs = p.dma("sync", invs_sb[:, :], invs_t.ap(), sem="dsi")
        p.const(1.0)
        p.const(-4.0)
        p.const(-3.5)
        p.const(-6.0)

        DQ_TILES = ([("gu", c, o0) for c in range(16) for o0 in range(0, 2 * II, 512)]
                    + [("dn", c, o0) for c in range(44) for o0 in range(0, D, 512)])
        dq_state = {"w": [None] * NBP, "o": [None] * NBP, "bank": [None, None],
                    "srow": [None] * NBP, "srowb": [None] * NBP,
                    "f_r": [None] * NBP, "s_r": [None] * NBP, "m_r": [None] * NBP,
                    "t1_r": [None] * NBP}

        NT_DQ = len(DQ_TILES)
        dq_cur = [0] * 12
        dq_tok = {}

        def _dq_info(j):
            kind, c, o0 = DQ_TILES[j]
            if kind == "gu":
                return (gupT_t, 2 * II, gusc_t, WGU, c, o0, j % NBP, 3 + (j % 2), kind)
            return (dnpT_t, D, dnsc_t, WDN, c, o0, j % NBP, 3 + (j % 2), kind)

        def dq_advance(n):
            st = dq_state
            olds = list(dq_cur)
            NB = NBP
            lims = [0] * 12
            lims[0] = min(olds[0] + n, NT_DQ, olds[1] + NB, olds[3] + NB)
            lims[1] = min(olds[0], olds[10] + NB)
            lims[2] = min(olds[1], olds[3] + NB)
            lims[3] = min(olds[2], olds[5] + NB)
            lims[4] = min(olds[3], olds[9] + NB)
            lims[5] = min(olds[4], olds[9] + NB, olds[10] + NB)
            lims[6] = min(olds[5], olds[8] + NB)
            lims[7] = olds[6]
            lims[8] = olds[7]
            lims[9] = olds[8]
            lims[10] = min(olds[9], olds[11] + NB)
            lims[11] = olds[10]
            for k in range(12):
                lims[k] = max(lims[k], olds[k])
            for k in range(12):
                for j in range(dq_cur[k], lims[k]):
                    srcw, ncols, scl, wdst, c, o0, par, bk, kind = _dq_info(j)
                    dq_f, dq_s, dq_m, dq_t1 = (dq_f_b[par], dq_s_b[par],
                                               dq_m_b[par], dq_t1_b[par])
                    if k == 0:
                        dq_tok[(0, j)] = p.dma("gpsimd", dq_w[par][:, :],
                                               bass.AP(srcw, (16 * c) * ncols + o0,
                                                       [[ncols, 16], [0, 8], [1, 512]]),
                                               waits=[st["w"][par]], sem=f"ddq{par}")
                        dq_tok[("sr", j)] = p.dma("gpsimd", dq_srow[par][:, :],
                                                  bass.AP(scl, c * ncols + o0,
                                                          [[1, 1], [1, 512]]),
                                                  waits=[st["srow"][par]], sem=f"ddqs{par}")
                    elif k == 1:
                        t = p.op("scalar", (lambda par: lambda e: e.mul(
                            dq_srowb[par][:, :], dq_srow[par][:, :], -1.0))(par),
                            waits=[dq_tok.pop(("sr", j)), st["srowb"][par]])
                        st["srow"][par] = t
                        dq_tok[(1, j)] = t
                    elif k == 2:
                        t = p.op("vector", (lambda par: lambda e: e.tensor_scalar(
                            dq_w[par][:, :], dq_w[par][:, :], shift_sb[:, :], None,
                            Alu.bitwise_and))(par),
                            waits=[dq_tok.pop((0, j)), t_shift])
                        dq_tok[(2, j)] = t
                    elif k == 3:
                        t = p.op("scalar", (lambda par, dq_f: lambda e: e.activation(
                            dq_f[:, :], dq_w[par][:, :], AF.Identity,
                            bias=-4.0, scale=invs_sb[:, :]))(par, dq_f),
                            waits=[dq_tok.pop((2, j)), t_invs, p.const_tok, st["f_r"][par]])
                        st["w"][par] = t
                        dq_tok[(3, j)] = t
                    elif k == 4:
                        dq_tok[(4, j)] = p.op("scalar", (lambda par, dq_f, dq_s: lambda e: e.activation(
                            dq_s[:, :], dq_f[:, :], AF.Sign, bias=-3.5))(par, dq_f, dq_s),
                            waits=[dq_tok[(3, j)], st["s_r"][par]])
                    elif k == 5:
                        t = p.op("vector", (lambda par, dq_s, dq_f, dq_m: lambda e: e.scalar_tensor_tensor(
                            dq_m[:, :], dq_s[:, :], -4.0, dq_f[:, :],
                            Alu.mult, Alu.add))(par, dq_s, dq_f, dq_m),
                            waits=[dq_tok.pop((4, j)), st["m_r"][par], dq_tok.pop((3, j))])
                        st["f_r"][par] = t
                        dq_tok[(5, j)] = t
                    elif k == 6:
                        dq_tok[(6, j)] = p.op("scalar", (lambda par, dq_m, dq_t1: lambda e: e.activation(
                            dq_t1[:, :], dq_m[:, :], AF.Relu, bias=-6.0))(par, dq_m, dq_t1),
                            waits=[dq_tok[(5, j)], st["t1_r"][par]])
                    elif k == 7:
                        dq_tok[(7, j)] = p.op("vector", (lambda par, dq_m, dq_t1: lambda e: e.scalar_tensor_tensor(
                            dq_t1[:, :], dq_m[:, :], -2.0, dq_t1[:, :],
                            Alu.add, Alu.add))(par, dq_m, dq_t1),
                            waits=[dq_tok.pop((6, j))])
                    elif k == 8:
                        t = p.op("vector", (lambda par, dq_m, dq_t1: lambda e: e.scalar_tensor_tensor(
                            dq_m[:, :], dq_m[:, :], 0.5, dq_t1[:, :],
                            Alu.mult, Alu.max))(par, dq_m, dq_t1),
                            waits=[dq_tok.pop((7, j)), dq_tok.pop((5, j))])
                        st["t1_r"][par] = t
                        dq_tok[(8, j)] = t
                    elif k == 9:
                        t = p.op("gpsimd", (lambda par, dq_s, dq_m: lambda e: e.tensor_tensor(
                            dq_m[:, :], dq_s[:, :], dq_m[:, :], Alu.mult))(par, dq_s, dq_m),
                            waits=[dq_tok.pop((8, j))])
                        st["s_r"][par] = t
                        st["m_r"][par] = t
                        dq_tok[(9, j)] = t
                    elif k == 10:
                        tbc = p.op("tensor", (lambda bk, par: lambda e: e.matmul(
                            psf[bk][:, :], onesr1[:, :], dq_srowb[par][0:1, :],
                            start=True, stop=True))(bk, par),
                            waits=[dq_tok.pop((1, j)), t_ones, st["bank"][j % 2]], sem="pe")
                        st["srowb"][par] = tbc
                        t_out = p.op("vector", (lambda par, bk, dq_m: lambda e: e.tensor_tensor(
                            dq_out[par][:, :], dq_m[:, :], psf[bk][:, :],
                            Alu.mult))(par, bk, dq_m),
                            waits=[dq_tok.pop((9, j)), tbc, st["o"][par]])
                        st["bank"][j % 2] = t_out
                        dq_tok[(10, j)] = t_out
                    else:
                        if kind == "gu":
                            dst = bass.AP(WGU, (o0 // 128) * 128 * 2048 + c * 128,
                                          [[2048, 128], [128 * 2048, 4], [1, 128]])
                        else:
                            dst = bass.AP(WDN, c * 128 * 2048 + o0, [[2048, 128], [1, 512]])
                        st["o"][par] = p.dma("gpsimd", dst, dq_out[par][:, :],
                                             waits=[dq_tok.pop((10, j))], sem=f"ddqo{par}")
                dq_cur[k] = lims[k]

        def emit_dequant(lo, hi):
            while dq_cur[11] < hi:
                dq_advance(max(2, hi - dq_cur[0]))

        bw_state = {"l": [None, None], "c": None}

        def emit_wcast(lo, hi):
            pairs = [(wqt_t, BWq), (wkt_t, BWk), (wvt_t, BWv), (wot_t, BWot)]
            for i in range(lo, hi):
                wi, r = i // 16, i % 16
                srct, dstt = pairs[wi]
                for hf in range(2):
                    par = (2 * i + hf) % 2
                    t_l = p.dma("scalar", wcst[par][:, :],
                                srct.ap()[r * 128:(r + 1) * 128, hf * 1024:(hf + 1) * 1024],
                                waits=[bw_state["l"][par]], sem=f"dbw{par}")
                    t_c = p.op("vector", (lambda par: lambda e: e.tensor_copy(
                        wcb[:, :], wcst[par][:, :]))(par), waits=[t_l, bw_state["c"]])
                    bw_state["l"][par] = t_c
                    if wi < 3:
                        dst = bass.AP(dstt, (hf * 8) * 128 * 2048 + r * 128,
                                      [[2048, 128], [128 * 2048, 8], [1, 128]])
                    else:
                        dst = bass.AP(dstt, r * 128 * 2048 + hf * 1024,
                                      [[2048, 128], [1, 1024]])
                    t_s = p.dma("scalar", dst, wcb[:, :], waits=[t_c], sem="dbws")
                    bw_state["c"] = t_s

        dq_cursor = [0, 0]

        def dq_tick(n):
            dq_advance(n)
            lo2 = dq_cursor[1]
            hi2 = min(lo2 + 1, 64)
            if hi2 > lo2:
                emit_wcast(lo2, hi2)
                dq_cursor[1] = hi2

        xtm_free = [None, None]
        psb_free = [None, None]
        wcast_free = [None, None]
        wbf_free = [None, None]
        wv_free = [None]
        wvbf_free = [None]
        kst_free = [[], []]
        ksq_free = [None]
        vst_free = [None, None]
        vrow_free = [None]
        bank_free = [None] * 7
        xnf_ready = None
        xnf_read = None
        nrm_idx = [0]
        prev_xn = [None]
        prev_r1 = [None]

        def ln_tile(src_ap, par):
            xt = xtm[par]
            t_load = p.dma("sync", xt[:, :], src_ap, waits=[xtm_free[par]], sem=f"dax{par}")
            tb_tok = None
            for i in range(4):
                tb_tok = p.op("vector", (lambda i: lambda e: e.bn_stats(
                    bns24[:, i * 6:(i + 1) * 6], xt[:, i * 512:(i + 1) * 512]))(i),
                    waits=[t_load])
            t_ag = p.op("vector", lambda e: e.bn_aggr(bna[:, :], bns24[:, :]),
                        waits=[tb_tok, prev_r1[0]])
            t_r1 = p.op("scalar", lambda e: e.activation(
                lncol[:, 0:1], bna[:, 1:2], AF.Ln, bias=EPS_LN), waits=[t_ag, p.const_tok])
            prev_r1[0] = t_r1
            t_r2 = p.op("scalar", lambda e: e.activation(
                lncol[:, 0:1], lncol[:, 0:1], AF.Exp, scale=-0.5), waits=[t_r1])
            t_r3 = p.op("vector", lambda e: e.tensor_tensor(
                lncol[:, 1:2], bna[:, 0:1], lncol[:, 0:1], Alu.mult),
                waits=[t_r2, prev_xn[0]])
            t_r4 = p.op("vector", lambda e: e.tensor_scalar(
                lncol[:, 1:2], lncol[:, 1:2], -1.0, None, Alu.mult), waits=[t_r3])
            t_xn = p.op("scalar", (lambda par: lambda e: e.activation(
                xntm[par][:, :], xtm[par][:, :], AF.Identity,
                bias=lncol[:, 1:2], scale=lncol[:, 0:1]))(par), waits=[t_r4])
            prev_xn[0] = t_xn
            xtm_free[par] = t_xn
            return t_xn

        def transpose_to_xnf(par, tb, t_xn):
            last = None
            for grp in range(4):
                bk = grp % 2
                half = grp // 2
                dcs = [half * 8 + bk + 2 * j for j in range(4)]
                t_tp = None
                for j, dc in enumerate(dcs):
                    slot = half * 4 + j
                    t_tp = p.op("tensor", (lambda dc, bk, slot, par: lambda e: e.transpose(
                        psbk[bk][:, slot * 128:(slot + 1) * 128],
                        xntm[par][:, dc * 128:(dc + 1) * 128], identb[:, :]))(dc, bk, slot, par),
                        waits=[t_xn, t_idb, psb_free[bk], xnf_read], sem="pe")
                dst = bass.AP(xnf.handle if hasattr(xnf, 'handle') else xnf._handle, 0, [[1, 1]]) if False else None
                t_ev = p.op("vector", (lambda dcs, bk, half, tb: lambda e: e.tensor_copy(
                    bass.AP(xnf[:, :].tensor, tb * 128 + dcs[0] * 2048,
                            [[DC * 2048, 128], [2 * 2048, 4], [1, 128]]),
                    psbk[bk][:, half * 512:(half + 1) * 512]))(dcs, bk, half, tb),
                    waits=[t_tp])
                psb_free[bk] = t_ev
                last = t_ev
            return last

        def proj_fm(seg, kind):
            col0 = D if kind == "k" else 0
            scdst = Ksc if kind == "k" else Qsc
            for oc in range(NH):
                wpar = oc % 2
                ocg = (col0 // 128) + oc
                tw = [p.dma("sync", wstg[wpar][:, :],
                            wqkvT_t.ap()[ocg * 128:(ocg + 1) * 128, :],
                            waits=[wcast_free[wpar]], sem=f"daw{wpar}")]
                t_cast = p.op("vector", (lambda wpar: lambda e: e.tensor_copy(
                    wbf[wpar][:, :], wstg[wpar][:, :]))(wpar),
                    waits=tw + [wbf_free[wpar]])
                wcast_free[wpar] = t_cast
                for tt in range(4):
                    bk = (oc * 4 + tt) % 2
                    ps = psf[bk]

                    def mm(e, wpar=wpar, tt=tt, ps=ps):
                        last = None
                        for dc in range(DC):
                            last = e.matmul(
                                ps[:, :], wbf[wpar][:, dc * 128:(dc + 1) * 128],
                                xnf[:, dc * 2048 + tt * 512: dc * 2048 + (tt + 1) * 512],
                                start=(dc == 0), stop=(dc == DC - 1))
                        return last
                    t_mm = p.op("tensor", mm, waits=[t_cast, xnf_ready, bank_free[bk]],
                                sem="pe")
                    kpar = (oc * 4 + tt) % 2
                    ci = (16 + oc) if kind == "k" else oc
                    t_ev = p.op("vector", (lambda kpar, ps, ci: lambda e: e.tensor_scalar(
                        kst[kpar][:, :], ps[:, :], bqkv_col[:, ci:ci + 1], None,
                        Alu.add))(kpar, ps, ci),
                        waits=[t_mm, t_bqc] + kst_free[kpar])
                    bank_free[bk] = t_ev
                    colbase = (seg * 2048 if kind == "k" else 0) + tt * 512
                    t_st = p.dma("sync",
                                 scdst.ap()[oc * 128:(oc + 1) * 128, colbase: colbase + 512],
                                 kst[kpar][:, :], waits=[t_ev], sem=f"dsck{kpar}")
                    t_sq = p.op("scalar", (lambda kpar: lambda e: e.activation(
                        ksq[:, :], kst[kpar][:, :], AF.Square))(kpar),
                        waits=[t_ev, ksq_free[0]])
                    kst_free[kpar] = [t_st, t_sq]
                    t_nmm = p.op("tensor", lambda e: e.matmul(
                        psf[2][0:1, :], onescol[:, :], ksq[:, :], start=True, stop=True),
                        waits=[t_sq, t_onec, bank_free[2]], sem="pe")
                    idx = nrm_idx[0]
                    nrm_idx[0] += 1
                    t_red = p.op("vector", (lambda idx: lambda e: e.reduce_max(
                        nrm[0:1, idx:idx + 1], psf[2][0:1, :], AX.X, negate=True))(idx),
                        waits=[t_nmm])
                    ksq_free[0] = t_red
                    bank_free[2] = t_red
                wbf_free[wpar] = p.pe_now()
                dq_tick(5)

        def proj_v(seg):
            for vc2 in range(4):
                t_vr = p.dma("sync", vrow[:, :],
                             bass.AP(bqkv_t, 2 * D + vc2 * 512, [[1, 1], [1, 512]]),
                             waits=[vrow_free[0]], sem="davr")
                t_vrb = p.op("vector", lambda e: e.tensor_copy(vrowb[:, :], vrow[:, :]),
                             waits=[t_vr])
                t_casts = [t_vrb]
                for dc in range(DC):
                    t_l = p.dma("sync", wvstg[:, :],
                                bass.AP(wqkv_t, dc * 128 * (3 * D) + 2 * D + vc2 * 512,
                                        [[3 * D, 128], [1, 512]]),
                                waits=[wv_free[0]], sem="davw")
                    t_c = p.op("vector", (lambda dc: lambda e: e.tensor_copy(
                        wvbf[:, dc * 512:(dc + 1) * 512], wvstg[:, :]))(dc),
                        waits=[t_l, wvbf_free[0]])
                    wv_free[0] = t_c
                    t_casts.append(t_c)
                for tb in range(16):
                    bk = tb % 2
                    ps = psf[bk]

                    def mm(e, tb=tb, vc2=vc2, ps=ps):
                        for dc in range(DC):
                            e.matmul(ps[:, :],
                                     xnf[:, dc * 2048 + tb * 128: dc * 2048 + (tb + 1) * 128],
                                     wvbf[:, dc * 512:(dc + 1) * 512],
                                     start=(dc == 0), stop=False)
                        return e.matmul(ps[:, :], onesr1[:, :], vrowb[0:1, :],
                                        start=False, stop=True)
                    t_mm = p.op("tensor", mm,
                                waits=t_casts + [xnf_ready, bank_free[bk], t_ones],
                                sem="pe")
                    vpar = tb % 2
                    t_ev = p.op("vector", (lambda vpar, ps: lambda e: e.tensor_copy(
                        vst[vpar][:, :], ps[:, :]))(vpar, ps),
                        waits=[t_mm, vst_free[vpar]])
                    bank_free[bk] = t_ev
                    kc_ = seg * 16 + tb
                    t_st = p.dma("sync",
                                 bass.AP(Vsc, (vc2 * 4) * 128 * 4096 + kc_ * 128,
                                         [[4096, 128], [128 * 4096, 4], [1, 128]]),
                                 vst[vpar][:, :], waits=[t_ev], sem=f"dscv{vpar}")
                    vst_free[vpar] = t_st
                wvbf_free[0] = p.pe_now()
                vrow_free[0] = p.pe_now()
                dq_tick(5)

        for seg in range(3):
            for tb in range(16):
                t_xn = ln_tile(
                    x_full_t.ap()[seg * 2048 + tb * 128: seg * 2048 + (tb + 1) * 128, :]
                    if seg < 2 else x_own_t.ap()[tb * 128:(tb + 1) * 128, :],
                    tb % 2)
                xnf_ready = transpose_to_xnf(tb % 2, tb, t_xn)
                dq_tick(4)
            if seg < 2:
                proj_fm(seg, "k")
                proj_v(seg)
            else:
                proj_fm(seg, "q")
            xnf_read = p.pe_now()
            wbf_free[0] = xnf_read
            wbf_free[1] = xnf_read

        for h in range(NH):
            p.op("vector", (lambda h: lambda e: e.tensor_reduce(
                rows[0:1, h:h + 1], nrm[0:1, h * 4:(h + 1) * 4], AX.X, Alu.min))(h))
            p.op("vector", (lambda h: lambda e: e.tensor_reduce(
                rows[0:1, 16 + h:17 + h], nrm[0:1, 64 + h * 4:64 + (h + 1) * 4],
                AX.X, Alu.min))(h))
            p.op("vector", (lambda h: lambda e: e.tensor_reduce(
                rows[0:1, 32 + h:33 + h], nrm[0:1, 128 + h * 4:128 + (h + 1) * 4],
                AX.X, Alu.min))(h))
        p.op("vector", lambda e: e.tensor_tensor(
            rows[0:1, 0:16], rows[0:1, 0:16], rows[0:1, 16:32], Alu.min))
        p.op("vector", lambda e: e.tensor_tensor(
            rows[0:1, 0:16], rows[0:1, 0:16], rows[0:1, 32:48], Alu.add))
        p.op("vector", lambda e: e.tensor_scalar(
            rows[0:1, 0:16], rows[0:1, 0:16], SCALE / 2.0, None, Alu.mult))
        t7 = p.op("vector", lambda e: e.tensor_copy(rows_b[0:1, 0:16], rows[0:1, 0:16]))
        t_mh_done = None
        for h in range(NH):
            t_b = p.op("tensor", (lambda h: lambda e: e.matmul(
                psf[2][0:128, 0:1], onesr1[:, :], rows_b[0:1, h:h + 1],
                start=True, stop=True))(h), waits=[t7, t_ones, bank_free[2]], sem="pe")
            t_mh_done = p.op("vector", (lambda h: lambda e: e.tensor_copy(
                mh_col[:, h:h + 1], psf[2][0:128, 0:1]))(h), waits=[t_b])
            bank_free[2] = t_mh_done

        while dq_cur[11] < NT_DQ:
            dq_advance(3)
        while dq_cursor[1] < 64:
            emit_wcast(dq_cursor[1], dq_cursor[1] + 1)
            dq_cursor[1] += 1
        a_done = [(sn, p.counts[sn]) for sn in
                  ("dsck0", "dsck1", "dscv0", "dscv1", "dbws")
                  + tuple(f"ddqo{i}" for i in range(NBP))]
        a_done.append(t_mh_done)
        a_done.append(p.pe_now())
        dq_bank_done = [dq_state["bank"][0], dq_state["bank"][1]]
        esA.close()

        stream = [sbg(f"st{i}", [128, OWN], F32) for i in range(DC)]

        esB = contextlib.ExitStack()
        sbb = mksb(esB)
        NS = 3
        NPT = 6
        LA = 3
        GH = 2
        NG = NH // GH
        sbanks = [psf[0], psf[1], psf[2]]
        ksb = [sbb(f"ksb{i}", [128, S], BF16) for i in range(2)]
        vsb = [sbb(f"vsb{i}", [128, S], BF16) for i in range(2)]
        qsb = [sbb(f"qsb{i}", [128, OWN], BF16) for i in range(2)]
        msb = sbb("msb", [128, MASK_W], BF16)
        ptb = [sbb(f"ptb{i}", [128, 512], BF16) for i in range(NPT)]
        rec_bc = sbb("rec_bc", [128, 512], BF16)
        recrow = sbb("recrow", [1, 512], F32)
        recrowb = sbb("recrowb", [1, 512], BF16)
        attn_g = [sbb(f"attn{i}", [128, OWN], BF16) for i in range(GH)]
        wo_stg = sbb("wo_stg", [128, 1024], F32)
        wo_bf = [sbb(f"wo_bf{i}", [128, 2048], BF16) for i in range(GH)]
        xch = [sbb(f"xch{i}", [128, 128], F32) for i in range(2)]
        xfm = sbb("xfm", [128, 512], F32)

        t_mask = p.dma("sync", msb[:, :], mask_t.ap(), waits=a_done, sem="dbm")

        kvq_free = [None, None]
        pt_free = [None] * NPT
        attn_free = []
        qidx = [0]
        pending_tail = [None]
        last_at = [None]
        sumbanks = [psf[3], psbk[0][:, :].bitcast(F32)]
        avbanks = [psf[4], psbk[1][:, :].bitcast(F32)]
        wo_stg_free = None
        wo_free = [None] * GH
        xch_free = [None, None]
        stream_tok = [[None] * NTT for _ in range(DC)]
        sbank_free = [bank_free[0], bank_free[1], bank_free[2]]
        sum_free = [dq_bank_done[0], psb_free[0]]
        av_free = [dq_bank_done[1], psb_free[1]]
        y_free = None

        for g in range(NG):
            attn_done = []
            for hh in range(GH):
                h = GH * g + hh
                par = h % 2
                t_k = p.dma("sync", ksb[par][:, :], Ksc.ap()[h * 128:(h + 1) * 128, :],
                            waits=a_done + [kvq_free[par]], sem=f"dbk{par}")
                t_q = p.dma("sync", qsb[par][:, :], Qsc.ap()[h * 128:(h + 1) * 128, :],
                            waits=a_done + [kvq_free[par]], sem=f"dbq{par}")
                t_v = p.dma("sync", vsb[par][:, :],
                            Vsc.ap()[h * 128:(h + 1) * 128, :],
                            waits=a_done + [kvq_free[par]], sem=f"dbv{par}")
                t_at = None
                for qb in range(NTT):
                    nkc = 20 + 4 * qb
                    m_toks = [None] * nkc
                    qi = qidx[0]
                    qidx[0] += 1
                    sb_ = qi % 2
                    sumbank = sumbanks[sb_]
                    avbank = avbanks[sb_]
                    t_sum_last = None
                    t_av_last = None

                    def emit_s(kc, qb=qb, par=par, h=h, t_k=t_k, t_q=t_q, m_toks=m_toks):
                        bs = kc % NS
                        t_s = p.op("tensor", (lambda bs, kc, qb, par: lambda e: e.matmul(
                            sbanks[bs][:, :], ksb[par][:, kc * 128:(kc + 1) * 128],
                            qsb[par][:, qb * 512:(qb + 1) * 512],
                            start=True, stop=True))(bs, kc, qb, par),
                            waits=[t_k, t_q, sbank_free[bs]], sem="pe")
                        t_e = p.op("scalar", (lambda bs, kc, h: lambda e: e.activation(
                            ptb[kc % NPT][:, :], sbanks[bs][:, :], AF.Exp,
                            bias=mh_col[:, h:h + 1], scale=SCALE))(bs, kc, h),
                            waits=[t_s, t_mh_done, pt_free[kc % NPT]])
                        sbank_free[bs] = t_e
                        tv = 128 * kc - 512 * qb
                        t_m = t_e
                        if tv >= 0:
                            eng = "gpsimd" if (kc % 2 == 0) else "vector"
                            t_m = p.op(eng, (lambda kc, tv: lambda e: e.tensor_tensor(
                                ptb[kc % NPT][:, :], ptb[kc % NPT][:, :],
                                msb[:, MASK_Y0 - tv: MASK_Y0 - tv + 512],
                                Alu.mult))(kc, tv),
                                waits=[t_e, t_mask])
                        m_toks[kc] = t_m

                    def emit_sumav(kc, nkc=nkc, par=par, m_toks=m_toks,
                                   sumbank=sumbank, avbank=avbank, sb_=sb_):
                        t_sum = p.op("tensor", (lambda kc, nkc: lambda e: e.matmul(
                            sumbank[0:1, :], onescol[:, :], ptb[kc % NPT][:, :],
                            start=(kc == 0), stop=(kc == nkc - 1)))(kc, nkc),
                            waits=[m_toks[kc], t_onec] + ([sum_free[sb_]] if kc == 0 else []),
                            sem="pe")
                        t_av = p.op("tensor", (lambda kc, nkc, par: lambda e: e.matmul(
                            avbank[:, :], vsb[par][:, kc * 128:(kc + 1) * 128],
                            ptb[kc % NPT][:, :],
                            start=(kc == 0), stop=(kc == nkc - 1)))(kc, nkc, par),
                            waits=[t_v] + ([av_free[sb_]] if kc == 0 else []),
                            sem="pe")
                        pt_free[kc % NPT] = t_av
                        return t_sum, t_av

                    for kc in range(nkc):
                        emit_s(kc)
                        if kc == 1 and pending_tail[0] is not None:
                            pending_tail[0]()
                            pending_tail[0] = None
                        if kc >= LA:
                            t_sum_last, t_av_last = emit_sumav(kc - LA)
                    for kc in range(nkc - LA, nkc):
                        t_sum_last, t_av_last = emit_sumav(kc)

                    def tail(h=h, hh=hh, qb=qb, t_sum_last=t_sum_last,
                             t_av_last=t_av_last, sumbank=sumbank, avbank=avbank, sb_=sb_):
                        nonlocal y_free
                        t_rc = p.op("vector", (lambda sumbank: lambda e: e.reciprocal(
                            recrow[0:1, :], sumbank[0:1, :]))(sumbank),
                            waits=[t_sum_last])
                        sum_free[sb_] = t_rc
                        t_rcb = p.op("vector", lambda e: e.tensor_copy(recrowb[0:1, :], recrow[0:1, :]),
                                     waits=[t_rc])
                        t_bc = p.op("tensor", lambda e: e.matmul(
                            psf[5][:, :], onesr1[:, :], recrowb[0:1, :], start=True, stop=True),
                            waits=[t_rcb, y_free, t_ones], sem="pe")
                        t_bce = p.op("vector", lambda e: e.tensor_copy(rec_bc[:, :], psf[5][:, :]),
                                     waits=[t_bc])
                        y_free = t_bce
                        t_at2 = p.op("vector", (lambda hh, qb, avbank: lambda e: e.tensor_tensor(
                            attn_g[hh][:, qb * 512:(qb + 1) * 512], avbank[:, :], rec_bc[:, :],
                            Alu.mult))(hh, qb, avbank),
                            waits=[t_av_last, t_bce] + attn_free)
                        av_free[sb_] = t_at2
                        attn_done.append(t_at2)
                        last_at[0] = t_at2
                    pending_tail[0] = tail
                if pending_tail[0] is not None:
                    pending_tail[0]()
                    pending_tail[0] = None
                t_at = last_at[0]
                kvq_free[par] = t_at
            for hh in range(GH):
                h = GH * g + hh
                tc0 = p.dma("scalar", wo_stg[:, :], woa_t.ap()[h * 128:(h + 1) * 128, 0:1024],
                            waits=a_done + [wo_stg_free], sem="dbwo")
                th0 = p.op("vector", (lambda hh: lambda e: e.tensor_copy(
                    wo_bf[hh][:, 0:1024], wo_stg[:, :]))(hh),
                    waits=[tc0, None if g == 0 else p.pe_now()])
                tc1 = p.dma("scalar", wo_stg[:, :], woa_t.ap()[h * 128:(h + 1) * 128, 1024:2048],
                            waits=[th0], sem="dbwo")
                wo_free[hh] = p.op("vector", (lambda hh: lambda e: e.tensor_copy(
                    wo_bf[hh][:, 1024:2048], wo_stg[:, :]))(hh), waits=[tc1])
                wo_stg_free = wo_free[hh]
            for tt in range(NTT):
                for dc in range(DC):
                    if g == 0:
                        tp_toks = []
                        for j in range(4):
                            xp = j % 2
                            t_xl = p.dma("scalar", xch[xp][:, :],
                                         x_own_t.ap()[tt * 512 + j * 128: tt * 512 + (j + 1) * 128,
                                                      dc * 128:(dc + 1) * 128],
                                         waits=a_done + [xch_free[xp]], sem=f"dbx{xp}")
                            t_tp = p.op("tensor", (lambda xp, j: lambda e: e.transpose(
                                psf[5][:, j * 128:(j + 1) * 128], xch[xp][:, :],
                                identf[:, :]))(xp, j),
                                waits=[t_xl, t_id, y_free], sem="pe")
                            xch_free[xp] = t_tp
                            tp_toks.append(t_tp)
                        t_xf = p.op("vector", (lambda: lambda e: e.tensor_copy(xfm[:, :], psf[5][:, 0:512]))(),
                                    waits=tp_toks)
                        y_free = t_xf

                    def mmy(e, tt=tt, dc=dc):
                        last = None
                        for hh2 in range(GH):
                            last = e.matmul(psf[5][:, :], wo_bf[hh2][:, dc * 128:(dc + 1) * 128],
                                            attn_g[hh2][:, tt * 512:(tt + 1) * 512],
                                            start=(hh2 == 0), stop=(hh2 == GH - 1))
                        return last
                    t_y = p.op("tensor", mmy,
                               waits=[y_free] + attn_done + list(wo_free),
                               sem="pe")
                    if g == 0:
                        t_se = p.op("vector", (lambda dc, tt: lambda e: e.scalar_tensor_tensor(
                            stream[dc][:, tt * 512:(tt + 1) * 512], psf[5][:, :],
                            bo_col[:, dc:dc + 1], xfm[:, :], Alu.add, Alu.add))(dc, tt),
                            waits=[t_y, t_bcol])
                    else:
                        t_se = p.op("vector", (lambda dc, tt: lambda e: e.tensor_tensor(
                            stream[dc][:, tt * 512:(tt + 1) * 512],
                            stream[dc][:, tt * 512:(tt + 1) * 512], psf[5][:, :],
                            Alu.add))(dc, tt),
                            waits=[t_y, stream_tok[dc][tt]])
                    y_free = t_se
                    stream_tok[dc][tt] = t_se
            attn_free = [p.pe_now()]
        esB.close()

        def make_lnfm(xnb, sqt, rows3, rbrow, eps):
            def lnfm(tt, stream_tok, bank_gate, xnb_gate=()):
                t_xb = []
                for dc in range(DC):
                    t_xb.append(p.op("vector", (lambda dc: lambda e: e.tensor_copy(
                        xnb[dc][:, :], stream[dc][:, tt * 512:(tt + 1) * 512]))(dc),
                        waits=[stream_tok[dc][tt]] + list(xnb_gate)))

                def mmsum(e):
                    last = None
                    for dc in range(DC):
                        last = e.matmul(psf[0][0:1, :], onescol[:, :], xnb[dc][:, :],
                                        start=(dc == 0), stop=(dc == DC - 1))
                    return last
                t_sx = p.op("tensor", mmsum, waits=t_xb + [t_onec] + bank_gate, sem="pe")
                sq_toks = []
                mm2 = None
                nsq = len(sqt)
                for dc in range(DC):
                    sp = dc % nsq
                    t_sq = p.op("scalar", (lambda dc, sp: lambda e: e.activation(
                        sqt[sp][:, :], xnb[dc][:, :], AF.Square))(dc, sp),
                        waits=[t_xb[dc]] + ([sq_toks[dc - nsq]] if dc >= nsq else []))
                    mm2 = p.op("tensor", (lambda dc, sp: lambda e: e.matmul(
                        psf[1][0:1, :], onescol[:, :], sqt[sp][:, :],
                        start=(dc == 0), stop=(dc == DC - 1)))(dc, sp),
                        waits=[t_sq] + (bank_gate if dc == 0 else []), sem="pe")
                    sq_toks.append(mm2)
                t_m = p.op("vector", lambda e: e.tensor_scalar(
                    rows3[0:1, 0:512], psf[0][0:1, :], 1.0 / D, None, Alu.mult),
                    waits=[t_sx])
                t_q = p.op("vector", lambda e: e.tensor_scalar(
                    rows3[0:1, 512:1024], psf[1][0:1, :], 1.0 / D, None, Alu.mult),
                    waits=[mm2])
                t_t = p.op("vector", lambda e: e.tensor_tensor(
                    psf[0][0:1, :], rows3[0:1, 0:512], rows3[0:1, 0:512], Alu.mult),
                    waits=[t_m])
                t_v = p.op("vector", lambda e: e.tensor_tensor(
                    rows3[0:1, 512:1024], rows3[0:1, 512:1024], psf[0][0:1, :],
                    Alu.subtract), waits=[t_q, t_t])
                t_l = p.op("scalar", (lambda: lambda e: e.activation(
                    rows3[0:1, 512:1024], rows3[0:1, 512:1024], AF.Ln, bias=eps))(),
                    waits=[t_v, p.const_tok])
                t_e2 = p.op("scalar", lambda e: e.activation(
                    rows3[0:1, 512:1024], rows3[0:1, 512:1024], AF.Exp, scale=-0.5),
                    waits=[t_l])
                t_rb1 = p.op("vector", lambda e: e.tensor_copy(rbrow[0:1, :], rows3[0:1, 512:1024]),
                             waits=[t_e2])
                t_b1 = p.op("tensor", lambda e: e.matmul(
                    psf[1][:, :], onesr1[:, :], rbrow[0:1, :], start=True, stop=True),
                    waits=[t_rb1, t_ones, t_q], sem="pe")
                tmul = []
                for dc in range(DC):
                    tmul.append(p.op("vector", (lambda dc: lambda e: e.tensor_tensor(
                        xnb[dc][:, :], xnb[dc][:, :], psf[1][:, :], Alu.mult))(dc),
                        waits=[t_b1]))
                t_mr = p.op("vector", lambda e: e.tensor_tensor(
                    rows3[0:1, 0:512], rows3[0:1, 0:512], rows3[0:1, 512:1024], Alu.mult),
                    waits=[t_e2])
                t_ng = p.op("vector", lambda e: e.tensor_scalar(
                    rows3[0:1, 0:512], rows3[0:1, 0:512], -1.0, None, Alu.mult),
                    waits=[t_mr])
                t_rb2 = p.op("vector", lambda e: e.tensor_copy(rbrow[0:1, :], rows3[0:1, 0:512]),
                             waits=[t_ng, t_b1])
                t_b2 = p.op("tensor", lambda e: e.matmul(
                    psf[1][:, :], onesr1[:, :], rbrow[0:1, :], start=True, stop=True),
                    waits=[t_rb2] + tmul, sem="pe")
                t_app = []
                for dc in range(DC):
                    tb = p.op("vector", (lambda dc: lambda e: e.tensor_tensor(
                        xnb[dc][:, :], xnb[dc][:, :], psf[1][:, :], Alu.add))(dc),
                        waits=[t_b2])
                    t_app.append(tb)
                return t_app, t_app[-1]
            return lnfm

        def emit_ffn(call):
            esD = contextlib.ExitStack()
            sbd = mksb(esD)
            xnb = [sbd(f"fx{call}_{i}", [128, 512], BF16) for i in range(DC)]
            hdn = [sbd(f"fh{call}_{i}", [128, 512], BF16) for i in range(44)]
            sqt = [sbd(f"fsq{call}_{i}", [128, 512], BF16) for i in range(1)]
            wgu3 = [sbd(f"fwg{call}_{i}", [128, 2048], BF16) for i in range(3)]
            rows3 = sbd(f"fr3{call}", [1, 1024], F32)
            rbrow = sbd(f"frb{call}", [1, 512], BF16)
            lnfm = make_lnfm(xnb, sqt, rows3, rbrow, EPS_LN)
            pg = p.phase_gate()
            bank_gate = pg
            wslot_free = [None, None, None]
            gidx = [0]
            hdn_free = [pg[0]] * 44
            g_free = None
            u_free = None
            y_free2 = [None] * 6
            first = True
            hd_prev = [None, None]
            for tt in range(NTT):
                t_app, _ = lnfm(tt, stream_tok, bank_gate,
                                xnb_gate=(pg if first else [p.pe_now()]))
                bank_gate = [None]
                for ih in range(2):
                    for ic in range(22):
                        gi = ih * 22 + ic
                        i0 = ih * 2816 + ic * 128
                        fgate = pg if first else []
                        sg_ = (2 * gidx[0]) % 3
                        su_ = (2 * gidx[0] + 1) % 3
                        gidx[0] += 1
                        wg = wgu3[sg_]
                        wu = wgu3[su_]
                        tg = [p.dma("gpsimd", wg[:, :],
                                    WGU.ap()[(i0 // 128) * 128:(i0 // 128) * 128 + 128, :],
                                    waits=[wslot_free[sg_]] + fgate, sem=f"dfg{call}_{sg_}")]
                        tu = [p.dma("scalar", wu[:, :],
                                    WGU.ap()[((II + i0) // 128) * 128:((II + i0) // 128) * 128 + 128, :],
                                    waits=[wslot_free[su_]] + fgate, sem=f"dfu{call}_{su_}")]
                        first = False

                        def mmg(e, wg=wg):
                            last = None
                            for dc in range(DC):
                                last = e.matmul(psf[0][:, :], wg[:, dc * 128:(dc + 1) * 128],
                                                xnb[dc][:, :], start=(dc == 0),
                                                stop=(dc == DC - 1))
                            return last

                        def mmu(e, wu=wu):
                            last = None
                            for dc in range(DC):
                                last = e.matmul(psf[1][:, :], wu[:, dc * 128:(dc + 1) * 128],
                                                xnb[dc][:, :], start=(dc == 0),
                                                stop=(dc == DC - 1))
                            return last
                        t_g = p.op("tensor", mmg, waits=tg + t_app + [g_free], sem="pe")
                        t_uu = p.op("tensor", mmu, waits=tu + t_app + [u_free], sem="pe")
                        wslot_free[sg_] = t_g
                        wslot_free[su_] = t_uu
                        sp = 0
                        t_sl = p.op("scalar", (lambda sp: lambda e: e.activation(
                            sqt[sp][:, :], psf[0][:, :], AF.Silu))(sp),
                            waits=[t_g, hd_prev[sp]])
                        t_hd = p.op("vector", (lambda gi, sp: lambda e: e.tensor_tensor(
                            hdn[gi][:, :], sqt[sp][:, :], psf[1][:, :], Alu.mult))(gi, sp),
                            waits=[t_sl, t_uu, hdn_free[gi]])
                        hd_prev[sp] = t_hd
                        g_free = t_sl
                        u_free = t_hd
                groups = [list(range(0, 6)), list(range(6, 12)), list(range(12, 16))]
                ev_toks = []
                for grp in groups:
                    for k in range(44):
                        wpar = k % 3
                        wdnb = wgu3[wpar]
                        t_w = p.dma("sync", wdnb[:, 0:len(grp) * 128],
                                    bass.AP(WDN, k * 128 * 2048 + grp[0] * 128,
                                            [[2048, 128], [1, len(grp) * 128]]),
                                    waits=[wslot_free[wpar]], sem=f"dfd{call}_{wpar}")

                        def mmd(e, grp=grp, k=k, wdnb=wdnb):
                            last = None
                            for j in range(len(grp)):
                                last = e.matmul(psf[j][:, :],
                                                wdnb[:, j * 128:(j + 1) * 128],
                                                hdn[k][:, :], start=(k == 0),
                                                stop=(k == 43))
                            return last
                        t_d = p.op("tensor", mmd,
                                   waits=[t_w, u_free] + ([y_free2[j] for j in range(len(grp))]
                                                          if k == 0 else []), sem="pe")
                        wslot_free[wpar] = t_d
                    for j, hc in enumerate(grp):
                        t_ev = p.op("vector", (lambda j, hc: lambda e: e.tensor_copy(
                            xnb[hc][:, :], psf[j][:, :]))(j, hc),
                            waits=[t_d])
                        y_free2[j] = t_ev
                        ev_toks.append(t_ev)
                pe_after_down = p.pe_now()
                for gi in range(44):
                    hdn_free[gi] = pe_after_down
                mm2 = None
                rms_mm = [None]
                for hc in range(DC):
                    sp = 0
                    t_sq = p.op("scalar", (lambda hc, sp: lambda e: e.activation(
                        sqt[sp][:, :], xnb[hc][:, :], AF.Square))(hc, sp),
                        waits=ev_toks + [rms_mm[sp]])
                    mm2 = p.op("tensor", (lambda hc, sp: lambda e: e.matmul(
                        psf[0][0:1, :], onescol[:, :], sqt[sp][:, :],
                        start=(hc == 0), stop=(hc == DC - 1)))(hc, sp),
                        waits=[t_sq] + ([y_free2[0]] if hc == 0 else []), sem="pe")
                    rms_mm[sp] = mm2
                t_q2 = p.op("vector", lambda e: e.tensor_scalar(
                    rows3[0:1, 0:512], psf[0][0:1, :], 1.0 / D, None, Alu.mult),
                    waits=[mm2])
                t_l2 = p.op("scalar", (lambda: lambda e: e.activation(
                    rows3[0:1, 0:512], rows3[0:1, 0:512], AF.Ln, bias=EPS_RMS))(),
                    waits=[t_q2])
                t_e3 = p.op("scalar", lambda e: e.activation(
                    rows3[0:1, 0:512], rows3[0:1, 0:512], AF.Exp, scale=-0.5),
                    waits=[t_l2])
                t_rb = p.op("vector", lambda e: e.tensor_copy(rbrow[0:1, :], rows3[0:1, 0:512]),
                            waits=[t_e3])
                t_b3 = p.op("tensor", lambda e: e.matmul(
                    psf[1][:, :], onesr1[:, :], rbrow[0:1, :], start=True, stop=True),
                    waits=[t_rb, u_free], sem="pe")
                t_su = None
                for hc in range(DC):
                    t_t2 = p.op("vector", (lambda hc: lambda e: e.tensor_tensor(
                        sqt[0][:, :], xnb[hc][:, :], psf[1][:, :], Alu.mult))(hc),
                        waits=[t_b3, mm2])
                    t_su = p.op("vector", (lambda hc, tt: lambda e: e.tensor_tensor(
                        stream[hc][:, tt * 512:(tt + 1) * 512],
                        stream[hc][:, tt * 512:(tt + 1) * 512], sqt[0][:, :],
                        Alu.add))(hc, tt),
                        waits=[t_t2, stream_tok[hc][tt]])
                    stream_tok[hc][tt] = t_su
                bank_gate = [p.pe_now(), t_su]
            esD.close()

        def emit_attnb():
            esE = contextlib.ExitStack()
            sbe = mksb(esE)
            xnb = [sbe(f"ex{i}", [128, 512], BF16) for i in range(DC)]
            sqt = [sbe(f"esq{i}", [128, 512], BF16) for i in range(2)]
            rows3 = sbe("er3", [1, 1024], F32)
            rbrow = sbe("erb", [1, 512], BF16)
            wk = sbe("ewk", [128, 2048], BF16)
            wv = sbe("ewv", [128, 2048], BF16)
            kb = sbe("ekb", [128, 512], BF16)
            kvf = sbe("ekv", [128, 512], F32)
            kvtf = sbe("ekt", [128, 512], F32)
            kvtb = sbe("ektb", [128, 512], BF16)
            onesf = sbe("eon", [128, 512], F32)
            carry = sbe("ecar", [128, 16], F32)
            ccol = sbe("eccl", [128, 16], F32)
            agi_sb = sbe("eagi", [16, 128], F32)
            ago_sb = sbe("eago", [8, D], F32)
            w8_sb = sbe("ew8", [8, 1], F32)
            vhat = [sbe(f"evh{i}", [128, 512], BF16) for i in range(DC)]
            wot = [sbe(f"ewo{i}", [128, 768], BF16) for i in range(2)]
            lnfm = make_lnfm(xnb, sqt, rows3, rbrow, EPS_LN)
            pg = p.phase_gate()
            t_on = p.op("vector", lambda e: e.memset(onesf[:, :], 1.0))
            t_zc = p.op("vector", lambda e: e.memset(carry[:, :], 0.0))
            t_w8 = p.dma("sync", w8_sb[:, :], w8_t.ap(), waits=pg, sem="dw8")
            bank_gate = pg
            wk_free = None
            wv_free = None
            kb_free = [None]
            kv_free = [None]
            kvtb_r = [None]
            xn_gate = pg
            for tt in range(NTT):
                t_app, _ = lnfm(tt, stream_tok, bank_gate, xnb_gate=xn_gate)
                bank_gate = [None]
                for dc in range(DC):
                    p.dma("sync", XN3.ap()[dc * 128:(dc + 1) * 128,
                                           tt * 512:(tt + 1) * 512],
                          xnb[dc][:, :], waits=[t_app[dc]], sem="dxn")
                for ec in range(DC):
                    egate = pg if (tt == 0 and ec == 0) else []
                    tk = [p.dma("gpsimd", wk[:, :], BWk.ap()[ec * 128:(ec + 1) * 128, :],
                                waits=[wk_free] + egate, sem="dek")]
                    tv = [p.dma("scalar", wv[:, :], BWv.ap()[ec * 128:(ec + 1) * 128, :],
                                waits=[wv_free] + egate, sem="dev")]

                    def mmk(e):
                        last = None
                        for dc in range(DC):
                            last = e.matmul(psf[0][:, :], wk[:, dc * 128:(dc + 1) * 128],
                                            xnb[dc][:, :], start=(dc == 0),
                                            stop=(dc == DC - 1))
                        return last

                    def mmv(e):
                        last = None
                        for dc in range(DC):
                            last = e.matmul(psf[1][:, :], wv[:, dc * 128:(dc + 1) * 128],
                                            xnb[dc][:, :], start=(dc == 0),
                                            stop=(dc == DC - 1))
                        return last
                    t_k = p.op("tensor", mmk, waits=tk + t_app + [kb_free[0]], sem="pe")
                    t_vv = p.op("tensor", mmv, waits=tv + t_app + [kv_free[0]], sem="pe")
                    t_kb = p.op("vector", lambda e: e.tensor_copy(kb[:, :], psf[0][:, :]),
                                waits=[t_k])
                    kb_free[0] = t_kb
                    t_kv = p.op("vector", lambda e: e.tensor_tensor(
                        kvf[:, :], psf[1][:, :], kb[:, :], Alu.mult), waits=[t_kb, t_vv])
                    kv_free[0] = t_kv
                    t_sc = p.op("vector", (lambda ec: lambda e: e.tensor_tensor_scan(
                        kvtf[:, :], onesf[:, :], kvf[:, :], carry[:, ec:ec + 1],
                        Alu.mult, Alu.add))(ec), waits=[t_kv, t_on, t_zc])
                    t_cu = p.op("vector", (lambda ec: lambda e: e.tensor_copy(
                        carry[:, ec:ec + 1], kvtf[:, 511:512]))(ec), waits=[t_sc])
                    t_cb = p.op("vector", (lambda: lambda e: e.tensor_copy(
                        kvtb[:, :], kvtf[:, :]))(), waits=[t_sc, kvtb_r[0]])
                    kvtb_r[0] = p.dma("sync", KVT.ap()[ec * 128:(ec + 1) * 128,
                                                       tt * 512:(tt + 1) * 512],
                                      kvtb[:, :], waits=[t_cb], sem="dkt")
                    wk_free = t_k
                    wv_free = t_vv
                bank_gate = [p.pe_now()]
                xn_gate = [p.pe_now(), ("dxn", p.counts["dxn"])]
            t_tpc = p.op("tensor", lambda e: e.transpose(
                psf[2][0:16, 0:128], carry[:, :], identf[:, :]),
                waits=[("_ser_vector", p.counts["_ser_vector"]), t_id], sem="pe")
            t_agi = p.op("vector", lambda e: e.tensor_copy(agi_sb[:, :], psf[2][0:16, 0:128]),
                         waits=[t_tpc])
            t_agid = p.dma("gpsimd", bass.AP(AGI, 0, [[128, 16], [1, 128]]), agi_sb[:, :],
                           waits=[t_agi], sem="dagi")
            t_cc = p.op("gpsimd", lambda e: e.collective_compute(
                "AllGather", Alu.bypass, replica_groups=[list(range(8))],
                ins=[AGI.ap().opt()], outs=[AGO.ap().opt()]),
                waits=[t_agid], sem="cc", by=1)
            t_ago = p.dma("gpsimd", ago_sb[:, :], AGO.ap(), waits=[("cc", 1)], sem="dago")
            ccol_toks = []
            for ec in range(DC):
                t_sel = p.op("tensor", (lambda ec: lambda e: e.matmul(
                    psf[2][0:128, 0:1], ago_sb[0:8, ec * 128:(ec + 1) * 128],
                    w8_sb[:, :], start=True, stop=True))(ec),
                    waits=[t_ago, t_w8, t_agi if ec == 0 else None,
                           ccol_toks[-1] if ec else None], sem="pe")
                t_ce = p.op("vector", (lambda ec: lambda e: e.tensor_copy(
                    ccol[:, ec:ec + 1], psf[2][0:128, 0:1]))(ec), waits=[t_sel])
                ccol_toks.append(t_ce)
            t_ccol = ccol_toks[-1]
            wq_free = p.pe_now()
            q_free[0] = p.pe_now()
            kvtb_free = [kvtb_r[0]]
            vh_free = [None] * DC
            wot_free = [None, None]
            xnb_free2 = [p.pe_now()] * DC
            for tt in range(NTT):
                xl = []
                for dc in range(DC):
                    xl.append(p.dma("sync", xnb[dc][:, :],
                                    XN3.ap()[dc * 128:(dc + 1) * 128,
                                             tt * 512:(tt + 1) * 512],
                                    waits=[("dxn", p.counts["dxn"]), xnb_free2[dc]],
                                    sem="dxl"))
                for ec in range(DC):
                    tq = [p.dma("gpsimd", wk[:, :], BWq.ap()[ec * 128:(ec + 1) * 128, :],
                                waits=[wq_free], sem="deq")]

                    def mmq(e):
                        last = None
                        for dc in range(DC):
                            last = e.matmul(psf[0][:, :], wk[:, dc * 128:(dc + 1) * 128],
                                            xnb[dc][:, :], start=(dc == 0),
                                            stop=(dc == DC - 1))
                        return last
                    t_qq = p.op("tensor", mmq, waits=tq + xl + [q_free[0]], sem="pe")
                    wq_free = t_qq
                    t_kl = p.dma("sync", kvtb[:, :],
                                 KVT.ap()[ec * 128:(ec + 1) * 128,
                                          tt * 512:(tt + 1) * 512],
                                 waits=[("dkt", p.counts["dkt"])] + list(kvtb_free), sem="dkl")
                    t_kc = p.op("vector", (lambda ec: lambda e: e.tensor_scalar(
                        kvf[:, :], kvtb[:, :], ccol[:, ec:ec + 1], None, Alu.add))(ec),
                        waits=[t_kl, t_ccol])
                    kvtb_free = [t_kc]
                    t_vh = p.op("vector", (lambda ec: lambda e: e.scalar_tensor_tensor(
                        vhat[ec][:, :], kvf[:, :], DENOM_B, psf[0][:, :],
                        Alu.mult, Alu.mult))(ec),
                        waits=[t_kc, t_qq, vh_free[ec]])
                    q_free[0] = t_vh
                groups = [list(range(0, 6)), list(range(6, 12)), list(range(12, 16))]
                for grp in groups:
                    t_d = None
                    for ec in range(DC):
                        wpar = ec % 2
                        t_w = p.dma("scalar", wot[wpar][:, 0:len(grp) * 128],
                                    bass.AP(BWot, ec * 128 * 2048 + grp[0] * 128,
                                            [[2048, 128], [1, len(grp) * 128]]),
                                    waits=[wot_free[wpar]] + (pg if wot_free[wpar] is None else []),
                                    sem=f"dew{wpar}")

                        def mmo(e, grp=grp, ec=ec, wpar=wpar):
                            last = None
                            for j in range(len(grp)):
                                last = e.matmul(psf[j][:, :],
                                                wot[wpar][:, j * 128:(j + 1) * 128],
                                                vhat[ec][:, :], start=(ec == 0),
                                                stop=(ec == DC - 1))
                            return last
                        t_d = p.op("tensor", mmo,
                                   waits=[t_w,
                                          p.last_tok["vector"] if ec == 0 else None]
                                   + ([yb_free[j] for j in range(len(grp))] if ec == 0 else []),
                                   sem="pe")
                        wot_free[wpar] = t_d
                    for j, fc in enumerate(grp):
                        t_su = p.op("vector", (lambda j, fc, tt: lambda e: e.tensor_tensor(
                            stream[fc][:, tt * 512:(tt + 1) * 512],
                            stream[fc][:, tt * 512:(tt + 1) * 512], psf[j][:, :],
                            Alu.add))(j, fc, tt),
                            waits=[t_d, stream_tok[fc][tt]])
                        yb_free[j] = t_su
                        stream_tok[fc][tt] = t_su
                pe_now2 = p.pe_now()
                for ec in range(DC):
                    vh_free[ec] = pe_now2
                for dc in range(DC):
                    xnb_free2[dc] = pe_now2
                q_free[0] = yb_free[0]
            esE.close()

        q_free = [None]
        yb_free = [None] * 6
        if STAGE >= 2:
            emit_ffn(0)
        if STAGE >= 3:
            emit_attnb()
        if STAGE >= 4:
            emit_ffn(1)

        esG = contextlib.ExitStack()
        sbgg = mksb(esG)
        otile = [sbgg(f"otile{i}", [128, D], F32) for i in range(2)]
        ot_free = [None, None]
        gtp_free = p.last_tok["vector"]
        for tb in range(16):
            par = tb % 2
            ev_last = None
            for dc in range(DC):
                t_tp = p.op("tensor", (lambda dc, tb: lambda e: e.transpose(
                    psf[5][:, 0:128], stream[dc][:, tb * 128:(tb + 1) * 128],
                    identf[:, :]))(dc, tb),
                    waits=[stream_tok[dc][tb // 4], gtp_free, ot_free[par]], sem="pe")
                ev_last = p.op("vector", (lambda dc, par: lambda e: e.tensor_copy(
                    otile[par][:, dc * 128:(dc + 1) * 128], psf[5][:, 0:128]))(dc, par),
                    waits=[t_tp])
                gtp_free = ev_last
            t_o = p.dma("sync", out_t.ap()[tb * 128:(tb + 1) * 128, :], otile[par][:, :],
                        waits=[ev_last], sem=f"dout{par}")
            ot_free[par] = t_o
        p.wait("sync", ("dout0", p.counts["dout0"]), ("dout1", p.counts["dout1"]))
        esG.close()

        with nc.Block() as block:
            p.finalize(block)
    return nc, p.const_host()



def make_mask(off):
    pp = np.arange(128)[:, None]
    yy = np.arange(MASK_W)[None, :]
    return (yy >= pp + MASK_Y0 - off).astype(BF16NP)


_CACHE = {}


def kernel(L0_attn_a_x, L0_attn_a_Wqkv, L0_attn_a_bqkv, L0_attn_a_Wo, L0_attn_a_bo,
           L0_attn_b_Wq, L0_attn_b_Wk, L0_attn_b_Wv, L0_attn_b_Wo,
           L0_ffn_gate_up_packed, L0_ffn_gate_up_scales,
           L0_ffn_down_packed, L0_ffn_down_scales):
    if "nc" not in _CACHE:
        _CACHE["nc"], _CACHE["consts"] = build_nc()
    nc = _CACHE["nc"]
    consts = _CACHE["consts"]

    x = np.ascontiguousarray(np.asarray(L0_attn_a_x, dtype=np.float32))
    wqkv = np.ascontiguousarray(np.asarray(L0_attn_a_Wqkv, dtype=np.float32))
    wqkvT = np.ascontiguousarray(
        wqkv.reshape(16, 128, 48, 128).transpose(2, 1, 0, 3).reshape(48 * 128, 2048))
    bqkv = np.ascontiguousarray(np.asarray(L0_attn_a_bqkv, dtype=np.float32))
    woa = np.ascontiguousarray(np.asarray(L0_attn_a_Wo, dtype=np.float32))
    bo = np.ascontiguousarray(np.asarray(L0_attn_a_bo, dtype=np.float32))
    ident = np.eye(128, dtype=np.float32)
    gupT = np.ascontiguousarray(np.asarray(L0_ffn_gate_up_packed, dtype=np.int32).T)
    gusc = np.ascontiguousarray(np.asarray(L0_ffn_gate_up_scales, dtype=np.float16))
    dnpT = np.ascontiguousarray(np.asarray(L0_ffn_down_packed, dtype=np.int32).T)
    dnsc = np.ascontiguousarray(np.asarray(L0_ffn_down_scales, dtype=np.float16))
    _sh = 4 * (np.arange(128) % 8)
    shifts = (np.uint32(15) << _sh.astype(np.uint32)).astype(np.uint32).view(np.int32).reshape(128, 1).copy()
    invs = (2.0 ** (-_sh.astype(np.float64))).astype(np.float32).reshape(128, 1)
    wqt = np.ascontiguousarray(np.asarray(L0_attn_b_Wq, dtype=np.float32).T)
    wkt = np.ascontiguousarray(np.asarray(L0_attn_b_Wk, dtype=np.float32).T)
    wvt = np.ascontiguousarray(np.asarray(L0_attn_b_Wv, dtype=np.float32).T)
    wot = np.ascontiguousarray(np.asarray(L0_attn_b_Wo, dtype=np.float32).T)

    in_maps = []
    for c in range(8):
        b, half = c // 2, c % 2
        off = half * OWN
        w8 = np.zeros((8, 1), dtype=np.float32)
        if half == 1:
            w8[2 * b, 0] = 1.0
        in_maps.append({
            "x_own": np.ascontiguousarray(x[b, off:off + OWN]),
            "x_full": np.ascontiguousarray(x[b]),
            "consts": consts,
            "wqkv": wqkv,
            "wqkvT": wqkvT,
            "bqkv": bqkv,
            "woa": woa,
            "bo": bo,
            "mask": make_mask(off),
            "identf": ident,
            "gupT": gupT,
            "gusc": gusc,
            "dnpT": dnpT,
            "dnsc": dnsc,
            "shifts": shifts,
            "invs": invs,
            "wqt": wqt,
            "wkt": wkt,
            "wvt": wvt,
            "wot": wot,
            "w8": w8,
        })
    res = run_bass_kernel_spmd(nc, in_maps, core_ids=list(range(8)))
    out = np.empty((4, S, D), dtype=np.float32)
    for c in range(8):
        b, half = c // 2, c % 2
        out[b, half * OWN:(half + 1) * OWN] = res.results[c]["out"]
    return out


# version 13
# speedup vs baseline: 1.1183x; 1.0034x over previous
import contextlib
import numpy as np
import ml_dtypes
import concourse.bass as bass
import concourse.mybir as mybir
from concourse.bass_utils import run_bass_kernel_spmd
from concourse.alu_op_type import AluOpType as Alu

F32 = mybir.dt.float32
BF16 = mybir.dt.bfloat16
F16 = mybir.dt.float16
I32 = mybir.dt.int32
U8 = mybir.dt.uint8
AF = mybir.ActivationFunctionType
AX = mybir.AxisListType
BF16NP = ml_dtypes.bfloat16

ENGINES = ("sync", "tensor", "vector", "scalar", "gpsimd")
SERIAL_ENGINES = ("vector", "scalar", "gpsimd")

D = 2048
DC = 16
NH = 16
HD = 128
S = 4096
OWN = 2048
TT = 512
NTT = OWN // TT
SCALE = 1.0 / float(np.sqrt(128.0))
EPS_LN = 1e-5
EPS_RMS = 1e-6
II = 5632
DENOM_B = float(np.sqrt(2048.0) / 4096.0)

MASK_Y0 = 2432
MASK_W = MASK_Y0 + 512

STAGE = 4
ATTN_LA = 2


class Prog:
    def __init__(self, nc, es):
        self.nc = nc
        self.es = es
        self.ops = {e: [] for e in ENGINES}
        self.counts = {}
        self.sems = {}
        self.last_tok = {e: None for e in ENGINES}
        self.ndma = 0

    def sem(self, name):
        if name not in self.sems:
            self.sems[name] = self.es.enter_context(self.nc.semaphore(name))
            self.counts[name] = 0
        return self.sems[name]

    MAX_CONSTS = 32

    def const_init(self, dram_handle):
        self._const_map = {}
        self._const_vals = []
        self._const_sb = self.es.enter_context(
            self.nc.sbuf_tensor("const_pool", [128, self.MAX_CONSTS], F32))
        self.const_tok = self.dma("sync", self._const_sb[:, :], dram_handle.ap(), sem="dcst")

    def const(self, val):
        val = float(val)
        if val not in self._const_map:
            i = len(self._const_vals)
            assert i < self.MAX_CONSTS
            self._const_vals.append(val)
            ap = self._const_sb[:, i:i + 1]
            self.nc.const_aps.register(F32, val, ap)
            self._const_map[val] = ap
        return self._const_map[val]

    def const_host(self):
        arr = np.zeros((128, self.MAX_CONSTS), dtype=np.float32)
        for i, v in enumerate(self._const_vals):
            arr[:, i] = v
        return arr

    def _tok(self, name, by):
        self.sem(name)
        self.counts[name] += by
        return (name, self.counts[name])

    def op(self, eng, emit, waits=(), sem=None, by=1, serialize=None):
        if serialize is None:
            serialize = eng in SERIAL_ENGINES
        w = [t for t in waits if t is not None]
        if serialize and self.last_tok[eng] is not None:
            w.append(self.last_tok[eng])
        if sem is None and serialize:
            sem = "_ser_" + eng
        tok = self._tok(sem, by) if sem is not None else None
        self.ops[eng].append((emit, w, tok, by))
        if serialize:
            self.last_tok[eng] = tok
        return tok

    def dma(self, eng, out, in_, waits=(), sem="dma", **kw):
        self.ndma += 1

        def emit(e, out=out, in_=in_, kw=kw):
            return e.dma_start(out, in_, **kw)
        return self.op(eng, emit, waits=waits, sem=sem, by=16, serialize=False)

    def wait(self, eng, *toks):
        self.ops[eng].append((None, [t for t in toks if t is not None], None, 0))

    def pe_now(self):
        return ("pe", self.counts.get("pe", 0)) if self.counts.get("pe", 0) else None

    def phase_gate(self):
        return [t for t in (self.pe_now(), self.last_tok["scalar"],
                            self.last_tok["vector"], self.last_tok["gpsimd"])
                if t is not None]

    def finalize(self, block):
        for ename in ENGINES:
            ops = self.ops[ename]
            if not ops:
                continue
            sems = self.sems

            def run(e, ops=ops):
                water = {}
                for emit, waits, tok, by in ops:
                    need = {}
                    for (sn, sv) in waits:
                        if sv > need.get(sn, 0):
                            need[sn] = sv
                    for sn, sv in need.items():
                        if water.get(sn, 0) >= sv:
                            continue
                        e.wait_ge(sems[sn], sv)
                        water[sn] = sv
                    if emit is None:
                        continue
                    instr = emit(e)
                    if tok is not None:
                        instr.then_inc(sems[tok[0]], by)
            getattr(block, ename)(run)



def build_nc():
    nc = bass.Bass("TRN2", target_bir_lowering=False)

    x_own_t = nc.dram_tensor("x_own", [OWN, D], F32, kind="ExternalInput")
    x_full_t = nc.dram_tensor("x_full", [S, D], F32, kind="ExternalInput")
    consts_t = nc.dram_tensor("consts", [128, 32], F32, kind="ExternalInput")
    wqkv_t = nc.dram_tensor("wqkv", [D, 3 * D], F32, kind="ExternalInput")
    wqkvT_t = nc.dram_tensor("wqkvT", [48 * 128, 2048], F32, kind="ExternalInput")
    bqkv_t = nc.dram_tensor("bqkv", [3 * D], F32, kind="ExternalInput")
    woa_t = nc.dram_tensor("woa", [D, D], F32, kind="ExternalInput")
    bo_t = nc.dram_tensor("bo", [D], F32, kind="ExternalInput")
    mask_t = nc.dram_tensor("mask", [128, MASK_W], BF16, kind="ExternalInput")
    ident_t = nc.dram_tensor("identf", [128, 128], F32, kind="ExternalInput")
    gupT_t = nc.dram_tensor("gupT", [256, 2 * II], I32, kind="ExternalInput")
    gusc_t = nc.dram_tensor("gusc", [16, 2 * II], F16, kind="ExternalInput")
    dnpT_t = nc.dram_tensor("dnpT", [704, D], I32, kind="ExternalInput")
    dnsc_t = nc.dram_tensor("dnsc", [44, D], F16, kind="ExternalInput")
    shift_t = nc.dram_tensor("shifts", [128, 1], I32, kind="ExternalInput")
    invs_t = nc.dram_tensor("invs", [128, 1], F32, kind="ExternalInput")
    wqt_t = nc.dram_tensor("wqt", [D, D], F32, kind="ExternalInput")
    wkt_t = nc.dram_tensor("wkt", [D, D], F32, kind="ExternalInput")
    wvt_t = nc.dram_tensor("wvt", [D, D], F32, kind="ExternalInput")
    wot_t = nc.dram_tensor("wot", [D, D], F32, kind="ExternalInput")
    w8_t = nc.dram_tensor("w8", [8, 1], F32, kind="ExternalInput")

    out_t = nc.dram_tensor("out", [OWN, D], F32, kind="ExternalOutput")

    Ksc = nc.dram_tensor("Ksc", [D, S], BF16)
    Vsc = nc.dram_tensor("Vsc", [16 * 128, 32 * 128], BF16)
    Qsc = nc.dram_tensor("Qsc", [D, OWN], BF16)
    WGU = nc.dram_tensor("WGU", [88 * 128, 2048], BF16)
    WDN = nc.dram_tensor("WDN", [44 * 128, 2048], BF16)
    BWq = nc.dram_tensor("BWq", [16 * 128, 2048], BF16)
    BWk = nc.dram_tensor("BWk", [16 * 128, 2048], BF16)
    BWv = nc.dram_tensor("BWv", [16 * 128, 2048], BF16)
    BWot = nc.dram_tensor("BWot", [16 * 128, 2048], BF16)
    XN3 = nc.dram_tensor("XN3", [D, OWN], BF16)
    KVT = nc.dram_tensor("KVT", [D, OWN], BF16)
    AGI = nc.dram_tensor("AGI", [1, D], F32)
    AGO = nc.dram_tensor("AGO", [8, D], F32)

    es = contextlib.ExitStack()
    with es:
        p = Prog(nc, es)
        p.const_init(consts_t)
        p.const(EPS_LN)
        p.const(EPS_RMS)

        def mksb(stack):
            def sb(n, sh, dt):
                return stack.enter_context(nc.sbuf_tensor(n, sh, dt))
            return sb

        sbg = mksb(es)
        identf = sbg("identf_sb", [128, 128], F32)
        onesr1 = sbg("onesr1", [1, 128], BF16)
        onescol = sbg("onescol", [128, 1], BF16)
        bo_col = sbg("bo_col", [128, 16], F32)
        mh_col = sbg("mh_col", [128, NH], F32)

        psf = [es.enter_context(nc.psum_tensor(f"psf{i}", [128, 512], F32)) for i in range(6)]
        psbk = [es.enter_context(nc.psum_tensor(f"psb{i}", [128, 1024], BF16)) for i in range(2)]

        t_id = p.dma("sync", identf[:, :], ident_t.ap(), sem="did")
        t_ones = p.op("vector", lambda e: e.memset(onesr1[:, :], 1.0))
        t_onec = p.op("vector", lambda e: e.memset(onescol[:, :], 1.0))
        t_bcol = None
        for oc in range(16):
            t_bcol = p.dma("sync", bo_col[:, oc:oc + 1],
                           bass.AP(bo_t, oc * 128, [[1, 128], [1, 1]]), sem="dbo")

        esA = contextlib.ExitStack()
        sba = mksb(esA)
        nrm = sba("nrm", [1, 256], F32)
        rows = sba("rows", [1, 64], F32)
        rows_b = sba("rows_b", [1, 64], BF16)
        identb = sba("identb_sb", [128, 128], BF16)
        bqkv_col = sba("bqkv_col", [128, 32], F32)
        xtm = [sba(f"xtm{i}", [128, D], F32) for i in range(2)]
        xntm = [sba(f"xntm{i}", [128, D], BF16) for i in range(2)]
        xnf = sba("xnf", [128, DC * 2048], BF16)
        bns24 = [sba(f"bns24_{i}", [128, 24], F32) for i in range(2)]
        bna = [sba(f"bna{i}", [128, 2], F32) for i in range(2)]
        lncol = [sba(f"lncol{i}", [128, 4], F32) for i in range(2)]
        wstg = [sba(f"wstg{i}", [128, 2048], F32) for i in range(2)]
        wbf = [sba(f"wbf{i}", [128, 2048], BF16) for i in range(2)]
        wvstg = sba("wvstg", [128, 512], F32)
        wvbf = sba("wvbf", [128, DC * 512], BF16)
        kst = [sba(f"kst{i}", [128, 512], BF16) for i in range(2)]
        ksq = sba("ksq", [128, 512], BF16)
        vst = [sba(f"vst{i}", [128, 512], BF16) for i in range(2)]
        vrow = sba("vrow", [1, 512], F32)
        vrowb = sba("vrowb", [1, 512], BF16)

        t_idb = p.op("vector", lambda e: e.tensor_copy(identb[:, :], identf[:, :]), waits=[t_id])
        t_bqc = None
        for oc in range(16):
            p.dma("sync", bqkv_col[:, oc:oc + 1],
                  bass.AP(bqkv_t, oc * 128, [[1, 128], [1, 1]]), sem="dbq")
            t_bqc = p.dma("sync", bqkv_col[:, 16 + oc:17 + oc],
                  bass.AP(bqkv_t, D + oc * 128, [[1, 128], [1, 1]]), sem="dbq")

        NBP = 6
        dq_w = [sba(f"dqw{i}", [128, 512], I32) for i in range(NBP)]
        dq_f_b = [sba(f"dq_f{i}", [128, 512], BF16) for i in range(NBP)]
        dq_s_b = [sba(f"dq_s{i}", [128, 512], BF16) for i in range(NBP)]
        dq_m_b = [sba(f"dq_m{i}", [128, 512], BF16) for i in range(NBP)]
        dq_t1_b = [sba(f"dq_t1{i}", [128, 512], BF16) for i in range(NBP)]
        dq_out = [sba(f"dqo{i}", [128, 512], BF16) for i in range(NBP)]
        dq_srow = [sba(f"dq_srow{i}", [1, 512], F16) for i in range(NBP)]
        dq_srowb = [sba(f"dq_srowb{i}", [1, 512], BF16) for i in range(NBP)]
        shift_sb = sba("shift_sb", [128, 1], I32)
        invs_sb = sba("invs_sb", [128, 1], F32)
        wcst = [sba(f"wcst{i}", [128, 1024], F32) for i in range(2)]
        wcb = sba("wcb", [128, 1024], BF16)
        t_shift = p.dma("sync", shift_sb[:, :], shift_t.ap(), sem="dsh")
        t_invs = p.dma("sync", invs_sb[:, :], invs_t.ap(), sem="dsi")
        p.const(1.0)
        p.const(-4.0)
        p.const(-3.5)
        p.const(-6.0)

        DQ_TILES = ([("gu", c, o0) for c in range(16) for o0 in range(0, 2 * II, 512)]
                    + [("dn", c, o0) for c in range(44) for o0 in range(0, D, 512)])
        dq_state = {"w": [None] * NBP, "o": [None] * NBP, "bank": [None, None],
                    "srow": [None] * NBP, "srowb": [None] * NBP,
                    "f_r": [None] * NBP, "s_r": [None] * NBP, "m_r": [None] * NBP,
                    "t1_r": [None] * NBP}

        NT_DQ = len(DQ_TILES)
        dq_cur = [0] * 12
        dq_tok = {}

        def _dq_info(j):
            kind, c, o0 = DQ_TILES[j]
            if kind == "gu":
                return (gupT_t, 2 * II, gusc_t, WGU, c, o0, j % NBP, 3 + (j % 2), kind)
            return (dnpT_t, D, dnsc_t, WDN, c, o0, j % NBP, 3 + (j % 2), kind)

        def dq_advance(n):
            st = dq_state
            olds = list(dq_cur)
            NB = NBP
            lims = [0] * 12
            lims[0] = min(olds[0] + n, NT_DQ, olds[1] + NB, olds[3] + NB)
            lims[1] = min(olds[0], olds[10] + NB)
            lims[2] = min(olds[1], olds[3] + NB)
            lims[3] = min(olds[2], olds[5] + NB)
            lims[4] = min(olds[3], olds[9] + NB)
            lims[5] = min(olds[4], olds[9] + NB, olds[10] + NB)
            lims[6] = min(olds[5], olds[8] + NB)
            lims[7] = olds[6]
            lims[8] = olds[7]
            lims[9] = olds[8]
            lims[10] = min(olds[9], olds[11] + NB)
            lims[11] = olds[10]
            for k in range(12):
                lims[k] = max(lims[k], olds[k])
            for k in range(12):
                for j in range(dq_cur[k], lims[k]):
                    srcw, ncols, scl, wdst, c, o0, par, bk, kind = _dq_info(j)
                    dq_f, dq_s, dq_m, dq_t1 = (dq_f_b[par], dq_s_b[par],
                                               dq_m_b[par], dq_t1_b[par])
                    if k == 0:
                        dq_tok[(0, j)] = p.dma("gpsimd", dq_w[par][:, :],
                                               bass.AP(srcw, (16 * c) * ncols + o0,
                                                       [[ncols, 16], [0, 8], [1, 512]]),
                                               waits=[st["w"][par]], sem=f"ddq{par}")
                        dq_tok[("sr", j)] = p.dma("gpsimd", dq_srow[par][:, :],
                                                  bass.AP(scl, c * ncols + o0,
                                                          [[1, 1], [1, 512]]),
                                                  waits=[st["srow"][par]], sem=f"ddqs{par}")
                    elif k == 1:
                        t = p.op("scalar", (lambda par: lambda e: e.mul(
                            dq_srowb[par][:, :], dq_srow[par][:, :], -1.0))(par),
                            waits=[dq_tok.pop(("sr", j)), st["srowb"][par]])
                        st["srow"][par] = t
                        dq_tok[(1, j)] = t
                    elif k == 2:
                        t = p.op("vector", (lambda par: lambda e: e.tensor_scalar(
                            dq_w[par][:, :], dq_w[par][:, :], shift_sb[:, :], None,
                            Alu.bitwise_and))(par),
                            waits=[dq_tok.pop((0, j)), t_shift])
                        dq_tok[(2, j)] = t
                    elif k == 3:
                        t = p.op("scalar", (lambda par, dq_f: lambda e: e.activation(
                            dq_f[:, :], dq_w[par][:, :], AF.Identity,
                            bias=-4.0, scale=invs_sb[:, :]))(par, dq_f),
                            waits=[dq_tok.pop((2, j)), t_invs, p.const_tok, st["f_r"][par]])
                        st["w"][par] = t
                        dq_tok[(3, j)] = t
                    elif k == 4:
                        dq_tok[(4, j)] = p.op("scalar", (lambda par, dq_f, dq_s: lambda e: e.activation(
                            dq_s[:, :], dq_f[:, :], AF.Sign, bias=-3.5))(par, dq_f, dq_s),
                            waits=[dq_tok[(3, j)], st["s_r"][par]])
                    elif k == 5:
                        t = p.op("vector", (lambda par, dq_s, dq_f, dq_m: lambda e: e.scalar_tensor_tensor(
                            dq_m[:, :], dq_s[:, :], -4.0, dq_f[:, :],
                            Alu.mult, Alu.add))(par, dq_s, dq_f, dq_m),
                            waits=[dq_tok.pop((4, j)), st["m_r"][par], dq_tok.pop((3, j))])
                        st["f_r"][par] = t
                        dq_tok[(5, j)] = t
                    elif k == 6:
                        dq_tok[(6, j)] = p.op("scalar", (lambda par, dq_m, dq_t1: lambda e: e.activation(
                            dq_t1[:, :], dq_m[:, :], AF.Relu, bias=-6.0))(par, dq_m, dq_t1),
                            waits=[dq_tok[(5, j)], st["t1_r"][par]])
                    elif k == 7:
                        dq_tok[(7, j)] = p.op("vector", (lambda par, dq_m, dq_t1: lambda e: e.scalar_tensor_tensor(
                            dq_t1[:, :], dq_m[:, :], -2.0, dq_t1[:, :],
                            Alu.add, Alu.add))(par, dq_m, dq_t1),
                            waits=[dq_tok.pop((6, j))])
                    elif k == 8:
                        t = p.op("vector", (lambda par, dq_m, dq_t1: lambda e: e.scalar_tensor_tensor(
                            dq_m[:, :], dq_m[:, :], 0.5, dq_t1[:, :],
                            Alu.mult, Alu.max))(par, dq_m, dq_t1),
                            waits=[dq_tok.pop((7, j)), dq_tok.pop((5, j))])
                        st["t1_r"][par] = t
                        dq_tok[(8, j)] = t
                    elif k == 9:
                        t = p.op("gpsimd", (lambda par, dq_s, dq_m: lambda e: e.tensor_tensor(
                            dq_m[:, :], dq_s[:, :], dq_m[:, :], Alu.mult))(par, dq_s, dq_m),
                            waits=[dq_tok.pop((8, j))])
                        st["s_r"][par] = t
                        st["m_r"][par] = t
                        dq_tok[(9, j)] = t
                    elif k == 10:
                        tbc = p.op("tensor", (lambda bk, par: lambda e: e.matmul(
                            psf[bk][:, :], onesr1[:, :], dq_srowb[par][0:1, :],
                            start=True, stop=True))(bk, par),
                            waits=[dq_tok.pop((1, j)), t_ones, st["bank"][j % 2]], sem="pe")
                        st["srowb"][par] = tbc
                        t_out = p.op("vector", (lambda par, bk, dq_m: lambda e: e.tensor_tensor(
                            dq_out[par][:, :], dq_m[:, :], psf[bk][:, :],
                            Alu.mult))(par, bk, dq_m),
                            waits=[dq_tok.pop((9, j)), tbc, st["o"][par]])
                        st["bank"][j % 2] = t_out
                        dq_tok[(10, j)] = t_out
                    else:
                        if kind == "gu":
                            dst = bass.AP(WGU, (o0 // 128) * 128 * 2048 + c * 128,
                                          [[2048, 128], [128 * 2048, 4], [1, 128]])
                        else:
                            dst = bass.AP(WDN, c * 128 * 2048 + o0, [[2048, 128], [1, 512]])
                        st["o"][par] = p.dma("gpsimd", dst, dq_out[par][:, :],
                                             waits=[dq_tok.pop((10, j))], sem=f"ddqo{par}")
                dq_cur[k] = lims[k]

        def emit_dequant(lo, hi):
            while dq_cur[11] < hi:
                dq_advance(max(2, hi - dq_cur[0]))

        bw_state = {"l": [None, None], "c": None}

        def emit_wcast(lo, hi):
            pairs = [(wqt_t, BWq), (wkt_t, BWk), (wvt_t, BWv), (wot_t, BWot)]
            for i in range(lo, hi):
                wi, r = i // 16, i % 16
                srct, dstt = pairs[wi]
                for hf in range(2):
                    par = (2 * i + hf) % 2
                    t_l = p.dma("scalar", wcst[par][:, :],
                                srct.ap()[r * 128:(r + 1) * 128, hf * 1024:(hf + 1) * 1024],
                                waits=[bw_state["l"][par]], sem=f"dbw{par}")
                    t_c = p.op("vector", (lambda par: lambda e: e.tensor_copy(
                        wcb[:, :], wcst[par][:, :]))(par), waits=[t_l, bw_state["c"]])
                    bw_state["l"][par] = t_c
                    if wi < 3:
                        dst = bass.AP(dstt, (hf * 8) * 128 * 2048 + r * 128,
                                      [[2048, 128], [128 * 2048, 8], [1, 128]])
                    else:
                        dst = bass.AP(dstt, r * 128 * 2048 + hf * 1024,
                                      [[2048, 128], [1, 1024]])
                    t_s = p.dma("scalar", dst, wcb[:, :], waits=[t_c], sem="dbws")
                    bw_state["c"] = t_s

        dq_cursor = [0, 0]

        def dq_tick(n):
            dq_advance(n)
            lo2 = dq_cursor[1]
            hi2 = min(lo2 + 1, 64)
            if hi2 > lo2:
                emit_wcast(lo2, hi2)
                dq_cursor[1] = hi2

        xtm_free = [None, None]
        psb_free = [None, None]
        wcast_free = [None, None]
        wbf_free = [None, None]
        wv_free = [None]
        wvbf_free = [None]
        kst_free = [[], []]
        ksq_free = [None]
        vst_free = [None, None]
        vrow_free = [None]
        bank_free = [None] * 7
        xnf_ready = None
        xnf_read = None
        nrm_idx = [0]
        prev_xn = [None, None]
        prev_r1 = [None, None]

        def ln_tile(src_ap, par):
            xt = xtm[par]
            bs, ba, lc = bns24[par], bna[par], lncol[par]
            t_load = p.dma("sync", xt[:, :], src_ap, waits=[xtm_free[par]], sem=f"dax{par}")
            tb_tok = None
            for i in range(4):
                tb_tok = p.op("vector", (lambda i: lambda e: e.bn_stats(
                    bs[:, i * 6:(i + 1) * 6], xt[:, i * 512:(i + 1) * 512]))(i),
                    waits=[t_load])
            t_ag = p.op("vector", lambda e: e.bn_aggr(ba[:, :], bs[:, :]),
                        waits=[tb_tok, prev_r1[par]])
            t_r1 = p.op("scalar", lambda e: e.activation(
                lc[:, 0:1], ba[:, 1:2], AF.Ln, bias=EPS_LN), waits=[t_ag, p.const_tok])
            prev_r1[par] = t_r1
            t_r2 = p.op("scalar", lambda e: e.activation(
                lc[:, 0:1], lc[:, 0:1], AF.Exp, scale=-0.5), waits=[t_r1])
            t_r3 = p.op("vector", lambda e: e.tensor_tensor(
                lc[:, 1:2], ba[:, 0:1], lc[:, 0:1], Alu.mult),
                waits=[t_r2, prev_xn[par]])
            t_r4 = p.op("vector", lambda e: e.tensor_scalar(
                lc[:, 1:2], lc[:, 1:2], -1.0, None, Alu.mult), waits=[t_r3])
            t_xn = p.op("scalar", (lambda par: lambda e: e.activation(
                xntm[par][:, :], xtm[par][:, :], AF.Identity,
                bias=lc[:, 1:2], scale=lc[:, 0:1]))(par), waits=[t_r4])
            prev_xn[par] = t_xn
            xtm_free[par] = t_xn
            return t_xn

        def transpose_to_xnf(par, tb, t_xn):
            last = None
            for grp in range(4):
                bk = grp % 2
                half = grp // 2
                dcs = [half * 8 + bk + 2 * j for j in range(4)]
                t_tp = None
                for j, dc in enumerate(dcs):
                    slot = half * 4 + j
                    t_tp = p.op("tensor", (lambda dc, bk, slot, par: lambda e: e.transpose(
                        psbk[bk][:, slot * 128:(slot + 1) * 128],
                        xntm[par][:, dc * 128:(dc + 1) * 128], identb[:, :]))(dc, bk, slot, par),
                        waits=[t_xn, t_idb, psb_free[bk], xnf_read], sem="pe")
                dst = bass.AP(xnf.handle if hasattr(xnf, 'handle') else xnf._handle, 0, [[1, 1]]) if False else None
                t_ev = p.op("vector", (lambda dcs, bk, half, tb: lambda e: e.tensor_copy(
                    bass.AP(xnf[:, :].tensor, tb * 128 + dcs[0] * 2048,
                            [[DC * 2048, 128], [2 * 2048, 4], [1, 128]]),
                    psbk[bk][:, half * 512:(half + 1) * 512]))(dcs, bk, half, tb),
                    waits=[t_tp])
                psb_free[bk] = t_ev
                last = t_ev
            return last

        def proj_fm(seg, kind):
            col0 = D if kind == "k" else 0
            scdst = Ksc if kind == "k" else Qsc
            for oc in range(NH):
                wpar = oc % 2
                ocg = (col0 // 128) + oc
                tw = [p.dma("sync", wstg[wpar][:, :],
                            wqkvT_t.ap()[ocg * 128:(ocg + 1) * 128, :],
                            waits=[wcast_free[wpar]], sem=f"daw{wpar}")]
                t_cast = p.op("vector", (lambda wpar: lambda e: e.tensor_copy(
                    wbf[wpar][:, :], wstg[wpar][:, :]))(wpar),
                    waits=tw + [wbf_free[wpar]])
                wcast_free[wpar] = t_cast
                for tt in range(4):
                    bk = (oc * 4 + tt) % 2
                    ps = psf[bk]

                    def mm(e, wpar=wpar, tt=tt, ps=ps):
                        last = None
                        for dc in range(DC):
                            last = e.matmul(
                                ps[:, :], wbf[wpar][:, dc * 128:(dc + 1) * 128],
                                xnf[:, dc * 2048 + tt * 512: dc * 2048 + (tt + 1) * 512],
                                start=(dc == 0), stop=(dc == DC - 1))
                        return last
                    t_mm = p.op("tensor", mm, waits=[t_cast, xnf_ready, bank_free[bk]],
                                sem="pe")
                    kpar = (oc * 4 + tt) % 2
                    ci = (16 + oc) if kind == "k" else oc
                    t_ev = p.op("vector", (lambda kpar, ps, ci: lambda e: e.tensor_scalar(
                        kst[kpar][:, :], ps[:, :], bqkv_col[:, ci:ci + 1], None,
                        Alu.add))(kpar, ps, ci),
                        waits=[t_mm, t_bqc] + kst_free[kpar])
                    bank_free[bk] = t_ev
                    colbase = (seg * 2048 if kind == "k" else 0) + tt * 512
                    t_st = p.dma("sync",
                                 scdst.ap()[oc * 128:(oc + 1) * 128, colbase: colbase + 512],
                                 kst[kpar][:, :], waits=[t_ev], sem=f"dsck{kpar}")
                    t_sq = p.op("scalar", (lambda kpar: lambda e: e.activation(
                        ksq[:, :], kst[kpar][:, :], AF.Square))(kpar),
                        waits=[t_ev, ksq_free[0]])
                    kst_free[kpar] = [t_st, t_sq]
                    t_nmm = p.op("tensor", lambda e: e.matmul(
                        psf[2][0:1, :], onescol[:, :], ksq[:, :], start=True, stop=True),
                        waits=[t_sq, t_onec, bank_free[2]], sem="pe")
                    idx = nrm_idx[0]
                    nrm_idx[0] += 1
                    t_red = p.op("vector", (lambda idx: lambda e: e.reduce_max(
                        nrm[0:1, idx:idx + 1], psf[2][0:1, :], AX.X, negate=True))(idx),
                        waits=[t_nmm])
                    ksq_free[0] = t_red
                    bank_free[2] = t_red
                wbf_free[wpar] = p.pe_now()
                dq_tick(5)

        def proj_v(seg):
            for vc2 in range(4):
                t_vr = p.dma("sync", vrow[:, :],
                             bass.AP(bqkv_t, 2 * D + vc2 * 512, [[1, 1], [1, 512]]),
                             waits=[vrow_free[0]], sem="davr")
                t_vrb = p.op("vector", lambda e: e.tensor_copy(vrowb[:, :], vrow[:, :]),
                             waits=[t_vr])
                t_casts = [t_vrb]
                for dc in range(DC):
                    t_l = p.dma("sync", wvstg[:, :],
                                bass.AP(wqkv_t, dc * 128 * (3 * D) + 2 * D + vc2 * 512,
                                        [[3 * D, 128], [1, 512]]),
                                waits=[wv_free[0]], sem="davw")
                    t_c = p.op("vector", (lambda dc: lambda e: e.tensor_copy(
                        wvbf[:, dc * 512:(dc + 1) * 512], wvstg[:, :]))(dc),
                        waits=[t_l, wvbf_free[0]])
                    wv_free[0] = t_c
                    t_casts.append(t_c)
                for tb in range(16):
                    bk = tb % 2
                    ps = psf[bk]

                    def mm(e, tb=tb, vc2=vc2, ps=ps):
                        for dc in range(DC):
                            e.matmul(ps[:, :],
                                     xnf[:, dc * 2048 + tb * 128: dc * 2048 + (tb + 1) * 128],
                                     wvbf[:, dc * 512:(dc + 1) * 512],
                                     start=(dc == 0), stop=False)
                        return e.matmul(ps[:, :], onesr1[:, :], vrowb[0:1, :],
                                        start=False, stop=True)
                    t_mm = p.op("tensor", mm,
                                waits=t_casts + [xnf_ready, bank_free[bk], t_ones],
                                sem="pe")
                    vpar = tb % 2
                    t_ev = p.op("vector", (lambda vpar, ps: lambda e: e.tensor_copy(
                        vst[vpar][:, :], ps[:, :]))(vpar, ps),
                        waits=[t_mm, vst_free[vpar]])
                    bank_free[bk] = t_ev
                    kc_ = seg * 16 + tb
                    t_st = p.dma("sync",
                                 bass.AP(Vsc, (vc2 * 4) * 128 * 4096 + kc_ * 128,
                                         [[4096, 128], [128 * 4096, 4], [1, 128]]),
                                 vst[vpar][:, :], waits=[t_ev], sem=f"dscv{vpar}")
                    vst_free[vpar] = t_st
                wvbf_free[0] = p.pe_now()
                vrow_free[0] = p.pe_now()
                dq_tick(5)

        for seg in range(3):
            for tb in range(16):
                t_xn = ln_tile(
                    x_full_t.ap()[seg * 2048 + tb * 128: seg * 2048 + (tb + 1) * 128, :]
                    if seg < 2 else x_own_t.ap()[tb * 128:(tb + 1) * 128, :],
                    tb % 2)
                xnf_ready = transpose_to_xnf(tb % 2, tb, t_xn)
                dq_tick(4)
            if seg < 2:
                proj_fm(seg, "k")
                proj_v(seg)
            else:
                proj_fm(seg, "q")
            xnf_read = p.pe_now()
            wbf_free[0] = xnf_read
            wbf_free[1] = xnf_read

        for h in range(NH):
            p.op("vector", (lambda h: lambda e: e.tensor_reduce(
                rows[0:1, h:h + 1], nrm[0:1, h * 4:(h + 1) * 4], AX.X, Alu.min))(h))
            p.op("vector", (lambda h: lambda e: e.tensor_reduce(
                rows[0:1, 16 + h:17 + h], nrm[0:1, 64 + h * 4:64 + (h + 1) * 4],
                AX.X, Alu.min))(h))
            p.op("vector", (lambda h: lambda e: e.tensor_reduce(
                rows[0:1, 32 + h:33 + h], nrm[0:1, 128 + h * 4:128 + (h + 1) * 4],
                AX.X, Alu.min))(h))
        p.op("vector", lambda e: e.tensor_tensor(
            rows[0:1, 0:16], rows[0:1, 0:16], rows[0:1, 16:32], Alu.min))
        p.op("vector", lambda e: e.tensor_tensor(
            rows[0:1, 0:16], rows[0:1, 0:16], rows[0:1, 32:48], Alu.add))
        p.op("vector", lambda e: e.tensor_scalar(
            rows[0:1, 0:16], rows[0:1, 0:16], SCALE / 2.0, None, Alu.mult))
        t7 = p.op("vector", lambda e: e.tensor_copy(rows_b[0:1, 0:16], rows[0:1, 0:16]))
        t_mh_done = None
        for h in range(NH):
            t_b = p.op("tensor", (lambda h: lambda e: e.matmul(
                psf[2][0:128, 0:1], onesr1[:, :], rows_b[0:1, h:h + 1],
                start=True, stop=True))(h), waits=[t7, t_ones, bank_free[2]], sem="pe")
            t_mh_done = p.op("vector", (lambda h: lambda e: e.tensor_copy(
                mh_col[:, h:h + 1], psf[2][0:128, 0:1]))(h), waits=[t_b])
            bank_free[2] = t_mh_done

        while dq_cur[11] < NT_DQ:
            dq_advance(3)
        while dq_cursor[1] < 64:
            emit_wcast(dq_cursor[1], dq_cursor[1] + 1)
            dq_cursor[1] += 1
        a_done = [(sn, p.counts[sn]) for sn in
                  ("dsck0", "dsck1", "dscv0", "dscv1", "dbws")
                  + tuple(f"ddqo{i}" for i in range(NBP))]
        a_done.append(t_mh_done)
        a_done.append(p.pe_now())
        dq_bank_done = [dq_state["bank"][0], dq_state["bank"][1]]
        esA.close()

        stream = [sbg(f"st{i}", [128, OWN], F32) for i in range(DC)]

        esB = contextlib.ExitStack()
        sbb = mksb(esB)
        NS = 3
        NPT = 6
        LA = 3
        GH = 2
        NG = NH // GH
        sbanks = [psf[0], psf[1], psf[2]]
        ksb = [sbb(f"ksb{i}", [128, S], BF16) for i in range(2)]
        vsb = [sbb(f"vsb{i}", [128, S], BF16) for i in range(2)]
        qsb = [sbb(f"qsb{i}", [128, OWN], BF16) for i in range(2)]
        msb = sbb("msb", [128, MASK_W], BF16)
        ptb = [sbb(f"ptb{i}", [128, 512], BF16) for i in range(NPT)]
        rec_bc = sbb("rec_bc", [128, 512], BF16)
        recrow = sbb("recrow", [1, 512], F32)
        recrowb = sbb("recrowb", [1, 512], BF16)
        attn_g = [sbb(f"attn{i}", [128, OWN], BF16) for i in range(GH)]
        wo_stg = sbb("wo_stg", [128, 1024], F32)
        wo_bf = [sbb(f"wo_bf{i}", [128, 2048], BF16) for i in range(GH)]
        xch = [sbb(f"xch{i}", [128, 128], F32) for i in range(2)]
        xfm = sbb("xfm", [128, 512], F32)

        t_mask = p.dma("sync", msb[:, :], mask_t.ap(), waits=a_done, sem="dbm")

        kvq_free = [None, None]
        pt_free = [None] * NPT
        attn_free = []
        qidx = [0]
        pending_tail = [None]
        last_at = [None]
        sumbanks = [psf[3], psbk[0][:, :].bitcast(F32)]
        avbanks = [psf[4], psbk[1][:, :].bitcast(F32)]
        wo_stg_free = None
        wo_free = [None] * GH
        xch_free = [None, None]
        stream_tok = [[None] * NTT for _ in range(DC)]
        sbank_free = [bank_free[0], bank_free[1], bank_free[2]]
        sum_free = [dq_bank_done[0], psb_free[0]]
        av_free = [dq_bank_done[1], psb_free[1]]
        y_free = None

        for g in range(NG):
            attn_done = []
            for hh in range(GH):
                h = GH * g + hh
                par = h % 2
                t_k = p.dma("sync", ksb[par][:, :], Ksc.ap()[h * 128:(h + 1) * 128, :],
                            waits=a_done + [kvq_free[par]], sem=f"dbk{par}")
                t_q = p.dma("sync", qsb[par][:, :], Qsc.ap()[h * 128:(h + 1) * 128, :],
                            waits=a_done + [kvq_free[par]], sem=f"dbq{par}")
                t_v = p.dma("sync", vsb[par][:, :],
                            Vsc.ap()[h * 128:(h + 1) * 128, :],
                            waits=a_done + [kvq_free[par]], sem=f"dbv{par}")
                t_at = None
                for qb in range(NTT):
                    nkc = 20 + 4 * qb
                    m_toks = [None] * nkc
                    qi = qidx[0]
                    qidx[0] += 1
                    sb_ = qi % 2
                    sumbank = sumbanks[sb_]
                    avbank = avbanks[sb_]
                    t_sum_last = None
                    t_av_last = None

                    def emit_s(kc, qb=qb, par=par, h=h, t_k=t_k, t_q=t_q, m_toks=m_toks):
                        bs = kc % NS
                        t_s = p.op("tensor", (lambda bs, kc, qb, par: lambda e: e.matmul(
                            sbanks[bs][:, :], ksb[par][:, kc * 128:(kc + 1) * 128],
                            qsb[par][:, qb * 512:(qb + 1) * 512],
                            start=True, stop=True))(bs, kc, qb, par),
                            waits=[t_k, t_q, sbank_free[bs]], sem="pe")
                        t_e = p.op("scalar", (lambda bs, kc, h: lambda e: e.activation(
                            ptb[kc % NPT][:, :], sbanks[bs][:, :], AF.Exp,
                            bias=mh_col[:, h:h + 1], scale=SCALE))(bs, kc, h),
                            waits=[t_s, t_mh_done, pt_free[kc % NPT]])
                        sbank_free[bs] = t_e
                        tv = 128 * kc - 512 * qb
                        t_m = t_e
                        if tv >= 0:
                            eng = "gpsimd" if (kc % 2 == 0) else "vector"
                            t_m = p.op(eng, (lambda kc, tv: lambda e: e.tensor_tensor(
                                ptb[kc % NPT][:, :], ptb[kc % NPT][:, :],
                                msb[:, MASK_Y0 - tv: MASK_Y0 - tv + 512],
                                Alu.mult))(kc, tv),
                                waits=[t_e, t_mask])
                        m_toks[kc] = t_m

                    def emit_sumav(kc, nkc=nkc, par=par, m_toks=m_toks,
                                   sumbank=sumbank, avbank=avbank, sb_=sb_):
                        t_sum = p.op("tensor", (lambda kc, nkc: lambda e: e.matmul(
                            sumbank[0:1, :], onescol[:, :], ptb[kc % NPT][:, :],
                            start=(kc == 0), stop=(kc == nkc - 1)))(kc, nkc),
                            waits=[m_toks[kc], t_onec] + ([sum_free[sb_]] if kc == 0 else []),
                            sem="pe")
                        t_av = p.op("tensor", (lambda kc, nkc, par: lambda e: e.matmul(
                            avbank[:, :], vsb[par][:, kc * 128:(kc + 1) * 128],
                            ptb[kc % NPT][:, :],
                            start=(kc == 0), stop=(kc == nkc - 1)))(kc, nkc, par),
                            waits=[t_v] + ([av_free[sb_]] if kc == 0 else []),
                            sem="pe")
                        pt_free[kc % NPT] = t_av
                        return t_sum, t_av

                    for kc in range(nkc):
                        emit_s(kc)
                        if kc == 1 and pending_tail[0] is not None:
                            pending_tail[0]()
                            pending_tail[0] = None
                        if kc >= LA:
                            t_sum_last, t_av_last = emit_sumav(kc - LA)
                    for kc in range(nkc - LA, nkc):
                        t_sum_last, t_av_last = emit_sumav(kc)

                    def tail(h=h, hh=hh, qb=qb, t_sum_last=t_sum_last,
                             t_av_last=t_av_last, sumbank=sumbank, avbank=avbank, sb_=sb_):
                        nonlocal y_free
                        t_rc = p.op("vector", (lambda sumbank: lambda e: e.reciprocal(
                            recrow[0:1, :], sumbank[0:1, :]))(sumbank),
                            waits=[t_sum_last])
                        sum_free[sb_] = t_rc
                        t_rcb = p.op("vector", lambda e: e.tensor_copy(recrowb[0:1, :], recrow[0:1, :]),
                                     waits=[t_rc])
                        t_bc = p.op("tensor", lambda e: e.matmul(
                            psf[5][:, :], onesr1[:, :], recrowb[0:1, :], start=True, stop=True),
                            waits=[t_rcb, y_free, t_ones], sem="pe")
                        t_bce = p.op("vector", lambda e: e.tensor_copy(rec_bc[:, :], psf[5][:, :]),
                                     waits=[t_bc])
                        y_free = t_bce
                        t_at2 = p.op("vector", (lambda hh, qb, avbank: lambda e: e.tensor_tensor(
                            attn_g[hh][:, qb * 512:(qb + 1) * 512], avbank[:, :], rec_bc[:, :],
                            Alu.mult))(hh, qb, avbank),
                            waits=[t_av_last, t_bce] + attn_free)
                        av_free[sb_] = t_at2
                        attn_done.append(t_at2)
                        last_at[0] = t_at2
                    pending_tail[0] = tail
                if pending_tail[0] is not None:
                    pending_tail[0]()
                    pending_tail[0] = None
                t_at = last_at[0]
                kvq_free[par] = t_at
            for hh in range(GH):
                h = GH * g + hh
                tc0 = p.dma("scalar", wo_stg[:, :], woa_t.ap()[h * 128:(h + 1) * 128, 0:1024],
                            waits=a_done + [wo_stg_free], sem="dbwo")
                th0 = p.op("vector", (lambda hh: lambda e: e.tensor_copy(
                    wo_bf[hh][:, 0:1024], wo_stg[:, :]))(hh),
                    waits=[tc0, None if g == 0 else p.pe_now()])
                tc1 = p.dma("scalar", wo_stg[:, :], woa_t.ap()[h * 128:(h + 1) * 128, 1024:2048],
                            waits=[th0], sem="dbwo")
                wo_free[hh] = p.op("vector", (lambda hh: lambda e: e.tensor_copy(
                    wo_bf[hh][:, 1024:2048], wo_stg[:, :]))(hh), waits=[tc1])
                wo_stg_free = wo_free[hh]
            for tt in range(NTT):
                for dc in range(DC):
                    if g == 0:
                        tp_toks = []
                        for j in range(4):
                            xp = j % 2
                            t_xl = p.dma("scalar", xch[xp][:, :],
                                         x_own_t.ap()[tt * 512 + j * 128: tt * 512 + (j + 1) * 128,
                                                      dc * 128:(dc + 1) * 128],
                                         waits=a_done + [xch_free[xp]], sem=f"dbx{xp}")
                            t_tp = p.op("tensor", (lambda xp, j: lambda e: e.transpose(
                                psf[5][:, j * 128:(j + 1) * 128], xch[xp][:, :],
                                identf[:, :]))(xp, j),
                                waits=[t_xl, t_id, y_free], sem="pe")
                            xch_free[xp] = t_tp
                            tp_toks.append(t_tp)
                        t_xf = p.op("vector", (lambda: lambda e: e.tensor_copy(xfm[:, :], psf[5][:, 0:512]))(),
                                    waits=tp_toks)
                        y_free = t_xf

                    def mmy(e, tt=tt, dc=dc):
                        last = None
                        for hh2 in range(GH):
                            last = e.matmul(psf[5][:, :], wo_bf[hh2][:, dc * 128:(dc + 1) * 128],
                                            attn_g[hh2][:, tt * 512:(tt + 1) * 512],
                                            start=(hh2 == 0), stop=(hh2 == GH - 1))
                        return last
                    t_y = p.op("tensor", mmy,
                               waits=[y_free] + attn_done + list(wo_free),
                               sem="pe")
                    if g == 0:
                        t_se = p.op("vector", (lambda dc, tt: lambda e: e.scalar_tensor_tensor(
                            stream[dc][:, tt * 512:(tt + 1) * 512], psf[5][:, :],
                            bo_col[:, dc:dc + 1], xfm[:, :], Alu.add, Alu.add))(dc, tt),
                            waits=[t_y, t_bcol])
                    else:
                        t_se = p.op("vector", (lambda dc, tt: lambda e: e.tensor_tensor(
                            stream[dc][:, tt * 512:(tt + 1) * 512],
                            stream[dc][:, tt * 512:(tt + 1) * 512], psf[5][:, :],
                            Alu.add))(dc, tt),
                            waits=[t_y, stream_tok[dc][tt]])
                    y_free = t_se
                    stream_tok[dc][tt] = t_se
            attn_free = [p.pe_now()]
        esB.close()

        def make_lnfm(xnb, sqt, rows3, rbrow, eps):
            def lnfm(tt, stream_tok, bank_gate, xnb_gate=()):
                t_xb = []
                for dc in range(DC):
                    t_xb.append(p.op("vector", (lambda dc: lambda e: e.tensor_copy(
                        xnb[dc][:, :], stream[dc][:, tt * 512:(tt + 1) * 512]))(dc),
                        waits=[stream_tok[dc][tt]] + list(xnb_gate)))

                def mmsum(e):
                    last = None
                    for dc in range(DC):
                        last = e.matmul(psf[0][0:1, :], onescol[:, :], xnb[dc][:, :],
                                        start=(dc == 0), stop=(dc == DC - 1))
                    return last
                t_sx = p.op("tensor", mmsum, waits=t_xb + [t_onec] + bank_gate, sem="pe")
                sq_toks = []
                mm2 = None
                nsq = len(sqt)
                for dc in range(DC):
                    sp = dc % nsq
                    t_sq = p.op("scalar", (lambda dc, sp: lambda e: e.activation(
                        sqt[sp][:, :], xnb[dc][:, :], AF.Square))(dc, sp),
                        waits=[t_xb[dc]] + ([sq_toks[dc - nsq]] if dc >= nsq else []))
                    mm2 = p.op("tensor", (lambda dc, sp: lambda e: e.matmul(
                        psf[1][0:1, :], onescol[:, :], sqt[sp][:, :],
                        start=(dc == 0), stop=(dc == DC - 1)))(dc, sp),
                        waits=[t_sq] + (bank_gate if dc == 0 else []), sem="pe")
                    sq_toks.append(mm2)
                t_m = p.op("vector", lambda e: e.tensor_scalar(
                    rows3[0:1, 0:512], psf[0][0:1, :], 1.0 / D, None, Alu.mult),
                    waits=[t_sx])
                t_q = p.op("vector", lambda e: e.tensor_scalar(
                    rows3[0:1, 512:1024], psf[1][0:1, :], 1.0 / D, None, Alu.mult),
                    waits=[mm2])
                t_t = p.op("vector", lambda e: e.tensor_tensor(
                    psf[0][0:1, :], rows3[0:1, 0:512], rows3[0:1, 0:512], Alu.mult),
                    waits=[t_m])
                t_v = p.op("vector", lambda e: e.tensor_tensor(
                    rows3[0:1, 512:1024], rows3[0:1, 512:1024], psf[0][0:1, :],
                    Alu.subtract), waits=[t_q, t_t])
                t_l = p.op("scalar", (lambda: lambda e: e.activation(
                    rows3[0:1, 512:1024], rows3[0:1, 512:1024], AF.Ln, bias=eps))(),
                    waits=[t_v, p.const_tok])
                t_e2 = p.op("scalar", lambda e: e.activation(
                    rows3[0:1, 512:1024], rows3[0:1, 512:1024], AF.Exp, scale=-0.5),
                    waits=[t_l])
                t_rb1 = p.op("vector", lambda e: e.tensor_copy(rbrow[0:1, :], rows3[0:1, 512:1024]),
                             waits=[t_e2])
                t_b1 = p.op("tensor", lambda e: e.matmul(
                    psf[1][:, :], onesr1[:, :], rbrow[0:1, :], start=True, stop=True),
                    waits=[t_rb1, t_ones, t_q], sem="pe")
                tmul = []
                for dc in range(DC):
                    tmul.append(p.op("vector", (lambda dc: lambda e: e.tensor_tensor(
                        xnb[dc][:, :], xnb[dc][:, :], psf[1][:, :], Alu.mult))(dc),
                        waits=[t_b1]))
                t_mr = p.op("vector", lambda e: e.tensor_tensor(
                    rows3[0:1, 0:512], rows3[0:1, 0:512], rows3[0:1, 512:1024], Alu.mult),
                    waits=[t_e2])
                t_ng = p.op("vector", lambda e: e.tensor_scalar(
                    rows3[0:1, 0:512], rows3[0:1, 0:512], -1.0, None, Alu.mult),
                    waits=[t_mr])
                t_rb2 = p.op("vector", lambda e: e.tensor_copy(rbrow[0:1, :], rows3[0:1, 0:512]),
                             waits=[t_ng, t_b1])
                t_b2 = p.op("tensor", lambda e: e.matmul(
                    psf[1][:, :], onesr1[:, :], rbrow[0:1, :], start=True, stop=True),
                    waits=[t_rb2] + tmul, sem="pe")
                t_app = []
                for dc in range(DC):
                    tb = p.op("vector", (lambda dc: lambda e: e.tensor_tensor(
                        xnb[dc][:, :], xnb[dc][:, :], psf[1][:, :], Alu.add))(dc),
                        waits=[t_b2])
                    t_app.append(tb)
                return t_app, t_app[-1]
            return lnfm

        def emit_ffn(call):
            esD = contextlib.ExitStack()
            sbd = mksb(esD)
            xnb = [sbd(f"fx{call}_{i}", [128, 512], BF16) for i in range(DC)]
            hdn = [sbd(f"fh{call}_{i}", [128, 512], BF16) for i in range(44)]
            sqt = [sbd(f"fsq{call}_{i}", [128, 512], BF16) for i in range(1)]
            wgu3 = [sbd(f"fwg{call}_{i}", [128, 2048], BF16) for i in range(3)]
            rows3 = sbd(f"fr3{call}", [1, 1024], F32)
            rbrow = sbd(f"frb{call}", [1, 512], BF16)
            lnfm = make_lnfm(xnb, sqt, rows3, rbrow, EPS_LN)
            pg = p.phase_gate()
            bank_gate = pg
            wslot_free = [None, None, None]
            gidx = [0]
            hdn_free = [pg[0]] * 44
            g_free = None
            u_free = None
            y_free2 = [None] * 6
            first = True
            hd_prev = [None, None]
            for tt in range(NTT):
                t_app, _ = lnfm(tt, stream_tok, bank_gate,
                                xnb_gate=(pg if first else [p.pe_now()]))
                bank_gate = [None]
                for ih in range(2):
                    for ic in range(22):
                        gi = ih * 22 + ic
                        i0 = ih * 2816 + ic * 128
                        fgate = pg if first else []
                        sg_ = (2 * gidx[0]) % 3
                        su_ = (2 * gidx[0] + 1) % 3
                        gidx[0] += 1
                        wg = wgu3[sg_]
                        wu = wgu3[su_]
                        tg = [p.dma("gpsimd", wg[:, :],
                                    WGU.ap()[(i0 // 128) * 128:(i0 // 128) * 128 + 128, :],
                                    waits=[wslot_free[sg_]] + fgate, sem=f"dfg{call}_{sg_}")]
                        tu = [p.dma("scalar", wu[:, :],
                                    WGU.ap()[((II + i0) // 128) * 128:((II + i0) // 128) * 128 + 128, :],
                                    waits=[wslot_free[su_]] + fgate, sem=f"dfu{call}_{su_}")]
                        first = False

                        def mmg(e, wg=wg):
                            last = None
                            for dc in range(DC):
                                last = e.matmul(psf[0][:, :], wg[:, dc * 128:(dc + 1) * 128],
                                                xnb[dc][:, :], start=(dc == 0),
                                                stop=(dc == DC - 1))
                            return last

                        def mmu(e, wu=wu):
                            last = None
                            for dc in range(DC):
                                last = e.matmul(psf[1][:, :], wu[:, dc * 128:(dc + 1) * 128],
                                                xnb[dc][:, :], start=(dc == 0),
                                                stop=(dc == DC - 1))
                            return last
                        t_g = p.op("tensor", mmg, waits=tg + t_app + [g_free], sem="pe")
                        t_uu = p.op("tensor", mmu, waits=tu + t_app + [u_free], sem="pe")
                        wslot_free[sg_] = t_g
                        wslot_free[su_] = t_uu
                        sp = 0
                        t_sl = p.op("scalar", (lambda sp: lambda e: e.activation(
                            sqt[sp][:, :], psf[0][:, :], AF.Silu))(sp),
                            waits=[t_g, hd_prev[sp]])
                        t_hd = p.op("vector", (lambda gi, sp: lambda e: e.tensor_tensor(
                            hdn[gi][:, :], sqt[sp][:, :], psf[1][:, :], Alu.mult))(gi, sp),
                            waits=[t_sl, t_uu, hdn_free[gi]])
                        hd_prev[sp] = t_hd
                        g_free = t_sl
                        u_free = t_hd
                groups = [list(range(0, 6)), list(range(6, 12)), list(range(12, 16))]
                ev_toks = []
                for grp in groups:
                    for k in range(44):
                        wpar = k % 3
                        wdnb = wgu3[wpar]
                        t_w = p.dma("sync", wdnb[:, 0:len(grp) * 128],
                                    bass.AP(WDN, k * 128 * 2048 + grp[0] * 128,
                                            [[2048, 128], [1, len(grp) * 128]]),
                                    waits=[wslot_free[wpar]], sem=f"dfd{call}_{wpar}")

                        def mmd(e, grp=grp, k=k, wdnb=wdnb):
                            last = None
                            for j in range(len(grp)):
                                last = e.matmul(psf[j][:, :],
                                                wdnb[:, j * 128:(j + 1) * 128],
                                                hdn[k][:, :], start=(k == 0),
                                                stop=(k == 43))
                            return last
                        t_d = p.op("tensor", mmd,
                                   waits=[t_w, u_free] + ([y_free2[j] for j in range(len(grp))]
                                                          if k == 0 else []), sem="pe")
                        wslot_free[wpar] = t_d
                    for j, hc in enumerate(grp):
                        t_ev = p.op("vector", (lambda j, hc: lambda e: e.tensor_copy(
                            xnb[hc][:, :], psf[j][:, :]))(j, hc),
                            waits=[t_d])
                        y_free2[j] = t_ev
                        ev_toks.append(t_ev)
                pe_after_down = p.pe_now()
                for gi in range(44):
                    hdn_free[gi] = pe_after_down
                mm2 = None
                rms_mm = [None]
                for hc in range(DC):
                    sp = 0
                    t_sq = p.op("scalar", (lambda hc, sp: lambda e: e.activation(
                        sqt[sp][:, :], xnb[hc][:, :], AF.Square))(hc, sp),
                        waits=ev_toks + [rms_mm[sp]])
                    mm2 = p.op("tensor", (lambda hc, sp: lambda e: e.matmul(
                        psf[0][0:1, :], onescol[:, :], sqt[sp][:, :],
                        start=(hc == 0), stop=(hc == DC - 1)))(hc, sp),
                        waits=[t_sq] + ([y_free2[0]] if hc == 0 else []), sem="pe")
                    rms_mm[sp] = mm2
                t_q2 = p.op("vector", lambda e: e.tensor_scalar(
                    rows3[0:1, 0:512], psf[0][0:1, :], 1.0 / D, None, Alu.mult),
                    waits=[mm2])
                t_l2 = p.op("scalar", (lambda: lambda e: e.activation(
                    rows3[0:1, 0:512], rows3[0:1, 0:512], AF.Ln, bias=EPS_RMS))(),
                    waits=[t_q2])
                t_e3 = p.op("scalar", lambda e: e.activation(
                    rows3[0:1, 0:512], rows3[0:1, 0:512], AF.Exp, scale=-0.5),
                    waits=[t_l2])
                t_rb = p.op("vector", lambda e: e.tensor_copy(rbrow[0:1, :], rows3[0:1, 0:512]),
                            waits=[t_e3])
                t_b3 = p.op("tensor", lambda e: e.matmul(
                    psf[1][:, :], onesr1[:, :], rbrow[0:1, :], start=True, stop=True),
                    waits=[t_rb, u_free], sem="pe")
                t_su = None
                for hc in range(DC):
                    t_t2 = p.op("vector", (lambda hc: lambda e: e.tensor_tensor(
                        sqt[0][:, :], xnb[hc][:, :], psf[1][:, :], Alu.mult))(hc),
                        waits=[t_b3, mm2])
                    t_su = p.op("vector", (lambda hc, tt: lambda e: e.tensor_tensor(
                        stream[hc][:, tt * 512:(tt + 1) * 512],
                        stream[hc][:, tt * 512:(tt + 1) * 512], sqt[0][:, :],
                        Alu.add))(hc, tt),
                        waits=[t_t2, stream_tok[hc][tt]])
                    stream_tok[hc][tt] = t_su
                bank_gate = [p.pe_now(), t_su]
            esD.close()

        def emit_attnb():
            esE = contextlib.ExitStack()
            sbe = mksb(esE)
            xnb = [sbe(f"ex{i}", [128, 512], BF16) for i in range(DC)]
            sqt = [sbe(f"esq{i}", [128, 512], BF16) for i in range(2)]
            rows3 = sbe("er3", [1, 1024], F32)
            rbrow = sbe("erb", [1, 512], BF16)
            wk = sbe("ewk", [128, 2048], BF16)
            wv = sbe("ewv", [128, 2048], BF16)
            kb = sbe("ekb", [128, 512], BF16)
            kvf = sbe("ekv", [128, 512], F32)
            kvtf = sbe("ekt", [128, 512], F32)
            kvtb = sbe("ektb", [128, 512], BF16)
            onesf = sbe("eon", [128, 512], F32)
            carry = sbe("ecar", [128, 16], F32)
            ccol = sbe("eccl", [128, 16], F32)
            agi_sb = sbe("eagi", [16, 128], F32)
            ago_sb = sbe("eago", [8, D], F32)
            w8_sb = sbe("ew8", [8, 1], F32)
            vhat = [sbe(f"evh{i}", [128, 512], BF16) for i in range(DC)]
            wot = [sbe(f"ewo{i}", [128, 768], BF16) for i in range(2)]
            lnfm = make_lnfm(xnb, sqt, rows3, rbrow, EPS_LN)
            pg = p.phase_gate()
            t_on = p.op("vector", lambda e: e.memset(onesf[:, :], 1.0))
            t_zc = p.op("vector", lambda e: e.memset(carry[:, :], 0.0))
            t_w8 = p.dma("sync", w8_sb[:, :], w8_t.ap(), waits=pg, sem="dw8")
            bank_gate = pg
            wk_free = None
            wv_free = None
            kb_free = [None]
            kv_free = [None]
            kvtb_r = [None]
            xn_gate = pg
            for tt in range(NTT):
                t_app, _ = lnfm(tt, stream_tok, bank_gate, xnb_gate=xn_gate)
                bank_gate = [None]
                for dc in range(DC):
                    p.dma("sync", XN3.ap()[dc * 128:(dc + 1) * 128,
                                           tt * 512:(tt + 1) * 512],
                          xnb[dc][:, :], waits=[t_app[dc]], sem="dxn")
                for ec in range(DC):
                    egate = pg if (tt == 0 and ec == 0) else []
                    tk = [p.dma("gpsimd", wk[:, :], BWk.ap()[ec * 128:(ec + 1) * 128, :],
                                waits=[wk_free] + egate, sem="dek")]
                    tv = [p.dma("scalar", wv[:, :], BWv.ap()[ec * 128:(ec + 1) * 128, :],
                                waits=[wv_free] + egate, sem="dev")]

                    def mmk(e):
                        last = None
                        for dc in range(DC):
                            last = e.matmul(psf[0][:, :], wk[:, dc * 128:(dc + 1) * 128],
                                            xnb[dc][:, :], start=(dc == 0),
                                            stop=(dc == DC - 1))
                        return last

                    def mmv(e):
                        last = None
                        for dc in range(DC):
                            last = e.matmul(psf[1][:, :], wv[:, dc * 128:(dc + 1) * 128],
                                            xnb[dc][:, :], start=(dc == 0),
                                            stop=(dc == DC - 1))
                        return last
                    t_k = p.op("tensor", mmk, waits=tk + t_app + [kb_free[0]], sem="pe")
                    t_vv = p.op("tensor", mmv, waits=tv + t_app + [kv_free[0]], sem="pe")
                    t_kb = p.op("vector", lambda e: e.tensor_copy(kb[:, :], psf[0][:, :]),
                                waits=[t_k])
                    kb_free[0] = t_kb
                    t_kv = p.op("vector", lambda e: e.tensor_tensor(
                        kvf[:, :], psf[1][:, :], kb[:, :], Alu.mult), waits=[t_kb, t_vv])
                    kv_free[0] = t_kv
                    t_sc = p.op("vector", (lambda ec: lambda e: e.tensor_tensor_scan(
                        kvtf[:, :], onesf[:, :], kvf[:, :], carry[:, ec:ec + 1],
                        Alu.mult, Alu.add))(ec), waits=[t_kv, t_on, t_zc])
                    t_cu = p.op("vector", (lambda ec: lambda e: e.tensor_copy(
                        carry[:, ec:ec + 1], kvtf[:, 511:512]))(ec), waits=[t_sc])
                    t_cb = p.op("vector", (lambda: lambda e: e.tensor_copy(
                        kvtb[:, :], kvtf[:, :]))(), waits=[t_sc, kvtb_r[0]])
                    kvtb_r[0] = p.dma("sync", KVT.ap()[ec * 128:(ec + 1) * 128,
                                                       tt * 512:(tt + 1) * 512],
                                      kvtb[:, :], waits=[t_cb], sem="dkt")
                    wk_free = t_k
                    wv_free = t_vv
                bank_gate = [p.pe_now()]
                xn_gate = [p.pe_now(), ("dxn", p.counts["dxn"])]
            t_tpc = p.op("tensor", lambda e: e.transpose(
                psf[2][0:16, 0:128], carry[:, :], identf[:, :]),
                waits=[("_ser_vector", p.counts["_ser_vector"]), t_id], sem="pe")
            t_agi = p.op("vector", lambda e: e.tensor_copy(agi_sb[:, :], psf[2][0:16, 0:128]),
                         waits=[t_tpc])
            t_agid = p.dma("gpsimd", bass.AP(AGI, 0, [[128, 16], [1, 128]]), agi_sb[:, :],
                           waits=[t_agi], sem="dagi")
            t_cc = p.op("gpsimd", lambda e: e.collective_compute(
                "AllGather", Alu.bypass, replica_groups=[list(range(8))],
                ins=[AGI.ap().opt()], outs=[AGO.ap().opt()]),
                waits=[t_agid], sem="cc", by=1)
            t_ago = p.dma("gpsimd", ago_sb[:, :], AGO.ap(), waits=[("cc", 1)], sem="dago")
            ccol_toks = []
            for ec in range(DC):
                t_sel = p.op("tensor", (lambda ec: lambda e: e.matmul(
                    psf[2][0:128, 0:1], ago_sb[0:8, ec * 128:(ec + 1) * 128],
                    w8_sb[:, :], start=True, stop=True))(ec),
                    waits=[t_ago, t_w8, t_agi if ec == 0 else None,
                           ccol_toks[-1] if ec else None], sem="pe")
                t_ce = p.op("vector", (lambda ec: lambda e: e.tensor_copy(
                    ccol[:, ec:ec + 1], psf[2][0:128, 0:1]))(ec), waits=[t_sel])
                ccol_toks.append(t_ce)
            t_ccol = ccol_toks[-1]
            wq_free = p.pe_now()
            q_free[0] = p.pe_now()
            kvtb_free = [kvtb_r[0]]
            vh_free = [None] * DC
            wot_free = [None, None]
            xnb_free2 = [p.pe_now()] * DC
            for tt in range(NTT):
                xl = []
                for dc in range(DC):
                    xl.append(p.dma("sync", xnb[dc][:, :],
                                    XN3.ap()[dc * 128:(dc + 1) * 128,
                                             tt * 512:(tt + 1) * 512],
                                    waits=[("dxn", p.counts["dxn"]), xnb_free2[dc]],
                                    sem="dxl"))
                for ec in range(DC):
                    tq = [p.dma("gpsimd", wk[:, :], BWq.ap()[ec * 128:(ec + 1) * 128, :],
                                waits=[wq_free], sem="deq")]

                    def mmq(e):
                        last = None
                        for dc in range(DC):
                            last = e.matmul(psf[0][:, :], wk[:, dc * 128:(dc + 1) * 128],
                                            xnb[dc][:, :], start=(dc == 0),
                                            stop=(dc == DC - 1))
                        return last
                    t_qq = p.op("tensor", mmq, waits=tq + xl + [q_free[0]], sem="pe")
                    wq_free = t_qq
                    t_kl = p.dma("sync", kvtb[:, :],
                                 KVT.ap()[ec * 128:(ec + 1) * 128,
                                          tt * 512:(tt + 1) * 512],
                                 waits=[("dkt", p.counts["dkt"])] + list(kvtb_free), sem="dkl")
                    t_kc = p.op("vector", (lambda ec: lambda e: e.tensor_scalar(
                        kvf[:, :], kvtb[:, :], ccol[:, ec:ec + 1], None, Alu.add))(ec),
                        waits=[t_kl, t_ccol])
                    kvtb_free = [t_kc]
                    t_vh = p.op("vector", (lambda ec: lambda e: e.scalar_tensor_tensor(
                        vhat[ec][:, :], kvf[:, :], DENOM_B, psf[0][:, :],
                        Alu.mult, Alu.mult))(ec),
                        waits=[t_kc, t_qq, vh_free[ec]])
                    q_free[0] = t_vh
                groups = [list(range(0, 6)), list(range(6, 12)), list(range(12, 16))]
                for grp in groups:
                    t_d = None
                    for ec in range(DC):
                        wpar = ec % 2
                        t_w = p.dma("scalar", wot[wpar][:, 0:len(grp) * 128],
                                    bass.AP(BWot, ec * 128 * 2048 + grp[0] * 128,
                                            [[2048, 128], [1, len(grp) * 128]]),
                                    waits=[wot_free[wpar]] + (pg if wot_free[wpar] is None else []),
                                    sem=f"dew{wpar}")

                        def mmo(e, grp=grp, ec=ec, wpar=wpar):
                            last = None
                            for j in range(len(grp)):
                                last = e.matmul(psf[j][:, :],
                                                wot[wpar][:, j * 128:(j + 1) * 128],
                                                vhat[ec][:, :], start=(ec == 0),
                                                stop=(ec == DC - 1))
                            return last
                        t_d = p.op("tensor", mmo,
                                   waits=[t_w,
                                          p.last_tok["vector"] if ec == 0 else None]
                                   + ([yb_free[j] for j in range(len(grp))] if ec == 0 else []),
                                   sem="pe")
                        wot_free[wpar] = t_d
                    for j, fc in enumerate(grp):
                        t_su = p.op("vector", (lambda j, fc, tt: lambda e: e.tensor_tensor(
                            stream[fc][:, tt * 512:(tt + 1) * 512],
                            stream[fc][:, tt * 512:(tt + 1) * 512], psf[j][:, :],
                            Alu.add))(j, fc, tt),
                            waits=[t_d, stream_tok[fc][tt]])
                        yb_free[j] = t_su
                        stream_tok[fc][tt] = t_su
                pe_now2 = p.pe_now()
                for ec in range(DC):
                    vh_free[ec] = pe_now2
                for dc in range(DC):
                    xnb_free2[dc] = pe_now2
                q_free[0] = yb_free[0]
            esE.close()

        q_free = [None]
        yb_free = [None] * 6
        if STAGE >= 2:
            emit_ffn(0)
        if STAGE >= 3:
            emit_attnb()
        if STAGE >= 4:
            emit_ffn(1)

        esG = contextlib.ExitStack()
        sbgg = mksb(esG)
        otile = [sbgg(f"otile{i}", [128, D], F32) for i in range(2)]
        ot_free = [None, None]
        gtp_free = p.last_tok["vector"]
        for tb in range(16):
            par = tb % 2
            ev_last = None
            for dc in range(DC):
                t_tp = p.op("tensor", (lambda dc, tb: lambda e: e.transpose(
                    psf[5][:, 0:128], stream[dc][:, tb * 128:(tb + 1) * 128],
                    identf[:, :]))(dc, tb),
                    waits=[stream_tok[dc][tb // 4], gtp_free, ot_free[par]], sem="pe")
                ev_last = p.op("vector", (lambda dc, par: lambda e: e.tensor_copy(
                    otile[par][:, dc * 128:(dc + 1) * 128], psf[5][:, 0:128]))(dc, par),
                    waits=[t_tp])
                gtp_free = ev_last
            t_o = p.dma("sync", out_t.ap()[tb * 128:(tb + 1) * 128, :], otile[par][:, :],
                        waits=[ev_last], sem=f"dout{par}")
            ot_free[par] = t_o
        p.wait("sync", ("dout0", p.counts["dout0"]), ("dout1", p.counts["dout1"]))
        esG.close()

        with nc.Block() as block:
            p.finalize(block)
    return nc, p.const_host()



def make_mask(off):
    pp = np.arange(128)[:, None]
    yy = np.arange(MASK_W)[None, :]
    return (yy >= pp + MASK_Y0 - off).astype(BF16NP)


_CACHE = {}


def kernel(L0_attn_a_x, L0_attn_a_Wqkv, L0_attn_a_bqkv, L0_attn_a_Wo, L0_attn_a_bo,
           L0_attn_b_Wq, L0_attn_b_Wk, L0_attn_b_Wv, L0_attn_b_Wo,
           L0_ffn_gate_up_packed, L0_ffn_gate_up_scales,
           L0_ffn_down_packed, L0_ffn_down_scales):
    if "nc" not in _CACHE:
        _CACHE["nc"], _CACHE["consts"] = build_nc()
    nc = _CACHE["nc"]
    consts = _CACHE["consts"]

    x = np.ascontiguousarray(np.asarray(L0_attn_a_x, dtype=np.float32))
    wqkv = np.ascontiguousarray(np.asarray(L0_attn_a_Wqkv, dtype=np.float32))
    wqkvT = np.ascontiguousarray(
        wqkv.reshape(16, 128, 48, 128).transpose(2, 1, 0, 3).reshape(48 * 128, 2048))
    bqkv = np.ascontiguousarray(np.asarray(L0_attn_a_bqkv, dtype=np.float32))
    woa = np.ascontiguousarray(np.asarray(L0_attn_a_Wo, dtype=np.float32))
    bo = np.ascontiguousarray(np.asarray(L0_attn_a_bo, dtype=np.float32))
    ident = np.eye(128, dtype=np.float32)
    gupT = np.ascontiguousarray(np.asarray(L0_ffn_gate_up_packed, dtype=np.int32).T)
    gusc = np.ascontiguousarray(np.asarray(L0_ffn_gate_up_scales, dtype=np.float16))
    dnpT = np.ascontiguousarray(np.asarray(L0_ffn_down_packed, dtype=np.int32).T)
    dnsc = np.ascontiguousarray(np.asarray(L0_ffn_down_scales, dtype=np.float16))
    _sh = 4 * (np.arange(128) % 8)
    shifts = (np.uint32(15) << _sh.astype(np.uint32)).astype(np.uint32).view(np.int32).reshape(128, 1).copy()
    invs = (2.0 ** (-_sh.astype(np.float64))).astype(np.float32).reshape(128, 1)
    wqt = np.ascontiguousarray(np.asarray(L0_attn_b_Wq, dtype=np.float32).T)
    wkt = np.ascontiguousarray(np.asarray(L0_attn_b_Wk, dtype=np.float32).T)
    wvt = np.ascontiguousarray(np.asarray(L0_attn_b_Wv, dtype=np.float32).T)
    wot = np.ascontiguousarray(np.asarray(L0_attn_b_Wo, dtype=np.float32).T)

    in_maps = []
    for c in range(8):
        b, half = c // 2, c % 2
        off = half * OWN
        w8 = np.zeros((8, 1), dtype=np.float32)
        if half == 1:
            w8[2 * b, 0] = 1.0
        in_maps.append({
            "x_own": np.ascontiguousarray(x[b, off:off + OWN]),
            "x_full": np.ascontiguousarray(x[b]),
            "consts": consts,
            "wqkv": wqkv,
            "wqkvT": wqkvT,
            "bqkv": bqkv,
            "woa": woa,
            "bo": bo,
            "mask": make_mask(off),
            "identf": ident,
            "gupT": gupT,
            "gusc": gusc,
            "dnpT": dnpT,
            "dnsc": dnsc,
            "shifts": shifts,
            "invs": invs,
            "wqt": wqt,
            "wkt": wkt,
            "wvt": wvt,
            "wot": wot,
            "w8": w8,
        })
    res = run_bass_kernel_spmd(nc, in_maps, core_ids=list(range(8)))
    out = np.empty((4, S, D), dtype=np.float32)
    for c in range(8):
        b, half = c // 2, c % 2
        out[b, half * OWN:(half + 1) * OWN] = res.results[c]["out"]
    return out
